# version 16
# baseline (speedup 1.0000x reference)
import contextlib
import numpy as np
import ml_dtypes
import concourse.bass as bass
import concourse.mybir as mybir
from concourse.bass_utils import run_bass_kernel_spmd

F32 = mybir.dt.float32
BF16 = mybir.dt.bfloat16
FP8 = mybir.dt.float8e4
AF = mybir.ActivationFunctionType
ALU = mybir.AluOpType
AX = mybir.AxisListType.X

NCORES = 8
H = 2048
HC = H // 128
T = 4096
B = 2
TPC = 1024
I0 = 6144
I1 = 5632
NH = 16
NKV = 4
DH = 128
LN_EPS = 1e-5
BFD = ml_dtypes.bfloat16


class Sync:

    def __init__(self, nc, ctx):
        self.nc = nc
        self.ctx = ctx
        self.sems = {}
        self.counts = {}
        self.alias = {}
        self.pinned = set()
        self.nphys = 0

    def pin(self, tag):
        self.pinned.add(tag)

    def new_phase(self):
        self.alias = {}
        self._next = 0

    def phys(self, tag):
        if tag in self.pinned:
            return "pin_" + tag
        import os
        if os.environ.get("SYNC_NOPOOL"):
            return tag
        if tag not in self.alias:
            if not hasattr(self, "_next"):
                self._next = 0
            p = f"ps{self._next}"
            self._next += 1
            self.alias[tag] = p
        return self.alias[tag]

    def _sem(self, tag):
        p = self.phys(tag)
        if p not in self.sems:
            self.sems[p] = self.ctx.enter_context(self.nc.semaphore(p))
        if p not in self.counts:
            self.counts[p] = 0
        return self.sems[p]

    def inc(self, inst, tag, amount=1):
        s = self._sem(tag)
        p = self.phys(tag)
        self.counts[p] += amount
        inst.then_inc(s, amount)
        return self.counts[p]

    def dma(self, inst, tag):
        return self.inc(inst, tag, 16)

    def wait(self, eng, tag, count=None):
        s = self._sem(tag)
        p = self.phys(tag)
        c = self.counts[p] if count is None else count
        if c > 0:
            eng.wait_ge(s, c)


class PhaseSync:

    class _Counts:
        def __init__(self, ps):
            self.ps = ps

        def __getitem__(self, tag):
            g = self.ps.g
            p = g.phys(tag)
            return g.counts.get(p, 0) - self.ps.base_of(tag)

    def __init__(self, g):
        self.g = g
        g.new_phase()
        self._base = dict(g.counts)
        self.counts = PhaseSync._Counts(self)

    def base_of(self, tag):
        return self._base.get(self.g.phys(tag), 0)

    def inc(self, inst, tag, amount=1):
        return self.g.inc(inst, tag, amount) - self.base_of(tag)

    def dma(self, inst, tag):
        return self.inc(inst, tag, 16)

    def wait(self, eng, tag, count=None):
        if count is None:
            self.g.wait(eng, tag, None)
        else:
            self.g.wait(eng, tag, count + self.base_of(tag))


_CURPFX = [""]


def _mangle(n):
    return _CURPFX[0] + n


def _phase_setup(nc, ext, prefix=""):
    _CURPFX[0] = prefix
    if nc is None:
        nc = bass.Bass("TRN2", target_bir_lowering=False)

    def dram(name, shape, dt, kind):
        if ext is not None and name in ext:
            return ext[name]
        nm = (prefix + name) if kind == "Internal" else name
        return nc.dram_tensor(nm, shape, dt, kind=kind).ap()

    return nc, dram


def fmaj(x):
    return np.ascontiguousarray(np.asarray(x, dtype=np.float32).T)


def cols128(v):
    v = np.asarray(v, dtype=np.float32).reshape(-1, 128)
    return np.ascontiguousarray(v.T)


def emit_postln(block, S, pre, idx, hres, hbf, hsq, onec, oner, rows, pb_sum,
                pb_bc, xo2, store_dst, epsc, hres_wait):
    TH = hres.shape[-1]
    perhc = len(hres_wait) == 4

    def _scalar1(scalar):
        if not perhc:
            S.wait(scalar, hres_wait[0], hres_wait[1])
            scalar.copy(out=hbf[:], in_=hres[:])
            i = scalar.activation(out=hsq[:], in_=hres[:], func=AF.Square)
            S.inc(i, f"{pre}_stat", HC)
        else:
            tag, base, step, grp = hres_wait
            for hc in range(HC):
                S.wait(scalar, tag, base + step * (hc // grp + 1))
                scalar.copy(out=hbf[:, hc, :], in_=hres[:, hc, :])
                i = scalar.activation(out=hsq[:, hc, :], in_=hres[:, hc, :],
                                      func=AF.Square)
                S.inc(i, f"{pre}_stat")
    block.scalar(_scalar1)

    def _tensor1(tensor):
        S.wait(tensor, "ones", 1)
        for hc in range(HC):
            S.wait(tensor, f"{pre}_stat", idx * HC + hc + 1)
            tensor.matmul(pb_sum[0][0:1, :], onec[:], hbf[:, hc, :],
                          start=(hc == 0), stop=(hc == HC - 1))
            i = tensor.matmul(pb_sum[1][0:1, :], onec[:], hsq[:, hc, :],
                              start=(hc == 0), stop=(hc == HC - 1))
        S.inc(i, f"{pre}_sums")
    block.tensor(_tensor1)

    def _vector1(vector):
        S.wait(vector, f"{pre}_sums", idx + 1)
        i = vector.tensor_scalar(out=rows[:, 0, :], in0=pb_sum[0][0:1, :],
                                 scalar1=1.0 / H, scalar2=None, op0=ALU.mult)
        c = S.inc(i, "vch")
        vector.tensor_scalar(out=rows[:, 1, :], in0=pb_sum[1][0:1, :],
                             scalar1=1.0 / H, scalar2=None, op0=ALU.mult)
        S.wait(vector, "vch", c)
        i = vector.tensor_tensor(out=rows[:, 2, :], in0=rows[:, 0, :],
                                 in1=rows[:, 0, :], op=ALU.mult)
        c = S.inc(i, "vch")
        S.wait(vector, "vch", c)
        i = vector.tensor_tensor(out=rows[:, 1, :], in0=rows[:, 1, :],
                                 in1=rows[:, 2, :], op=ALU.subtract)
        S.inc(i, f"{pre}_var")
    block.vector(_vector1)

    def _scalar2(scalar):
        S.wait(scalar, f"{pre}_var", idx + 1)
        i = scalar.activation(out=rows[:, 3, :], in_=rows[:, 1, :],
                              func=AF.Sqrt, bias=epsc[:])
        S.inc(i, f"{pre}_sd")
    block.scalar(_scalar2)

    def _vector2(vector):
        S.wait(vector, f"{pre}_sd", idx + 1)
        i = vector.reciprocal(out=rows[:, 4, :], in_=rows[:, 3, :])
        S.inc(i, f"{pre}_rinv")
    block.vector(_vector2)

    def _tensor2(tensor):
        S.wait(tensor, f"{pre}_rinv", idx + 1)
        tensor.matmul(pb_bc[0][:], oner[:], rows[:, 0, :], start=True, stop=True)
        i = tensor.matmul(pb_bc[1][:], oner[:], rows[:, 4, :], start=True, stop=True)
        S.inc(i, f"{pre}_bc")
    block.tensor(_tensor2)

    def _vector3(vector):
        S.wait(vector, f"{pre}_bc", idx + 1)
        for hc in range(HC):
            sl = hc % 2
            g = idx * HC + hc
            if g >= 2:
                S.wait(vector, f"{pre}_st{sl}", 16 * (g // 2))
            i = vector.tensor_tensor(out=xo2[:, sl, :], in0=hres[:, hc, :],
                                     in1=pb_bc[0][:], op=ALU.subtract)
            c = S.inc(i, "vch")
            S.wait(vector, "vch", c)
            i = vector.tensor_tensor(out=xo2[:, sl, :], in0=xo2[:, sl, :],
                                     in1=pb_bc[1][:], op=ALU.mult)
            S.inc(i, f"{pre}_norm")
    block.vector(_vector3)

    def _sync1(sync):
        for hc in range(HC):
            sl = hc % 2
            S.wait(sync, f"{pre}_norm", idx * HC + hc + 1)
            i = sync.dma_start(out=store_dst[:, hc, :], in_=xo2[:, sl, :])
            S.dma(i, f"{pre}_st{sl}")
    block.sync(_sync1)



def build_l0_ffn(nc=None, G=None, ext=None):
    nc, dram = _phase_setup(nc, ext, "c_")
    x2 = dram("x2", [H, TPC], F32, "ExternalInput")
    gw = dram("gw", [H, I0], F32, "ExternalInput")
    uw = dram("uw", [H, I0], F32, "ExternalInput")
    dw = dram("dw", [I0, H], F32, "ExternalInput")
    gsc = dram("gsc", [128, I0 // 128], F32, "ExternalInput")
    usc = dram("usc", [128, I0 // 128], F32, "ExternalInput")
    dsc = dram("dsc", [128, HC], F32, "ExternalInput")
    x3 = dram("x3", [H, TPC], F32, "ExternalOutput")

    x2v = x2.rearrange("(c p) t -> p c t", p=128)
    x3v = x3.rearrange("(c p) t -> p c t", p=128)

    NOB = I0 // 128
    OSB = 8
    OPB = NOB // OSB
    OSW = OPB * 128
    NTH = 2
    TH = TPC // NTH

    with contextlib.ExitStack() as ctx:
        sb = lambda n, *a: ctx.enter_context(nc.sbuf_tensor(_mangle(n), *a))
        psb = lambda n, *a: ctx.enter_context(nc.psum_tensor(_mangle(n), *a))
        S = PhaseSync(G) if G is not None else PhaseSync(Sync(nc, ctx))

        x2bf = sb("x2bf", [128, HC, TH], BF16)
        g8 = sb("g8", [128, 2, HC, OSW], FP8)
        u8 = sb("u8", [128, 2, HC, OSW], FP8)
        d8 = sb("d8", [128, 8, 1024], FP8)
        fuse = sb("fuse", [128, NOB, TH], BF16)
        gsil = sb("gsil", [128, 2, TH], BF16)
        hres = sb("hres", [128, HC, TH], F32)
        x2f = sb("x2f", [128, 8, TH], F32)
        xo2 = sb("xo2", [128, 2, TH], F32)
        gscT = sb("gscT", [128, NOB], F32)
        uscT = sb("uscT", [128, NOB], F32)
        dscT = sb("dscT", [128, HC], F32)
        onec = sb("onec", [128, 1], BF16)
        oner = sb("oner", [1, 128], F32)
        epsc = sb("epsc", [1, 1], F32)
        rows = sb("rows", [1, 5, TH], F32)
        pb = [psb(f"pb{i}", [128, 512], F32) for i in range(8)]
        hbf = fuse[:, 0:HC, :]
        hsq = fuse[:, HC:2 * HC, :]

        block = ctx.enter_context(nc.Block())

        def _consts(sync):
            for ap_, src in ((gscT, gsc), (uscT, usc), (dscT, dsc)):
                i = sync.dma_start(out=ap_[:], in_=src)
                S.dma(i, "consts")
        block.sync(_consts)

        def _ones(gpsimd):
            gpsimd.memset(onec[:], 1.0)
            gpsimd.memset(epsc[:], LN_EPS)
            i = gpsimd.memset(oner[:], 1.0)
            S.inc(i, "ones")
        block.gpsimd(_ones)

        for h in range(NTH):
            def _wdma(gpsimd, h=h):
                if h > 0:
                    S.wait(gpsimd, "gups", h * NOB)
                i = gpsimd.dma_start(out=x2bf[:],
                                     in_=x2v[:, :, h * TH:(h + 1) * TH])
                S.dma(i, "x2bf")
                for o in range(OSB):
                    sl = o % 2
                    g = h * OSB + o
                    if g >= 2:
                        S.wait(gpsimd, "gups", (g - 1) * OPB)
                    for hc in range(HC):
                        i = gpsimd.dma_start(
                            out=g8[:, sl, hc, :],
                            in_=gw[hc * 128:(hc + 1) * 128, o * OSW:(o + 1) * OSW])
                        S.dma(i, f"g8s{sl}")
                    for hc in range(HC):
                        i = gpsimd.dma_start(
                            out=u8[:, sl, hc, :],
                            in_=uw[hc * 128:(hc + 1) * 128, o * OSW:(o + 1) * OSW])
                        S.dma(i, f"u8s{sl}")
                for dg in range(2):
                    for oc in range(NOB):
                        sl = oc % 8
                        k = (h * 2 + dg) * NOB + oc
                        if k >= 8:
                            S.wait(gpsimd, "d_mm", k - 7)
                        i = gpsimd.dma_start(
                            out=d8[:, sl, :],
                            in_=dw[oc * 128:(oc + 1) * 128,
                                   dg * 1024:(dg + 1) * 1024])
                        S.dma(i, f"d8s{sl}")
            block.gpsimd(_wdma)

            def _x2f(sync, h=h):
                for dg in range(2):
                    idx = h * 2 + dg
                    if idx > 0:
                        S.wait(sync, "dev", idx)
                    i = sync.dma_start(
                        out=x2f[:],
                        in_=x2v[:, dg * 8:(dg + 1) * 8, h * TH:(h + 1) * TH])
                    S.dma(i, "x2f")
            block.sync(_x2f)

            def _mm(tensor, h=h):
                S.wait(tensor, "x2bf", 16 * (h + 1))
                if h > 0:
                    S.wait(tensor, "dev", 2 * h)
                for o in range(OSB):
                    sl = o % 2
                    g = h * OSB + o
                    S.wait(tensor, f"g8s{sl}", 256 * (g // 2 + 1))
                    S.wait(tensor, f"u8s{sl}", 256 * (g // 2 + 1))
                    for ob in range(OPB):
                        gob = o * OPB + ob
                        gbank, ubank = (0, 1) if gob % 2 == 0 else (2, 3)
                        if gob >= 2:
                            S.wait(tensor, "guev", h * NOB + gob - 1)
                        for hc in range(HC):
                            tensor.matmul(pb[gbank][:],
                                          g8[:, sl, hc, ob * 128:(ob + 1) * 128],
                                          x2bf[:, hc, :],
                                          start=(hc == 0), stop=(hc == HC - 1))
                        for hc in range(HC):
                            i = tensor.matmul(pb[ubank][:],
                                              u8[:, sl, hc, ob * 128:(ob + 1) * 128],
                                              x2bf[:, hc, :],
                                              start=(hc == 0), stop=(hc == HC - 1))
                        S.inc(i, "gups")
                S.wait(tensor, "guev", (h + 1) * NOB)
                if h > 0:
                    S.wait(tensor, "ln_norm", h * HC)
                for dg in range(2):
                    if dg == 1:
                        S.wait(tensor, "dev", h * 2 + 1)
                    for oc in range(NOB):
                        sl = oc % 8
                        k = (h * 2 + dg) * NOB + oc
                        S.wait(tensor, f"d8s{sl}", 16 * (k // 8 + 1))
                        for db in range(8):
                            i = tensor.matmul(pb[db][:],
                                              d8[:, sl, db * 128:(db + 1) * 128],
                                              fuse[:, oc, :],
                                              start=(oc == 0), stop=(oc == NOB - 1))
                        S.inc(i, "d_mm")
            block.tensor(_mm)

            def _silu(scalar, h=h):
                if h == 0:
                    S.wait(scalar, "consts", 48)
                for gob in range(NOB):
                    gbank = 0 if gob % 2 == 0 else 2
                    sl = gob % 2
                    S.wait(scalar, "gups", h * NOB + gob + 1)
                    if gob >= 2:
                        S.wait(scalar, "guev", h * NOB + gob - 1)
                    i = scalar.activation(out=gsil[:, sl, :], in_=pb[gbank][:],
                                          func=AF.Silu,
                                          scale=gscT[:, gob:gob + 1])
                    S.inc(i, "gsil")
            block.scalar(_silu)

            def _evac(vector, h=h):
                if h == 0:
                    S.wait(vector, "consts", 48)
                for gob in range(NOB):
                    ubank = 1 if gob % 2 == 0 else 3
                    sl = gob % 2
                    if h > 0 and gob == 0:
                        S.wait(vector, "ln_sums", h)
                    S.wait(vector, "gups", h * NOB + gob + 1)
                    i = vector.tensor_scalar(out=fuse[:, gob, :], in0=pb[ubank][:],
                                             scalar1=uscT[:, gob:gob + 1],
                                             scalar2=None, op0=ALU.mult)
                    c = S.inc(i, "vch")
                    S.wait(vector, "gsil", h * NOB + gob + 1)
                    S.wait(vector, "vch", c)
                    i = vector.tensor_tensor(out=fuse[:, gob, :],
                                             in0=fuse[:, gob, :],
                                             in1=gsil[:, sl, :], op=ALU.mult)
                    S.inc(i, "guev")
                for dg in range(2):
                    idx = h * 2 + dg
                    S.wait(vector, "d_mm", (idx + 1) * NOB)
                    S.wait(vector, "x2f", 16 * (idx + 1))
                    for db in range(8):
                        hc = dg * 8 + db
                        i = vector.tensor_scalar(out=hres[:, hc, :], in0=pb[db][:],
                                                 scalar1=dscT[:, hc:hc + 1],
                                                 scalar2=None, op0=ALU.mult)
                        c = S.inc(i, "vch")
                        S.wait(vector, "vch", c)
                        i = vector.tensor_tensor(out=hres[:, hc, :],
                                                 in0=hres[:, hc, :],
                                                 in1=x2f[:, db, :], op=ALU.add)
                    S.inc(i, "dev")
            block.vector(_evac)

            emit_postln(block, S, "ln", h, hres, hbf, hsq, onec, oner, rows,
                        (pb[4], pb[5]), (pb[6], pb[7]), xo2,
                        x3v[:, :, h * TH:(h + 1) * TH], epsc,
                        ("dev", 2 * (h + 1)))

        def _final(sync):
            S.wait(sync, "ln_st0", 16 * HC * NTH // 2)
            S.wait(sync, "ln_st1", 16 * HC * NTH // 2)
        block.sync(_final)

    return nc


def _l0_ffn_in_maps(inputs, x2_full):
    gw = np.ascontiguousarray(np.asarray(inputs["L0_ffn_gate_w"], dtype=np.float32))
    uw = np.ascontiguousarray(np.asarray(inputs["L0_ffn_up_w"], dtype=np.float32))
    dw = np.ascontiguousarray(np.asarray(inputs["L0_ffn_down_w"], dtype=np.float32))
    gsc = cols128(inputs["L0_ffn_gate_scale"])
    usc = cols128(inputs["L0_ffn_up_scale"])
    dsc = cols128(inputs["L0_ffn_down_scale"])
    maps = []
    for c in range(NCORES):
        b, q = c // 4, c % 4
        x2 = fmaj(x2_full[b, q * TPC:(q + 1) * TPC, :])
        maps.append({"x2": x2, "gw": gw, "uw": uw, "dw": dw,
                     "gsc": gsc, "usc": usc, "dsc": dsc})
    return maps


def _gather_fm(results, key):
    out = np.empty((B, T, H), dtype=np.float32)
    for c in range(NCORES):
        b, q = c // 4, c % 4
        out[b, q * TPC:(q + 1) * TPC, :] = results[c][key].T
    return out


def run_l0_ffn(inputs, x2_full):
    nc = build_l0_ffn()
    res = run_bass_kernel_spmd(nc, _l0_ffn_in_maps(inputs, x2_full),
                               core_ids=list(range(NCORES)))
    return _gather_fm(res.results, "x3")



def build_l1_ffn(nc=None, G=None, ext=None):
    nc, dram = _phase_setup(nc, ext, "g_")
    x4 = dram("x4", [H, TPC], F32, "ExternalInput")
    tw1 = dram("tw1", [H, I1], FP8, "ExternalInput")
    tw2 = dram("tw2", [H, I1], FP8, "ExternalInput")
    tw3 = dram("tw3", [I1, H], FP8, "ExternalInput")
    x5 = dram("x5", [H, TPC], F32, "ExternalOutput")

    x4v = x4.rearrange("(c p) t -> p c t", p=128)
    x5v = x5.rearrange("(c p) t -> p c t", p=128)

    NOB = I1 // 128
    OSB = 22
    OPB = NOB // OSB
    OSW = OPB * 128
    NTH = 2
    TH = TPC // NTH

    with contextlib.ExitStack() as ctx:
        sb = lambda n, *a: ctx.enter_context(nc.sbuf_tensor(_mangle(n), *a))
        psb = lambda n, *a: ctx.enter_context(nc.psum_tensor(_mangle(n), *a))
        S = PhaseSync(G) if G is not None else PhaseSync(Sync(nc, ctx))

        x4bf = sb("x4bf", [128, HC, TH], BF16)
        t1 = sb("t1", [128, 2, HC, OSW], FP8)
        t2 = sb("t2", [128, 2, HC, OSW], FP8)
        t3 = sb("t3", [128, 4, 512], FP8)
        hidden = sb("hidden", [128, NOB, TH], BF16)
        lnst = sb("lnst", [128, 2 * HC, TH], BF16)
        sig = sb("sig", [128, 2, TH], BF16)
        hres = sb("hres", [128, HC, TH], F32)
        x4f = sb("x4f", [128, 4, TH], F32)
        xo2 = sb("xo2", [128, 2, TH], F32)
        onec = sb("onec", [128, 1], BF16)
        oner = sb("oner", [1, 128], F32)
        epsc = sb("epsc", [1, 1], F32)
        rows = sb("rows", [1, 5, TH], F32)
        pb = [psb(f"pb{i}", [128, 512], F32) for i in range(8)]
        hbf = lnst[:, 0:HC, :]
        hsq = lnst[:, HC:2 * HC, :]

        block = ctx.enter_context(nc.Block())

        def _init(gpsimd):
            gpsimd.memset(onec[:], 1.0)
            gpsimd.memset(epsc[:], LN_EPS)
            i = gpsimd.memset(oner[:], 1.0)
            S.inc(i, "ones")
        block.gpsimd(_init)

        for h in range(NTH):
            def _wdma(gpsimd, h=h):
                if h == 0 and G is not None:
                    S.g.wait(gpsimd, "cct", 7)
                if h > 0:
                    S.wait(gpsimd, "mm12ps", h * NOB)
                i = gpsimd.dma_start(out=x4bf[:],
                                     in_=x4v[:, :, h * TH:(h + 1) * TH])
                S.dma(i, "x4bf")
                for o in range(OSB):
                    sl = o % 2
                    g = h * OSB + o
                    if g >= 2:
                        S.wait(gpsimd, "mm12ps", (g - 1) * OPB)
                    for hc in range(HC):
                        i = gpsimd.dma_start(
                            out=t1[:, sl, hc, 0:OSW],
                            in_=tw1[hc * 128:(hc + 1) * 128, o * OSW:(o + 1) * OSW])
                        S.dma(i, f"w1ld{sl}")
                    for hc in range(HC):
                        i = gpsimd.dma_start(
                            out=t2[:, sl, hc, 0:OSW],
                            in_=tw2[hc * 128:(hc + 1) * 128, o * OSW:(o + 1) * OSW])
                        S.dma(i, f"w2ld{sl}")
                for g4 in range(4):
                    for oc in range(NOB):
                        k = (h * 4 + g4) * NOB + oc
                        sl = k % 4
                        if k >= 4:
                            S.wait(gpsimd, "mm3", k - 3)
                        i = gpsimd.dma_start(
                            out=t3[:, sl, :],
                            in_=tw3[oc * 128:(oc + 1) * 128,
                                    g4 * 512:(g4 + 1) * 512])
                        S.dma(i, f"w3ld{sl}")
            block.gpsimd(_wdma)

            def _x4f(sync, h=h):
                for hc in range(HC):
                    ci = h * 16 + hc
                    if ci >= 4:
                        S.wait(sync, "devc", ci - 3)
                    i = sync.dma_start(
                        out=x4f[:, ci % 4, :],
                        in_=x4v[:, hc, h * TH:(h + 1) * TH])
                    S.dma(i, f"x4f{ci % 4}")
            block.sync(_x4f)

            def _vec(vector, h=h):
                for gob in range(NOB):
                    b2 = 1 if gob % 2 == 0 else 3
                    sl = gob % 2
                    S.wait(vector, "mm12ps", h * NOB + gob + 1)
                    S.wait(vector, "sig", h * NOB + gob + 1)
                    i = vector.tensor_tensor(out=hidden[:, gob, :],
                                             in0=pb[b2][:],
                                             in1=sig[:, sl, :], op=ALU.mult)
                    S.inc(i, "hev")
                for g4 in range(4):
                    S.wait(vector, "mm3", (h * 4 + g4 + 1) * NOB)
                    for j in range(4):
                        hc = g4 * 4 + j
                        ci = h * 16 + hc
                        i = vector.tensor_copy(out=hres[:, hc, :], in_=pb[j][:])
                        c = S.inc(i, "vch")
                        S.wait(vector, "vch", c)
                        S.wait(vector, f"x4f{ci % 4}", 16 * (ci // 4 + 1))
                        i = vector.tensor_tensor(out=hres[:, hc, :],
                                                 in0=hres[:, hc, :],
                                                 in1=x4f[:, ci % 4, :], op=ALU.add)
                        S.inc(i, "devc")
            block.vector(_vec)

            def _mm(tensor, h=h):
                S.wait(tensor, "x4bf", 16 * (h + 1))
                if h > 0:
                    S.wait(tensor, "devc", h * 16)
                for o in range(OSB):
                    sl = o % 2
                    g = h * OSB + o
                    S.wait(tensor, f"w1ld{sl}", 256 * (g // 2 + 1))
                    S.wait(tensor, f"w2ld{sl}", 256 * (g // 2 + 1))
                    for ob in range(OPB):
                        gob = o * OPB + ob
                        b1, b2 = (0, 1) if gob % 2 == 0 else (2, 3)
                        if gob >= 2:
                            S.wait(tensor, "hev", h * NOB + gob - 1)
                        for hc in range(HC):
                            tensor.matmul(pb[b1][:],
                                          t1[:, sl, hc, ob * 128:(ob + 1) * 128],
                                          x4bf[:, hc, :],
                                          start=(hc == 0), stop=(hc == HC - 1))
                        for hc in range(HC):
                            i = tensor.matmul(pb[b2][:],
                                              t2[:, sl, hc, ob * 128:(ob + 1) * 128],
                                              x4bf[:, hc, :],
                                              start=(hc == 0), stop=(hc == HC - 1))
                        S.inc(i, "mm12ps")
                for g4 in range(4):
                    if g4 == 0:
                        S.wait(tensor, "hev", (h + 1) * NOB)
                        if h > 0:
                            S.wait(tensor, "ln_norm", h * HC)
                    else:
                        S.wait(tensor, "devc", h * 16 + g4 * 4)
                    for oc in range(NOB):
                        k = (h * 4 + g4) * NOB + oc
                        sl = k % 4
                        S.wait(tensor, f"w3ld{sl}", 16 * (k // 4 + 1))
                        for db in range(4):
                            i = tensor.matmul(pb[db][:],
                                              t3[:, sl, db * 128:(db + 1) * 128],
                                              hidden[:, oc, :],
                                              start=(oc == 0), stop=(oc == NOB - 1))
                        S.inc(i, "mm3")
            block.tensor(_mm)

            def _sig(scalar, h=h):
                for gob in range(NOB):
                    b1 = 0 if gob % 2 == 0 else 2
                    sl = gob % 2
                    S.wait(scalar, "mm12ps", h * NOB + gob + 1)
                    if gob >= 2:
                        S.wait(scalar, "hev", h * NOB + gob - 1)
                    i = scalar.activation(out=sig[:, sl, :], in_=pb[b1][:],
                                          func=AF.Sigmoid)
                    S.inc(i, "sig")
            block.scalar(_sig)

            emit_postln(block, S, "ln", h, hres, hbf, hsq, onec, oner, rows,
                        (pb[4], pb[5]), (pb[6], pb[7]), xo2,
                        x5v[:, :, h * TH:(h + 1) * TH], epsc,
                        ("devc", h * 16, 1, 1))

        def _final(sync):
            S.wait(sync, "ln_st0", 16 * HC * NTH // 2)
            S.wait(sync, "ln_st1", 16 * HC * NTH // 2)
        block.sync(_final)

    return nc


def _l1_ffn_in_maps(inputs, x4_full):
    w1t = np.ascontiguousarray(np.asarray(inputs["L1_ffn_w1"], dtype=np.float32).T)
    w2t = np.ascontiguousarray(np.asarray(inputs["L1_ffn_w2"], dtype=np.float32).T)
    w3t = np.ascontiguousarray(np.asarray(inputs["L1_ffn_w3"], dtype=np.float32).T)
    maps = []
    for c in range(NCORES):
        b, q = c // 4, c % 4
        x4 = fmaj(x4_full[b, q * TPC:(q + 1) * TPC, :])
        maps.append({"x4": x4, "w1t": w1t, "w2t": w2t, "w3t": w3t})
    return maps


def run_l1_ffn(inputs, x4_full):
    nc = build_l1_ffn()
    res = run_bass_kernel_spmd(nc, _l1_ffn_in_maps(inputs, x4_full),
                               core_ids=list(range(NCORES)))
    return _gather_fm(res.results, "x5")





def build_l0_mixer_a(nc=None, G=None, ext=None):
    nc, dram = _phase_setup(nc, ext, "a_")
    xin = dram("xin", [H, TPC + 1], F32, "ExternalInput")
    wkT = dram("wkT", [H, H], F32, "ExternalInput")
    wvT = dram("wvT", [H, H], F32, "ExternalInput")
    wrT = dram("wrT", [H, H], F32, "ExternalInput")
    bkc = dram("bkc", [128, HC], F32, "ExternalInput")
    bvc = dram("bvc", [128, HC], F32, "ExternalInput")
    brc = dram("brc", [128, HC], F32, "ExternalInput")
    tmk = dram("tmk", [128, HC], F32, "ExternalInput")
    tmv = dram("tmv", [128, HC], F32, "ExternalInput")
    tmr = dram("tmr", [128, HC], F32, "ExternalInput")
    tdec = dram("tdec", [128, HC], F32, "ExternalInput")
    tfir = dram("tfir", [128, HC], F32, "ExternalInput")
    nout = dram("nout", [H, TPC], BF16, "ExternalOutput")
    dout = dram("dout", [H, TPC], BF16, "ExternalOutput")
    srout = dram("srout", [H, TPC], BF16, "ExternalOutput")
    totout = dram("totout", [128, 48], F32, "ExternalOutput")

    xv_ = xin.rearrange("(c p) t -> p c t", p=128)
    noutv = nout.rearrange("(c p) t -> p c t", p=128)
    doutv = dout.rearrange("(c p) t -> p c t", p=128)
    sroutv = srout.rearrange("(c p) t -> p c t", p=128)

    SBW = 256
    NSB = H // SBW
    OCPS = SBW // 128

    with contextlib.ExitStack() as ctx:
        sb = lambda n, *a: ctx.enter_context(nc.sbuf_tensor(_mangle(n), *a))
        psb = lambda n, *a: ctx.enter_context(nc.psum_tensor(_mangle(n), *a))
        S = PhaseSync(G) if G is not None else PhaseSync(Sync(nc, ctx))

        xbf = sb("xbf", [128, HC, TPC + 1], BF16)
        xl = sb("xl", [128, HC, TPC], BF16)
        z = sb("z", [128, HC, TPC], BF16)
        ws = sb("ws", [128, 2, HC, SBW], BF16)
        vtmp = sb("vtmp", [128, TPC], BF16)
        ytmp = sb("ytmp", [128, TPC], BF16)
        dt_t = sb("dt_t", [128, TPC], F32)
        s_oc = sb("s_oc", [128, TPC], F32)
        z_oc = sb("z_oc", [128, TPC], F32)
        t1 = sb("t1", [128, TPC], F32)
        ndb = sb("ndb", [128, 2, TPC], BF16)
        ddb = sb("ddb", [128, 2, TPC], BF16)
        srdb = sb("srdb", [128, 2, TPC], BF16)
        one4 = sb("one4", [128, TPC], F32)
        tmkc = sb("tmkc", [128, HC], F32)
        tmvc = sb("tmvc", [128, HC], F32)
        tmrc = sb("tmrc", [128, HC], F32)
        bkcc = sb("bkcc", [128, HC], F32)
        bvcc = sb("bvcc", [128, HC], F32)
        brcc = sb("brcc", [128, HC], F32)
        tdc = sb("tdc", [128, HC], F32)
        tfc = sb("tfc", [128, HC], F32)
        euc = sb("euc", [128, HC], F32)
        dcol = sb("dcol", [128, HC], F32)
        bkm = sb("bkm", [128, HC], F32)
        mc2 = sb("mc2", [128, 2], F32)
        tots = sb("tots", [128, 48], F32)
        pb = [psb(f"pb{i}", [128, 512], F32) for i in range(4)]

        block = ctx.enter_context(nc.Block(no_gpsimd_drain=True))

        def _consts(sync):
            for ap_, src in ((tmkc, tmk), (tmvc, tmv), (tmrc, tmr),
                             (bkcc, bkc), (bvcc, bvc), (brcc, brc),
                             (tdc, tdec), (tfc, tfir)):
                i = sync.dma_start(out=ap_[:], in_=src)
                S.dma(i, "consts")
        block.sync(_consts)

        def _xld(gpsimd):
            i = gpsimd.memset(one4[:], 1.0)
            S.inc(i, "ones")
            i = gpsimd.dma_start(out=xbf[:], in_=xv_)
            S.dma(i, "xbf")
        block.gpsimd(_xld)

        def _prep(scalar):
            S.wait(scalar, "consts", 128)
            i = scalar.activation(out=euc[:], in_=tfc[:], func=AF.Exp)
            c = S.inc(i, "sch")
            i = scalar.activation(out=dcol[:], in_=tdc[:], func=AF.Exp)
            c = S.inc(i, "sch")
            S.wait(scalar, "sch", c)
            i = scalar.mul(out=dcol[:], in_=dcol[:], mul=-1.0)
            c = S.inc(i, "sch")
            S.wait(scalar, "sch", c)
            i = scalar.activation(out=dcol[:], in_=dcol[:], func=AF.Exp)
            S.inc(i, "prep")
        block.scalar(_prep)

        PROJ = (("k", wkT, "zev"), ("v", wvT, "vsc"), ("r", wrT, "rsc"))

        def _wdma(gpsimd):
            for pi, (pname, wT, _) in enumerate(PROJ):
                for sbi in range(NSB):
                    sl = sbi % 2
                    gsb = pi * NSB + sbi
                    if gsb >= 2:
                        prev = gsb - 2
                        if prev < NSB:
                            S.wait(gpsimd, "kps", (prev + 1) * OCPS)
                        elif prev < 2 * NSB:
                            S.wait(gpsimd, "vps", (prev - NSB + 1) * OCPS)
                        else:
                            S.wait(gpsimd, "rps", (prev - 2 * NSB + 1) * OCPS)
                    for hc in range(HC):
                        i = gpsimd.dma_start(
                            out=ws[:, sl, hc, :],
                            in_=wT[hc * 128:(hc + 1) * 128,
                                   sbi * SBW:(sbi + 1) * SBW])
                        S.dma(i, f"ws{sl}")
        block.gpsimd(_wdma)

        def _vec(vector):
            S.wait(vector, "prep", 1)

            def lerp(tm, dst_tag):
                i = vector.tensor_tensor(out=xl[:], in0=xbf[:, :, 1:TPC + 1],
                                         in1=xbf[:, :, 0:TPC], op=ALU.subtract)
                c = S.inc(i, "vch")
                S.wait(vector, "vch", c)
                for hc in range(HC):
                    i = vector.scalar_tensor_tensor(
                        out=xl[:, hc, :], in0=xl[:, hc, :],
                        scalar=tm[:, hc:hc + 1], in1=xbf[:, hc, 0:TPC],
                        op0=ALU.mult, op1=ALU.add)
                    if hc == HC - 1:
                        S.inc(i, dst_tag)
                    else:
                        S.inc(i, "vch")

            S.wait(vector, "xbf", 16)
            lerp(tmkc, "xlk")
            for oc in range(HC):
                e, o = (0, 1) if oc % 2 == 0 else (2, 3)
                S.wait(vector, "kps", oc + 1)
                S.wait(vector, "mrdy", oc)
                i = vector.reduce_max(out=mc2[:, 0:1], in_=pb[e][:], axis=AX)
                c = S.inc(i, "vch")
                i = vector.reduce_max(out=mc2[:, 1:2], in_=pb[o][:], axis=AX)
                c = S.inc(i, "vch")
                S.wait(vector, "vch", c)
                i = vector.tensor_tensor(out=tots[:, 32 + oc:33 + oc],
                                         in0=mc2[:, 0:1], in1=mc2[:, 1:2],
                                         op=ALU.max)
                c = S.inc(i, "vch")
                S.wait(vector, "vch", c)
                i = vector.tensor_tensor(out=bkm[:, oc:oc + 1],
                                         in0=bkcc[:, oc:oc + 1],
                                         in1=tots[:, 32 + oc:33 + oc],
                                         op=ALU.subtract)
                S.inc(i, "mrdy")
            S.wait(vector, "kps", HC)
            lerp(tmvc, "xlv")
            for oc in range(HC):
                S.wait(vector, "vsc", 2 * (oc + 1))
                S.wait(vector, "ndrdy", oc)
                i = vector.tensor_tensor(out=ytmp[:], in0=z[:, oc, :],
                                         in1=vtmp[:], op=ALU.mult)
                c = S.inc(i, "vch")
                i = vector.tensor_scalar(out=dt_t[:], in0=one4[:],
                                         scalar1=dcol[:, oc:oc + 1],
                                         scalar2=None, op0=ALU.mult)
                c = S.inc(i, "vch")
                S.wait(vector, "vch", c)
                i = vector.tensor_tensor_scan(out=s_oc[:], data0=dt_t[:],
                                              data1=ytmp[:], initial=0.0,
                                              op0=ALU.mult, op1=ALU.add)
                c = S.inc(i, "vch")
                i = vector.tensor_tensor_scan(out=z_oc[:], data0=dt_t[:],
                                              data1=z[:, oc, :], initial=0.0,
                                              op0=ALU.mult, op1=ALU.add)
                c = S.inc(i, "vch")
                i = vector.tensor_scalar(out=t1[:], in0=ytmp[:],
                                         scalar1=euc[:, oc:oc + 1],
                                         scalar2=None, op0=ALU.mult)
                c = S.inc(i, "vch")
                S.wait(vector, "vch", c)
                sl = oc % 2
                if oc >= 2:
                    S.wait(vector, "ndst0" if sl == 0 else "ndst1",
                           32 * (oc // 2))
                i = vector.tensor_tensor(out=ndb[:, sl, 1:TPC], in0=t1[:, 1:TPC],
                                         in1=s_oc[:, 0:TPC - 1], op=ALU.add)
                c = S.inc(i, "vch")
                i = vector.tensor_copy(out=ndb[:, sl, 0:1], in_=t1[:, 0:1])
                c = S.inc(i, "vch")
                S.wait(vector, "vch", c)
                i = vector.tensor_scalar(out=t1[:], in0=z[:, oc, :],
                                         scalar1=euc[:, oc:oc + 1],
                                         scalar2=None, op0=ALU.mult)
                c = S.inc(i, "vch")
                S.wait(vector, "vch", c)
                i = vector.tensor_tensor(out=ddb[:, sl, 1:TPC], in0=t1[:, 1:TPC],
                                         in1=z_oc[:, 0:TPC - 1], op=ALU.add)
                c = S.inc(i, "vch")
                i = vector.tensor_copy(out=ddb[:, sl, 0:1], in_=t1[:, 0:1])
                c = S.inc(i, "vch")
                vector.tensor_copy(out=tots[:, oc:oc + 1],
                                   in_=s_oc[:, TPC - 1:TPC])
                S.wait(vector, "vch", c)
                i = vector.tensor_copy(out=tots[:, 16 + oc:17 + oc],
                                       in_=z_oc[:, TPC - 1:TPC])
                S.inc(i, "ndrdy")
            S.wait(vector, "vps", HC)
            lerp(tmrc, "xlr")
        block.vector(_vec)

        def _mm(tensor):
            S.wait(tensor, "xlk", 1)
            for oc in range(HC):
                sbi = oc // OCPS
                sl = sbi % 2
                e, o = (0, 1) if oc % 2 == 0 else (2, 3)
                S.wait(tensor, "ws0" if sl == 0 else "ws1",
                       256 * (sbi // 2 + 1))
                if oc >= 2:
                    S.wait(tensor, "zev", 2 * (oc - 1))
                for hh in range(2):
                    for hc in range(HC):
                        i = tensor.matmul(
                            pb[e if hh == 0 else o][:],
                            ws[:, sl, hc, (oc % OCPS) * 128:(oc % OCPS + 1) * 128],
                            xl[:, hc, hh * 512:(hh + 1) * 512],
                            start=(hc == 0), stop=(hc == HC - 1))
                S.inc(i, "kps")
            S.wait(tensor, "xlv", 1)
            S.wait(tensor, "zev", 2 * HC)
            for oc in range(HC):
                sbi = oc // OCPS
                sl = sbi % 2
                gsb = NSB + sbi
                e, o = (0, 1) if oc % 2 == 0 else (2, 3)
                S.wait(tensor, "ws0" if sl == 0 else "ws1",
                       256 * (gsb // 2 + 1))
                if oc >= 2:
                    S.wait(tensor, "vsc", 2 * (oc - 1))
                for hh in range(2):
                    for hc in range(HC):
                        i = tensor.matmul(
                            pb[e if hh == 0 else o][:],
                            ws[:, sl, hc, (oc % OCPS) * 128:(oc % OCPS + 1) * 128],
                            xl[:, hc, hh * 512:(hh + 1) * 512],
                            start=(hc == 0), stop=(hc == HC - 1))
                S.inc(i, "vps")
            S.wait(tensor, "xlr", 1)
            S.wait(tensor, "vsc", 2 * HC)
            for oc in range(HC):
                sbi = oc // OCPS
                sl = sbi % 2
                gsb = 2 * NSB + sbi
                e, o = (0, 1) if oc % 2 == 0 else (2, 3)
                S.wait(tensor, "ws0" if sl == 0 else "ws1",
                       256 * (gsb // 2 + 1))
                if oc >= 2:
                    S.wait(tensor, "rsc", 2 * (oc - 1))
                for hh in range(2):
                    for hc in range(HC):
                        i = tensor.matmul(
                            pb[e if hh == 0 else o][:],
                            ws[:, sl, hc, (oc % OCPS) * 128:(oc % OCPS + 1) * 128],
                            xl[:, hc, hh * 512:(hh + 1) * 512],
                            start=(hc == 0), stop=(hc == HC - 1))
                S.inc(i, "rps")
        block.tensor(_mm)

        def _sev(scalar):
            for oc in range(HC):
                e, o = (0, 1) if oc % 2 == 0 else (2, 3)
                S.wait(scalar, "mrdy", oc + 1)
                scalar.activation(out=z[:, oc, 0:512], in_=pb[e][:],
                                  func=AF.Exp, bias=bkm[:, oc:oc + 1])
                i = scalar.activation(out=z[:, oc, 512:1024], in_=pb[o][:],
                                      func=AF.Exp, bias=bkm[:, oc:oc + 1])
                S.inc(i, "zev", 2)
            for oc in range(HC):
                e, o = (0, 1) if oc % 2 == 0 else (2, 3)
                S.wait(scalar, "vps", oc + 1)
                if oc >= 1:
                    S.wait(scalar, "ndrdy", oc)
                scalar.activation(out=vtmp[:, 0:512], in_=pb[e][:],
                                  func=AF.Identity, bias=bvcc[:, oc:oc + 1])
                i = scalar.activation(out=vtmp[:, 512:1024], in_=pb[o][:],
                                      func=AF.Identity, bias=bvcc[:, oc:oc + 1])
                S.inc(i, "vsc", 2)
            for oc in range(HC):
                e, o = (0, 1) if oc % 2 == 0 else (2, 3)
                sl = oc % 2
                S.wait(scalar, "rps", oc + 1)
                if oc >= 2:
                    S.wait(scalar, "srst0" if sl == 0 else "srst1",
                           16 * (oc // 2))
                scalar.activation(out=srdb[:, sl, 0:512], in_=pb[e][:],
                                  func=AF.Sigmoid, bias=brcc[:, oc:oc + 1])
                i = scalar.activation(out=srdb[:, sl, 512:1024], in_=pb[o][:],
                                      func=AF.Sigmoid, bias=brcc[:, oc:oc + 1])
                S.inc(i, "rsc", 2)
        block.scalar(_sev)

        def _st(sync):
            for oc in range(HC):
                sl = oc % 2
                S.wait(sync, "ndrdy", oc + 1)
                i = sync.dma_start(out=noutv[:, oc, :], in_=ndb[:, sl, :])
                S.dma(i, f"ndst{sl}")
                i = sync.dma_start(out=doutv[:, oc, :], in_=ddb[:, sl, :])
                S.dma(i, f"ndst{sl}")
            for oc in range(HC):
                sl = oc % 2
                S.wait(sync, "rsc", 2 * (oc + 1))
                i = sync.dma_start(out=sroutv[:, oc, :], in_=srdb[:, sl, :])
                S.dma(i, f"srst{sl}")
            S.wait(sync, "ndrdy", HC)
            i = sync.dma_start(out=totout, in_=tots[:])
            S.dma(i, "totst")
            S.wait(sync, "ndst0", 32 * 8)
            S.wait(sync, "ndst1", 32 * 8)
            S.wait(sync, "srst0", 16 * 8)
            S.wait(sync, "srst1", 16 * 8)
            S.wait(sync, "totst", 16)
        block.sync(_st)

        if G is not None:
            TCH = _tern_chunks()
            shv = {nm: dram(f"sh{nm}", [rs, cols], F32, "ExternalInput")
                   for nm, rows, cols, rs in TMATS}
            gprt = dram("gprt", [128, 8], F32, "Internal")
            gprd = dram("gprd", [128, 8], F32, "Internal")
            tstg = sb("tstg", [128, 2, I1], BF16)
            tgacc = sb("tgacc", [128, 8], F32)
            tmcol = sb("tmcol", [128, 1], F32)

            def _tgld(gpsimd):
                gpsimd.memset(tgacc[:, 7:8], 0.0)
                for n, (mi, nm, off, nr, cols) in enumerate(TCH):
                    sl = n % 2
                    if n >= 2:
                        S.wait(gpsimd, "tgred", n - 1)
                    i = gpsimd.dma_start(out=tstg[0:nr, sl, 0:cols],
                                         in_=shv[nm][off:off + nr, :])
                    S.dma(i, f"tgld{sl}")
                S.wait(gpsimd, "tgred", len(TCH))
                i = gpsimd.dma_start(out=gprt, in_=tgacc[:])
                S.dma(i, "tgx")
                S.wait(gpsimd, "tgx", 16)
                i = gpsimd.collective_compute("AllReduce", ALU.add,
                                              replica_groups=[list(range(8))],
                                              ins=[gprt.opt()],
                                              outs=[gprd.opt()])
                S.g.pin("gar")
                S.g.inc(i, "gar")
            block.gpsimd(_tgld)

            def _tgred(vector):
                for n, (mi, nm, off, nr, cols) in enumerate(TCH):
                    sl = n % 2
                    S.wait(vector, f"tgld{sl}", 16 * (n // 2 + 1))
                    if off == 0:
                        i = vector.reduce_sum(out=tgacc[0:nr, mi:mi + 1],
                                              in_=tstg[0:nr, sl, 0:cols],
                                              axis=AX,
                                              apply_absolute_value=True)
                    else:
                        i = vector.reduce_sum(out=tmcol[0:nr, :],
                                              in_=tstg[0:nr, sl, 0:cols],
                                              axis=AX,
                                              apply_absolute_value=True)
                        c = S.inc(i, "vch")
                        S.wait(vector, "vch", c)
                        i = vector.tensor_tensor(out=tgacc[0:nr, mi:mi + 1],
                                                 in0=tgacc[0:nr, mi:mi + 1],
                                                 in1=tmcol[0:nr, :],
                                                 op=ALU.add)
                    S.inc(i, "tgred")
            block.vector(_tgred)

    return nc



def build_l0_mixer_b(nc=None, G=None, ext=None):
    nc, dram = _phase_setup(nc, ext, "b_")
    nin = dram("nin", [H, TPC], BF16, "ExternalInput")
    din = dram("din", [H, TPC], BF16, "ExternalInput")
    srin = dram("srin", [H, TPC], BF16, "ExternalInput")
    totga = dram("totga", [512, 48], F32, "ExternalInput")
    ownt = dram("ownt", [128, 48], F32, "ExternalInput")
    vg = dram("vg", [128, 8], F32, "ExternalInput")
    woT = dram("woT", [H, H], F32, "ExternalInput")
    boc = dram("boc", [128, HC], F32, "ExternalInput")
    tdec = dram("tdec", [128, HC], F32, "ExternalInput")
    iota = dram("iota", [1, TPC], F32, "ExternalInput")
    xres = dram("xres", [H, TPC], F32, "ExternalInput")
    x2 = dram("x2", [H, TPC], F32, "ExternalOutput")

    ninv = nin.rearrange("(c p) t -> p c t", p=128)
    dinv = din.rearrange("(c p) t -> p c t", p=128)
    srinv = srin.rearrange("(c p) t -> p c t", p=128)
    xrv = xres.rearrange("(c p) t -> p c t", p=128)
    x2v = x2.rearrange("(c p) t -> p c t", p=128)

    SBW = 256
    NSB = H // SBW
    OCPS = SBW // 128
    NTH = 2
    TH = TPC // NTH

    with contextlib.ExitStack() as ctx:
        sb = lambda n, *a: ctx.enter_context(nc.sbuf_tensor(_mangle(n), *a))
        psb = lambda n, *a: ctx.enter_context(nc.psum_tensor(_mangle(n), *a))
        S = PhaseSync(G) if G is not None else PhaseSync(Sync(nc, ctx))

        rwkv = sb("rwkv", [128, HC, TPC], BF16)
        ws = sb("ws", [128, 2, HC, SBW], BF16)
        ndb = sb("ndb", [128, 2, TPC], BF16)
        ddb = sb("ddb", [128, 2, TPC], BF16)
        srdb = sb("srdb", [128, 2, TPC], BF16)
        iotab = sb("iotab", [128, TPC], F32)
        dp = sb("dp", [128, TPC], F32)
        nf = sb("nf", [128, TPC], F32)
        df = sb("df", [128, TPC], F32)
        hres = sb("hres", [128, HC, TH], F32)
        xrf = sb("xrf", [128, 8, TH], F32)
        xo2 = sb("xo2", [128, 2, TH], F32)
        predst = sb("predst", [128, 4, 48], F32)
        ownc = sb("ownc", [128, 48], F32)
        vgc = sb("vgc", [128, 8], F32)
        tdc = sb("tdc", [128, HC], F32)
        dcc = sb("dcc", [128, HC], F32)
        bocc = sb("bocc", [128, HC], F32)
        sinz = sb("sinz", [128, 2, HC], F32)
        tq = sb("tq", [128, 2, HC], F32)
        iorow = sb("iorow", [1, TPC], F32)
        onec = sb("onec", [128, 1], BF16)
        oner = sb("oner", [1, 128], F32)
        epsc = sb("epsc", [1, 1], F32)
        rows = sb("rows", [1, 5, TH], F32)
        pb = [psb(f"pb{i}", [128, 512], F32) for i in range(8)]
        hbf = rwkv[:, 0:HC, 0:TH]
        lnst = sb("lnst", [128, 2 * HC, TH], BF16)
        hbf = lnst[:, 0:HC, :]
        hsq = lnst[:, HC:2 * HC, :]

        block = ctx.enter_context(nc.Block())

        def _consts(sync):
            for ap_, src in ((predst, totga.rearrange("(j p) c -> p j c", p=128)),
                             (ownc, ownt), (tdc, tdec), (bocc, boc),
                             (iorow, iota), (vgc, vg)):
                i = sync.dma_start(out=ap_[:], in_=src)
                S.dma(i, "consts")
        block.sync(_consts)

        def _ones(gpsimd):
            gpsimd.memset(onec[:], 1.0)
            gpsimd.memset(epsc[:], LN_EPS)
            gpsimd.memset(sinz[:], 0.0)
            i = gpsimd.memset(oner[:], 1.0)
            S.inc(i, "ones")
        block.gpsimd(_ones)

        def _prep(scalar):
            S.wait(scalar, "consts", 96)
            i = scalar.activation(out=dcc[:], in_=tdc[:], func=AF.Exp)
            c = S.inc(i, "sch")
            S.wait(scalar, "sch", c)
            i = scalar.mul(out=dcc[:], in_=dcc[:], mul=-1.0)
            S.inc(i, "prep")
        block.scalar(_prep)

        def _iobc(tensor):
            S.wait(tensor, "ones", 1)
            S.wait(tensor, "consts", 96)
            tensor.matmul(pb[0][:], oner[:], iorow[:, 0:512], start=True, stop=True)
            i = tensor.matmul(pb[1][:], oner[:], iorow[:, 512:1024],
                              start=True, stop=True)
            S.inc(i, "iops")
        block.tensor(_iobc)

        def _sin(vector):
            S.wait(vector, "iops", 1)
            vector.tensor_copy(out=iotab[:, 0:512], in_=pb[0][:])
            i = vector.tensor_copy(out=iotab[:, 512:1024], in_=pb[1][:])
            S.inc(i, "iotab")
            S.wait(vector, "prep", 1)
            for j in range(4):
                S.wait(vector, "sinz", j)
                i = vector.tensor_tensor(out=tq[:, 0, :],
                                         in0=predst[:, j, 32:48],
                                         in1=ownc[:, 32:48], op=ALU.subtract)
                c = S.inc(i, "vch")
                i = vector.tensor_scalar(out=tq[:, 1, :], in0=dcc[:],
                                         scalar1=vgc[:, 4 + j:5 + j],
                                         scalar2=None, op0=ALU.mult)
                c = S.inc(i, "vch")
                S.wait(vector, "vch", c)
                i = vector.tensor_tensor(out=tq[:, 0, :], in0=tq[:, 0, :],
                                         in1=tq[:, 1, :], op=ALU.add)
                S.inc(i, f"alpha{j}")
                S.wait(vector, f"alphae{j}", 1)
                i = vector.tensor_scalar(out=tq[:, 1, :], in0=tq[:, 1, :],
                                         scalar1=vgc[:, j:j + 1],
                                         scalar2=None, op0=ALU.mult)
                c = S.inc(i, "vch")
                S.wait(vector, "vch", c)
                i = vector.tensor_tensor(out=tq[:, 0, :], in0=predst[:, j, 0:16],
                                         in1=tq[:, 1, :], op=ALU.mult)
                c = S.inc(i, "vch")
                S.wait(vector, "vch", c)
                i = vector.tensor_tensor(out=sinz[:, 0, :], in0=sinz[:, 0, :],
                                         in1=tq[:, 0, :], op=ALU.add)
                c = S.inc(i, "vch")
                S.wait(vector, "vch", c)
                i = vector.tensor_tensor(out=tq[:, 0, :], in0=predst[:, j, 16:32],
                                         in1=tq[:, 1, :], op=ALU.mult)
                c = S.inc(i, "vch")
                S.wait(vector, "vch", c)
                i = vector.tensor_tensor(out=sinz[:, 1, :], in0=sinz[:, 1, :],
                                         in1=tq[:, 0, :], op=ALU.add)
                S.inc(i, "sinz")
        block.vector(_sin)

        def _alphae(scalar):
            for j in range(4):
                S.wait(scalar, f"alpha{j}", 1)
                i = scalar.activation(out=tq[:, 1, :], in_=tq[:, 0, :], func=AF.Exp)
                S.inc(i, f"alphae{j}")
        block.scalar(_alphae)

        def _ndld(sync):
            for hc in range(HC):
                sl = hc % 2
                if hc >= 2:
                    S.wait(sync, "fix", hc - 1)
                i = sync.dma_start(out=ndb[:, sl, :], in_=ninv[:, hc, :])
                S.dma(i, f"ndld{sl}")
                i = sync.dma_start(out=ddb[:, sl, :], in_=dinv[:, hc, :])
                S.dma(i, f"ndld{sl}")
                i = sync.dma_start(out=srdb[:, sl, :], in_=srinv[:, hc, :])
                S.dma(i, f"ndld{sl}")
        block.sync(_ndld)

        def _dpe(scalar):
            for hc in range(HC):
                S.wait(scalar, "dpv", hc + 1)
                i = scalar.activation(out=dp[:], in_=dp[:], func=AF.Exp)
                S.inc(i, "dpe")
        block.scalar(_dpe)

        def _fix(vector):
            S.wait(vector, "sinz", 4)
            S.wait(vector, "iotab", 1)
            for hc in range(HC):
                sl = hc % 2
                if hc >= 1:
                    S.wait(vector, "fix", hc)
                i = vector.tensor_scalar(out=dp[:], in0=iotab[:],
                                         scalar1=dcc[:, hc:hc + 1], scalar2=None,
                                         op0=ALU.mult)
                S.inc(i, "dpv")
                S.wait(vector, "dpe", hc + 1)
                S.wait(vector, f"ndld{sl}", 48 * (hc // 2 + 1))
                i = vector.scalar_tensor_tensor(
                    out=nf[:], in0=dp[:], scalar=sinz[:, 0, hc:hc + 1],
                    in1=ndb[:, sl, :], op0=ALU.mult, op1=ALU.add)
                c = S.inc(i, "vch")
                i = vector.scalar_tensor_tensor(
                    out=df[:], in0=dp[:], scalar=sinz[:, 1, hc:hc + 1],
                    in1=ddb[:, sl, :], op0=ALU.mult, op1=ALU.add)
                c = S.inc(i, "vch")
                S.wait(vector, "vch", c)
                i = vector.reciprocal(out=df[:], in_=df[:])
                c = S.inc(i, "vch")
                S.wait(vector, "vch", c)
                i = vector.tensor_tensor(out=nf[:], in0=nf[:], in1=df[:],
                                         op=ALU.mult)
                c = S.inc(i, "vch")
                S.wait(vector, "vch", c)
                i = vector.tensor_tensor(out=rwkv[:, hc, :], in0=nf[:],
                                         in1=srdb[:, sl, :], op=ALU.mult)
                S.inc(i, "fix")
        block.vector(_fix)

        def _wdma(gpsimd):
            for h in range(NTH):
                for sbi in range(NSB):
                    sl = sbi % 2
                    gsb = h * NSB + sbi
                    if gsb >= 2:
                        S.wait(gpsimd, "ops", (gsb - 1) * OCPS)
                    for hc in range(HC):
                        i = gpsimd.dma_start(
                            out=ws[:, sl, hc, :],
                            in_=woT[hc * 128:(hc + 1) * 128,
                                    sbi * SBW:(sbi + 1) * SBW])
                        S.dma(i, f"ws{sl}")
        block.gpsimd(_wdma)

        for h in range(NTH):
            def _xrf(sync, h=h):
                for dg in range(2):
                    idx = h * 2 + dg
                    if idx > 0:
                        S.wait(sync, "oev", idx * 8)
                    i = sync.dma_start(
                        out=xrf[:],
                        in_=xrv[:, dg * 8:(dg + 1) * 8, h * TH:(h + 1) * TH])
                    S.dma(i, "xrf")
            block.sync(_xrf)

            def _omm(tensor, h=h):
                S.wait(tensor, "fix", HC)
                if h > 0:
                    S.wait(tensor, "ln_norm", h * HC)
                for oc in range(HC):
                    sbi = oc // OCPS
                    sl = sbi % 2
                    gsb = h * NSB + sbi
                    bank = 0 if oc % 2 == 0 else 1
                    S.wait(tensor, f"ws{sl}", 256 * (gsb // 2 + 1))
                    if oc >= 2:
                        S.wait(tensor, "oev", h * HC + oc - 1)
                    for hc in range(HC):
                        i = tensor.matmul(
                            pb[bank][:],
                            ws[:, sl, hc, (oc % OCPS) * 128:(oc % OCPS + 1) * 128],
                            rwkv[:, hc, h * TH:(h + 1) * TH],
                            start=(hc == 0), stop=(hc == HC - 1))
                    S.inc(i, "ops")
            block.tensor(_omm)

            def _oev(vector, h=h):
                for oc in range(HC):
                    bank = 0 if oc % 2 == 0 else 1
                    dg = oc // 8
                    idx = h * 2 + dg
                    S.wait(vector, "ops", h * HC + oc + 1)
                    S.wait(vector, "xrf", 16 * (idx + 1))
                    i = vector.scalar_tensor_tensor(
                        out=hres[:, oc, :], in0=pb[bank][:],
                        scalar=bocc[:, oc:oc + 1], in1=xrf[:, oc % 8, :],
                        op0=ALU.add, op1=ALU.add)
                    S.inc(i, "oev")
            block.vector(_oev)

            emit_postln(block, S, "ln", h, hres, hbf, hsq, onec, oner, rows,
                        (pb[4], pb[5]), (pb[6], pb[7]), xo2,
                        x2v[:, :, h * TH:(h + 1) * TH], epsc,
                        ("oev", h * HC, 1, 1))

        def _final(sync):
            S.wait(sync, "ln_st0", 16 * HC * NTH // 2)
            S.wait(sync, "ln_st1", 16 * HC * NTH // 2)
        block.sync(_final)

    return nc


def _l0_mixer_a_in_maps(inputs, x_full):
    wkT = np.ascontiguousarray(np.asarray(inputs["L0_attn_Wk"], dtype=np.float32).T)
    wvT = np.ascontiguousarray(np.asarray(inputs["L0_attn_Wv"], dtype=np.float32).T)
    wrT = np.ascontiguousarray(np.asarray(inputs["L0_attn_Wr"], dtype=np.float32).T)
    cc = {"bkc": cols128(inputs["L0_attn_bk"]), "bvc": cols128(inputs["L0_attn_bv"]),
          "brc": cols128(inputs["L0_attn_br"]),
          "tmk": cols128(np.asarray(inputs["L0_attn_tmk"]).reshape(-1)),
          "tmv": cols128(np.asarray(inputs["L0_attn_tmv"]).reshape(-1)),
          "tmr": cols128(np.asarray(inputs["L0_attn_tmr"]).reshape(-1)),
          "tdec": cols128(inputs["L0_attn_time_decay"]),
          "tfir": cols128(inputs["L0_attn_time_first"])}
    maps = []
    for c in range(NCORES):
        b, q = c // 4, c % 4
        lo = q * TPC
        xin = np.zeros((H, TPC + 1), dtype=np.float32)
        xin[:, 1:] = np.asarray(x_full[b, lo:lo + TPC, :], dtype=np.float32).T
        if q > 0:
            xin[:, 0] = np.asarray(x_full[b, lo - 1, :], dtype=np.float32)
        maps.append({"xin": np.ascontiguousarray(xin), "wkT": wkT, "wvT": wvT,
                     "wrT": wrT, **cc})
    return maps


def _l0_mixer_b_in_maps(inputs, x_full, a_results):
    woT = np.ascontiguousarray(np.asarray(inputs["L0_attn_Wo"], dtype=np.float32).T)
    boc = cols128(inputs["L0_attn_bo"])
    tdec = cols128(inputs["L0_attn_time_decay"])
    iota = np.arange(TPC, dtype=np.float32).reshape(1, TPC)
    maps = []
    for c in range(NCORES):
        b, q = c // 4, c % 4
        own = np.ascontiguousarray(a_results[c]["totout"])
        totga = np.vstack([a_results[b * 4 + j]["totout"] for j in range(4)])
        maps.append({"nin": a_results[c]["nout"], "din": a_results[c]["dout"],
                     "srin": a_results[c]["srout"],
                     "totga": np.ascontiguousarray(totga), "ownt": own,
                     "vg": _mixer_vg(q),
                     "woT": woT, "boc": boc, "tdec": tdec, "iota": iota,
                     "xres": fmaj(x_full[b, q * TPC:(q + 1) * TPC, :])})
    return maps


def _mixer_vg(q):
    vg = np.zeros((128, 8), dtype=np.float32)
    for j in range(4):
        if j < q:
            vg[:, j] = 1.0
            vg[:, 4 + j] = 1024.0 * (q - 1 - j)
    return vg


def run_l0_mixer(inputs, x_full):
    nca = build_l0_mixer_a()
    resa = run_bass_kernel_spmd(nca, _l0_mixer_a_in_maps(inputs, x_full),
                                core_ids=list(range(NCORES)))
    ncb = build_l0_mixer_b()
    resb = run_bass_kernel_spmd(ncb, _l0_mixer_b_in_maps(inputs, x_full,
                                                         resa.results),
                                core_ids=list(range(NCORES)))
    return _gather_fm(resb.results, "x2")


MAGIC = 12582912.0
ROPE_SCALE = 1.0


def build_l1_qkv(nc=None, G=None, ext=None):
    nc, dram = _phase_setup(nc, ext, "d_")
    x3 = dram("x3", [H, TPC], F32, "ExternalInput")
    twq = dram("twq", [H, H], FP8, "ExternalInput")
    twk = dram("twk", [H, 512], FP8, "ExternalInput")
    twv = dram("twv", [H, 512], FP8, "ExternalInput")
    swv = dram("swv", [1, 8], F32, "ExternalInput")
    cosr = dram("cosr", [64, TPC], F32, "ExternalInput")
    sinr = dram("sinr", [64, TPC], F32, "ExternalInput")
    idtf = dram("idtf", [128, 128], F32, "ExternalInput")
    idtb = dram("idtb", [128, 128], BF16, "ExternalInput")
    dvs = dram("dvs", [4, 1], F32, "ExternalInput")
    qout = dram("qout", [H, TPC], BF16, "ExternalOutput")
    kout = dram("kout", [1024, TPC], BF16, "ExternalOutput")
    vout = dram("vout", [2 * NKV, TPC, DH], BF16, "ExternalOutput")

    x3v = x3.rearrange("(c p) t -> p c t", p=128)
    qov = qout.rearrange("(c p) t -> p c t", p=128)
    kov = kout.rearrange("(c p) t -> p c t", p=128)
    vov = vout.rearrange("kv (tb p) d -> p kv tb d", p=128)

    SBW = 256
    NTB = TPC // 128
    NOCQ, NOCK = NH, NKV
    SBL = []
    for m, src, noc in ((0, twq, NOCQ), (1, twk, NOCK), (2, twv, NOCK)):
        for sbi in range(noc * 128 // SBW):
            SBL.append((m, src, sbi))
    OCL = []
    for gsb, (m, src, sbi) in enumerate(SBL):
        for obi in range(SBW // 128):
            OCL.append((m, sbi * (SBW // 128) + obi, gsb, obi))

    with contextlib.ExitStack() as ctx:
        sb = lambda n, *a: ctx.enter_context(nc.sbuf_tensor(_mangle(n), *a))
        psb = lambda n, *a: ctx.enter_context(nc.psum_tensor(_mangle(n), *a))
        S = PhaseSync(G) if G is not None else PhaseSync(Sync(nc, ctx))

        xq = sb("xq", [128, HC, TPC], BF16)
        q = sb("q", [128, NH, TPC], BF16)
        kk = sb("kk", [128, NKV, TPC], BF16)
        vraw = sb("vraw", [128, NKV, TPC], BF16)
        vtm = sb("vtm", [128, NKV * NTB, 128], BF16)
        terb = sb("terb", [128, 2, HC, SBW], FP8)
        xjit = sb("xjit", [128, 2, TPC], F32)
        cosb = sb("cosb", [64, TPC], BF16)
        sinb = sb("sinb", [64, TPC], BF16)
        rt = sb("rt", [64, 5, TPC], BF16)
        idf = sb("idf", [128, 128], F32)
        idb = sb("idb", [128, 128], BF16)
        dvsc = sb("dvsc", [4, 1], F32)
        grow = sb("grow", [1, 8], F32)
        swcc = sb("swcc", [128, 1], F32)
        mx8 = sb("mx8", [128, 8], F32)
        mxc = sb("mxc", [128, 8], F32)
        scvc = sb("scvc", [128, 8], F32)
        mcol = sb("mcol", [128, 1], F32)
        msb = sb("msb", [8, 128], F32)
        mrow = sb("mrow", [1, TPC], F32)
        sxrow = sb("sxrow", [1, TPC], F32)
        screw = sb("screw", [1, 2, TPC], F32)
        sxb = sb("sxb", [128, TPC], F32)
        scb = sb("scb", [128, 2, TPC], F32)
        tmp = sb("tmp", [128, TPC], F32)
        oner = sb("oner", [1, 128], F32)
        pb = [psb(f"pb{i}", [128, 512], F32) for i in range(6)]
        pbt = psb("pbt", [128, 128], BF16)

        block = ctx.enter_context(nc.Block())

        def _consts(sync):
            for ap_, src in ((idf, idtf), (dvsc, dvs)):
                i = sync.dma_start(out=ap_[:], in_=src)
                S.dma(i, "consts")
        block.sync(_consts)

        def _init(gpsimd):
            gpsimd.memset(mx8[:], 0.0)
            i = gpsimd.memset(oner[:], 1.0)
            S.inc(i, "ones")
            i = gpsimd.dma_start(out=cosb[:], in_=cosr)
            S.dma(i, "ropet")
            i = gpsimd.dma_start(out=sinb[:], in_=sinr)
            S.dma(i, "ropet")
            i = gpsimd.dma_start(out=idb[:], in_=idtb)
            S.dma(i, "idbl")
            if G is not None:
                S.g.wait(gpsimd, "cct", 7)
            i = gpsimd.dma_start(out=grow[:, 0:8], in_=swv)
            S.dma(i, "gx")
        block.gpsimd(_init)

        def _gbc(tensor):
            S.wait(tensor, "gx", 16)
            S.wait(tensor, "ones", 1)
            i = tensor.matmul(pb[4][:, 0:4], oner[:], grow[:, 0:4],
                              start=True, stop=True)
            S.inc(i, "gbc")
        block.tensor(_gbc)

        def _gcols(vector):
            S.wait(vector, "gbc", 1)
            i = vector.tensor_copy(out=swcc[:], in_=pb[4][:, 2:3])
            S.inc(i, "gcols")
        block.vector(_gcols)

        def _xjit1(sync):
            for hc in range(HC):
                sl = hc % 2
                if hc >= 2:
                    S.wait(sync, "tmax", hc - 1)
                i = sync.dma_start(out=xjit[:, sl, :], in_=x3v[:, hc, :])
                S.dma(i, f"xj{sl}")
        block.sync(_xjit1)

        def _xtp(tensor):
            S.wait(tensor, "consts", 32)
            S.wait(tensor, "gbc", 1)
            for hc in range(HC):
                sl = hc % 2
                S.wait(tensor, f"xj{sl}", 16 * (hc // 2 + 1))
                if hc >= 1:
                    S.wait(tensor, "tmax", hc)
                for tb in range(NTB):
                    bank = tb // 4
                    i = tensor.transpose(
                        pb[bank][:, (tb % 4) * 128:(tb % 4 + 1) * 128],
                        xjit[:, sl, tb * 128:(tb + 1) * 128], idf[:])
                S.inc(i, "xtp")
        block.tensor(_xtp)

        def _vmax(vector):
            for hc in range(HC):
                S.wait(vector, "xtp", hc + 1)
                for tb in range(NTB):
                    bank = tb // 4
                    i = vector.reduce_max(
                        out=mcol[:],
                        in_=pb[bank][:, (tb % 4) * 128:(tb % 4 + 1) * 128],
                        axis=AX, apply_absolute_value=True)
                    c = S.inc(i, "vch")
                    S.wait(vector, "vch", c)
                    i = vector.tensor_tensor(out=mx8[:, tb:tb + 1],
                                             in0=mx8[:, tb:tb + 1], in1=mcol[:],
                                             op=ALU.max)
                    if tb == NTB - 1:
                        S.inc(i, "tmax")
                    else:
                        c = S.inc(i, "vch")
                        S.wait(vector, "vch", c)
        block.vector(_vmax)

        def _mclip(vector):
            S.wait(vector, "tmax", HC)
            S.wait(vector, "gcols", 1)
            i = vector.tensor_scalar(out=mxc[:], in0=mx8[:], scalar1=1e-5,
                                     scalar2=None, op0=ALU.max)
            c = S.inc(i, "vch")
            S.wait(vector, "vch", c)
            i = vector.tensor_scalar(out=scvc[:], in0=mxc[:],
                                     scalar1=swcc[:],
                                     scalar2=1.0 / 127.0,
                                     op0=ALU.mult, op1=ALU.mult)
            S.inc(i, "mclip")
        block.vector(_mclip)

        def _mtp(tensor):
            S.wait(tensor, "mclip", 1)
            i = tensor.transpose(pb[5][0:8, 0:128], mxc[:], idf[:])
            S.inc(i, "mtp")
        block.tensor(_mtp)

        def _mev(vector):
            S.wait(vector, "mtp", 1)
            i = vector.tensor_copy(out=msb[:], in_=pb[5][0:8, 0:128])
            S.inc(i, "mev")
        block.vector(_mev)

        def _mmv(gpsimd):
            S.wait(gpsimd, "mev", 1)
            for tb in range(NTB):
                i = gpsimd.dma_start(out=mrow[:, tb * 128:(tb + 1) * 128],
                                     in_=msb[tb:tb + 1, :])
                S.dma(i, "mrowd")
        block.gpsimd(_mmv)

        def _sxr(vector):
            S.wait(vector, "mrowd", 128)
            i = vector.reciprocal(out=sxrow[:], in_=mrow[:])
            c = S.inc(i, "vch")
            S.wait(vector, "vch", c)
            i = vector.tensor_scalar(out=sxrow[:], in0=sxrow[:], scalar1=127.0,
                                     scalar2=None, op0=ALU.mult)
            c = S.inc(i, "vch")
            for m in range(2):
                i = vector.tensor_scalar(out=screw[:, m, :], in0=mrow[:],
                                         scalar1=grow[:, m:m + 1],
                                         scalar2=1.0 / 127.0,
                                         op0=ALU.mult, op1=ALU.mult)
                c = S.inc(i, "vch")
            S.wait(vector, "vch", c)
            i = vector.nop()
            S.inc(i, "sxr")
        block.vector(_sxr)

        def _sxbc(tensor):
            S.wait(tensor, "sxr", 1)
            for half in range(2):
                tensor.matmul(pb[0][:], oner[:],
                              sxrow[:, half * 512:(half + 1) * 512],
                              start=True, stop=True)
                tensor.matmul(pb[1][:], oner[:],
                              screw[:, 0, half * 512:(half + 1) * 512],
                              start=True, stop=True)
                i = tensor.matmul(pb[2][:], oner[:],
                                  screw[:, 1, half * 512:(half + 1) * 512],
                                  start=True, stop=True)
                S.inc(i, "sxps")
                S.wait(tensor, "sxev", half + 1)
        block.tensor(_sxbc)

        def _sxev(vector):
            for half in range(2):
                S.wait(vector, "sxps", half + 1)
                vector.tensor_copy(out=sxb[:, half * 512:(half + 1) * 512],
                                   in_=pb[0][:])
                vector.tensor_copy(out=scb[:, 0, half * 512:(half + 1) * 512],
                                   in_=pb[1][:])
                i = vector.tensor_copy(out=scb[:, 1, half * 512:(half + 1) * 512],
                                       in_=pb[2][:])
                S.inc(i, "sxev")
        block.vector(_sxev)

        def _xjit2(sync):
            for hc in range(HC):
                sl = hc % 2
                if hc >= 2:
                    S.wait(sync, "xqv", hc - 1)
                else:
                    S.wait(sync, "xtp", HC)
                i = sync.dma_start(out=xjit[:, sl, :], in_=x3v[:, hc, :])
                S.dma(i, f"xq{sl}")
        block.sync(_xjit2)

        def _xqv(vector):
            S.wait(vector, "sxev", 2)
            for hc in range(HC):
                sl = hc % 2
                S.wait(vector, f"xq{sl}", 16 * (hc // 2 + 1))
                S.wait(vector, "xqv", hc)
                i = vector.tensor_tensor(out=tmp[:], in0=xjit[:, sl, :],
                                         in1=sxb[:], op=ALU.mult)
                c = S.inc(i, "vch")
                S.wait(vector, "vch", c)
                i = vector.tensor_scalar(out=tmp[:], in0=tmp[:], scalar1=MAGIC,
                                         scalar2=None, op0=ALU.add)
                c = S.inc(i, "vch")
                S.wait(vector, "vch", c)
                i = vector.tensor_scalar(out=xq[:, hc, :], in0=tmp[:],
                                         scalar1=MAGIC, scalar2=None,
                                         op0=ALU.subtract)
                S.inc(i, "xqv")
        block.vector(_xqv)

        def _wdma(gpsimd):
            for gsb, (m, src, sbi) in enumerate(SBL):
                sl = gsb % 2
                if gsb >= 2:
                    S.wait(gpsimd, "mmps", 2 * (gsb - 1))
                for hc in range(HC):
                    i = gpsimd.dma_start(
                        out=terb[:, sl, hc, :],
                        in_=src[hc * 128:(hc + 1) * 128,
                                sbi * SBW:(sbi + 1) * SBW])
                    S.dma(i, f"ws{sl}")
        block.gpsimd(_wdma)

        def _vec3(vector):
            S.wait(vector, "ropet", 32)

            def evac(go):
                m, oc, gsb, obi = OCL[go]
                bank = go % 2
                S.wait(vector, "mmps", go + 1)
                for hh in range(2):
                    if m == 0:
                        dsl = q[:, oc, hh * 512:(hh + 1) * 512]
                        scs = scb[:, 0, hh * 512:(hh + 1) * 512]
                    elif m == 1:
                        dsl = kk[:, oc, hh * 512:(hh + 1) * 512]
                        scs = scb[:, 1, hh * 512:(hh + 1) * 512]
                    else:
                        dsl = vraw[:, oc, hh * 512:(hh + 1) * 512]
                        scs = None
                    if m == 2:
                        i = vector.tensor_copy(out=dsl, in_=pb[bank * 2 + hh][:])
                    else:
                        i = vector.tensor_tensor(out=dsl,
                                                 in0=pb[bank * 2 + hh][:],
                                                 in1=scs, op=ALU.mult)
                    S.inc(i, "qev")

            def rope1(gh, buf, hd):
                S.wait(vector, "roped", gh)
                S.wait(vector, "qev", S.counts["qev"])
                x1 = buf[0:64, hd, :]
                x2 = buf[64:128, hd, :]
                i = vector.tensor_copy(out=rt[:, 2, :], in_=x2)
                c = S.inc(i, "vch")
                i = vector.tensor_tensor(out=rt[:, 0, :], in0=x1,
                                         in1=cosb[:], op=ALU.mult)
                S.inc(i, "vch")
                S.wait(vector, "vch", c)
                i = vector.tensor_tensor(out=rt[:, 1, :], in0=rt[:, 2, :],
                                         in1=sinb[:], op=ALU.mult)
                S.inc(i, "vch")
                i = vector.tensor_tensor(out=rt[:, 3, :], in0=rt[:, 2, :],
                                         in1=cosb[:], op=ALU.mult)
                S.inc(i, "vch")
                i = vector.tensor_tensor(out=rt[:, 4, :], in0=x1,
                                         in1=sinb[:], op=ALU.mult)
                c = S.inc(i, "vch")
                S.wait(vector, "vch", c)
                i = vector.tensor_tensor(out=x1, in0=rt[:, 0, :],
                                         in1=rt[:, 1, :], op=ALU.subtract)
                S.inc(i, "vch")
                i = vector.tensor_tensor(out=rt[:, 1, :], in0=rt[:, 3, :],
                                         in1=rt[:, 4, :], op=ALU.add)
                c = S.inc(i, "vch")
                S.wait(vector, "vch", c)
                i = vector.tensor_copy(out=x2, in_=rt[:, 1, :])
                S.inc(i, "roped")

            def post(go):
                m, oc, gsb, obi = OCL[go]
                if m == 0:
                    rope1(go, q, oc)
                elif m == 1:
                    rope1(NH + oc, kk, oc)

            for go in range(len(OCL)):
                evac(go)
                post(go)
        block.vector(_vec3)

        def _mm(tensor):
            S.wait(tensor, "xqv", HC)
            S.wait(tensor, "sxev", 2)
            for go, (m, oc, gsb, obi) in enumerate(OCL):
                sl = gsb % 2
                bank = go % 2
                S.wait(tensor, f"ws{sl}", 256 * (gsb // 2 + 1))
                if go >= 2:
                    S.wait(tensor, "qev", 2 * (go - 1))
                for hh in range(2):
                    for hc in range(HC):
                        i = tensor.matmul(
                            pb[bank * 2 + hh][:],
                            terb[:, sl, hc, obi * 128:(obi + 1) * 128],
                            xq[:, hc, hh * 512:(hh + 1) * 512],
                            start=(hc == 0), stop=(hc == HC - 1))
                S.inc(i, "mmps")
        block.tensor(_mm)

        def _vtp(tensor):
            S.wait(tensor, "idbl", 16)
            S.wait(tensor, "qev", 48)
            n = 0
            for kv in range(NKV):
                for tb in range(NTB):
                    if n >= 1:
                        S.wait(tensor, "vtev", n)
                    i = tensor.transpose(pbt[:],
                                         vraw[:, kv, tb * 128:(tb + 1) * 128],
                                         idb[:])
                    S.inc(i, "vtp")
                    n += 1
        block.tensor(_vtp)

        def _vtev(vector):
            S.wait(vector, "mclip", 1)
            n = 0
            for kv in range(NKV):
                for tb in range(NTB):
                    S.wait(vector, "vtp", n + 1)
                    i = vector.tensor_scalar(out=vtm[:, kv * NTB + tb, :],
                                             in0=pbt[:],
                                             scalar1=scvc[:, tb:tb + 1],
                                             scalar2=None, op0=ALU.mult)
                    S.inc(i, "vtev")
                    n += 1
        block.vector(_vtev)

        def _st(sync):
            for hd in range(NH):
                S.wait(sync, "roped", hd + 1)
                i = sync.dma_start(out=qov[:, hd, :], in_=q[:, hd, :])
                S.dma(i, "qst")
            for hd in range(NKV):
                S.wait(sync, "roped", NH + hd + 1)
                i = sync.dma_start(out=kov[:, 2 * hd, :], in_=kk[:, hd, :])
                S.dma(i, "qst")
                i = sync.dma_start(out=kov[:, 2 * hd + 1, :], in_=kk[:, hd, :])
                S.dma(i, "qst")
            for kv in range(NKV):
                for tb in range(NTB):
                    S.wait(sync, "vtev", kv * NTB + tb + 1)
                    i = sync.dma_start(out=vov[:, 2 * kv, tb, :],
                                       in_=vtm[:, kv * NTB + tb, :])
                    S.dma(i, "qst")
                    i = sync.dma_start(out=vov[:, 2 * kv + 1, tb, :],
                                       in_=vtm[:, kv * NTB + tb, :])
                    S.dma(i, "qst")
            S.wait(sync, "qst", 16 * (NH + 2 * NKV + 2 * NKV * NTB))
        block.sync(_st)

    return nc



def build_l1_attn(nc=None, G=None, ext=None):
    nc, dram = _phase_setup(nc, ext, "e_")
    qin = dram("qin", [4 * DH, T], BF16, "ExternalInput")
    kin = dram("kin", [DH, T], BF16, "ExternalInput")
    vin = dram("vin", [T, DH], BF16, "ExternalInput")
    msk = dram("msk", [128, 4, 512], BF16, "ExternalInput")
    aout = dram("aout", [4 * DH, T], BF16, "ExternalOutput")
    sout = dram("sout", [4, T], F32, "ExternalOutput")

    qv = qin.rearrange("(h p) t -> p h t", p=128)
    av = aout.rearrange("(h p) t -> p h t", p=128)
    vv = vin.rearrange("(kb p) d -> p kb d", p=128)
    SCALE = float(1.0 / np.sqrt(DH))
    NQB = T // 512

    with contextlib.ExitStack() as ctx:
        sb = lambda n, *a: ctx.enter_context(nc.sbuf_tensor(_mangle(n), *a))
        psb = lambda n, *a: ctx.enter_context(nc.psum_tensor(_mangle(n), *a))
        S = PhaseSync(G) if G is not None else PhaseSync(Sync(nc, ctx))

        qt = sb("qt", [128, 4, T], BF16)
        kt = sb("kt", [128, T], BF16)
        vt = sb("vt", [128, T // 128, DH], BF16)
        mskt = sb("mskt", [128, 4, 512], BF16)
        pt = sb("pt", [128, 2, 512], BF16)
        onec = sb("onec", [128, 1], BF16)
        ao = sb("ao", [128, 4, T], BF16)
        sums = sb("sums", [1, 4 * T], F32)
        pb = [psb(f"pb{i}", [128, 512], F32) for i in range(4)]

        block = ctx.enter_context(nc.Block())

        def _ld(sync):
            i = sync.dma_start(out=qt[:], in_=qv)
            S.dma(i, "ld")
            i = sync.dma_start(out=kt[:], in_=kin)
            S.dma(i, "ld")
            i = sync.dma_start(out=vt[:], in_=vv)
            S.dma(i, "ld")
            i = sync.dma_start(out=mskt[:], in_=msk)
            S.dma(i, "ld")
        block.sync(_ld)

        def _ones(gpsimd):
            i = gpsimd.memset(onec[:], 1.0)
            S.inc(i, "ones")
        block.gpsimd(_ones)

        nit = 0
        ni = 0
        nb = 0
        for hd in range(4):
            for qb in range(NQB):
                nkb = (qb + 1) * 4
                ni0, nb0 = ni, nb
                def _qk(tensor, hd=hd, qb=qb, nkb=nkb, nit=nit, ni0=ni0,
                        nb0=nb0):
                    S.wait(tensor, "ld", 64)
                    S.wait(tensor, "ones", 1)
                    g = hd * NQB + qb
                    if g > 0:
                        S.wait(tensor, "oev", 2 * g)
                    lni, lnb = ni0, nb0
                    for j in range(nkb):
                        it = nit + j
                        sl = j % 2
                        if j >= 2:
                            S.wait(tensor, "pvd", it - 1)
                        i = tensor.matmul(pb[sl][:],
                                          kt[:, j * 128:(j + 1) * 128],
                                          qt[:, hd, qb * 512:(qb + 1) * 512],
                                          start=True, stop=True)
                        S.inc(i, "qkps")
                        if j < qb * 4:
                            lni += 1
                            S.wait(tensor, "ptei", lni)
                        else:
                            lnb += 1
                            S.wait(tensor, "pteb", lnb)
                        i = tensor.matmul(pb[2][:], vt[:, j, :], pt[:, sl, :],
                                          start=(j == 0), stop=(j == nkb - 1))
                        i = tensor.matmul(pb[3][0:1, :], onec[:], pt[:, sl, :],
                                          start=(j == 0), stop=(j == nkb - 1))
                        S.inc(i, "pvd")
                block.tensor(_qk)

                def _pt(scalar, hd=hd, qb=qb, nkb=nkb, nit=nit, ni0=ni0,
                        nb0=nb0):
                    lni = ni0
                    for j in range(nkb):
                        it = nit + j
                        sl = j % 2
                        S.wait(scalar, "qkps", it + 1)
                        if j >= 2:
                            S.wait(scalar, "pvd", it - 1)
                        if j < qb * 4:
                            lni += 1
                            i = scalar.activation(out=pt[:, sl, :],
                                                  in_=pb[sl][:],
                                                  func=AF.Exp, scale=SCALE)
                            S.inc(i, "ptei")
                        else:
                            i = scalar.activation(out=pt[:, sl, :],
                                                  in_=pb[sl][:],
                                                  func=AF.Exp, scale=SCALE)
                            S.inc(i, "ptee")
                block.scalar(_pt)

                def _msk(vector, hd=hd, qb=qb, nkb=nkb, nit=nit, nb0=nb0):
                    lnb = nb0
                    for j in range(qb * 4, nkb):
                        sl = j % 2
                        off = j - qb * 4
                        lnb += 1
                        S.wait(vector, "ptee", lnb)
                        i = vector.tensor_tensor(out=pt[:, sl, :],
                                                 in0=pt[:, sl, :],
                                                 in1=mskt[:, off, :],
                                                 op=ALU.mult)
                        S.inc(i, "pteb")
                block.vector(_msk)

                def _oev(vector, hd=hd, qb=qb, nit=nit, nkb=nkb):
                    S.wait(vector, "pvd", nit + nkb)
                    i = vector.tensor_copy(out=ao[:, hd, qb * 512:(qb + 1) * 512],
                                           in_=pb[2][:])
                    S.inc(i, "oev")
                    i = vector.tensor_copy(
                        out=sums[:, hd * T + qb * 512:hd * T + (qb + 1) * 512],
                        in_=pb[3][0:1, :])
                    S.inc(i, "oev")
                block.vector(_oev)

                nit += nkb
                ni += qb * 4
                nb += 4

        def _st(sync):
            for hd in range(4):
                S.wait(sync, "oev", 2 * ((hd + 1) * NQB))
                i = sync.dma_start(out=av[:, hd, :], in_=ao[:, hd, :])
                S.dma(i, "ast")
                i = sync.dma_start(
                    out=sout[hd:hd + 1, :],
                    in_=sums[:, hd * T:(hd + 1) * T])
                S.dma(i, "ast")
            S.wait(sync, "ast", 16 * 8)
        block.sync(_st)

    return nc


def build_l1_oproj(nc=None, G=None, ext=None):
    nc, dram = _phase_setup(nc, ext, "f_")
    ain = dram("ain", [H, TPC], BF16, "ExternalInput")
    sin_ = dram("sin_", [NH, TPC], F32, "ExternalInput")
    two = dram("two", [H, H], FP8, "ExternalInput")
    swv = dram("swv", [1, 8], F32, "ExternalInput")
    x3r = dram("x3r", [H, TPC], F32, "ExternalInput")
    idtf = dram("idtf", [128, 128], F32, "ExternalInput")
    idtb = dram("idtb", [128, 128], BF16, "ExternalInput")
    x4 = dram("x4", [H, TPC], F32, "ExternalOutput")
    fscr = dram("fscr", [NH, TPC], F32, "Internal")

    av = ain.rearrange("(c p) t -> p c t", p=128)
    xrv = x3r.rearrange("(c p) t -> p c t", p=128)
    x4v = x4.rearrange("(c p) t -> p c t", p=128)

    SBW = 256
    NSB = H // SBW
    OCPSF = SBW // 128
    NTB = TPC // 128
    NTH = 2
    TH = 512

    with contextlib.ExitStack() as ctx:
        sb = lambda n, *a: ctx.enter_context(nc.sbuf_tensor(_mangle(n), *a))
        psb = lambda n, *a: ctx.enter_context(nc.psum_tensor(_mangle(n), *a))
        S = PhaseSync(G) if G is not None else PhaseSync(Sync(nc, ctx))

        at = sb("at", [128, HC, TPC], BF16)
        aq = sb("aq", [128, HC, TPC], BF16)
        terb = sb("terb", [128, 2, HC, SBW], FP8)
        srec = sb("srec", [NH, TPC], F32)
        fact = sb("fact", [NH, TPC], F32)
        frj = sb("frj", [1, 2, TPC], F32)
        idf = sb("idf", [128, 128], F32)
        idb = sb("idb", [128, 128], BF16)
        grow = sb("grow", [1, 2], F32)
        mx8 = sb("mx8", [128, 8], F32)
        mcol = sb("mcol", [128, 1], F32)
        msb = sb("msb", [8, 128], F32)
        mrow = sb("mrow", [1, TPC], F32)
        screw = sb("screw", [1, TPC], F32)
        scb = sb("scb", [128, TPC], F32)
        tmp = sb("tmp", [128, TPC], F32)
        hres = sb("hres", [128, HC, TH], F32)
        x3j = sb("x3j", [128, 8, TH], F32)
        xo2 = sb("xo2", [128, 2, TH], F32)
        onec = sb("onec", [128, 1], BF16)
        oner = sb("oner", [1, 128], F32)
        epsc = sb("epsc", [1, 1], F32)
        rows = sb("rows", [1, 5, TH], F32)
        hbf = at[:, :, 0:TH]
        hsq = at[:, :, TH:TPC]
        pb = [psb(f"pb{i}", [128, 512], F32) for i in range(6)]
        pbt = psb("pbt", [128, 128], BF16)

        block = ctx.enter_context(nc.Block())

        def _consts(sync):
            i = sync.dma_start(out=idf[:], in_=idtf)
            S.dma(i, "consts")
            i = sync.dma_start(out=idb[:], in_=idtb)
            S.dma(i, "consts")
            i = sync.dma_start(out=srec[:], in_=sin_)
            S.dma(i, "consts")
            i = sync.dma_start(out=at[:], in_=av)
            S.dma(i, "consts")
        block.sync(_consts)

        def _init(gpsimd):
            gpsimd.memset(mx8[:], 0.0)
            gpsimd.memset(onec[:], 1.0)
            gpsimd.memset(epsc[:], LN_EPS)
            i = gpsimd.memset(oner[:], 1.0)
            S.inc(i, "ones")
            if G is not None:
                S.g.wait(gpsimd, "cct", 7)
            i = gpsimd.dma_start(out=grow[:, 0:1], in_=swv[0:1, 3:4])
            S.dma(i, "gx")
        block.gpsimd(_init)

        def _srq(vector):
            S.wait(vector, "consts", 64)
            i = vector.reciprocal(out=srec[:], in_=srec[:])
            S.inc(i, "srq")
        block.vector(_srq)

        def _atp(tensor):
            S.wait(tensor, "consts", 64)
            S.wait(tensor, "srq", 1)
            for tb in range(NTB):
                if tb > 0:
                    S.wait(tensor, "hmax", tb * NH)
                i = tensor.transpose(pb[5][:, 0:NH],
                                     srec[:, tb * 128:(tb + 1) * 128],
                                     idf[0:NH, 0:NH])
                S.inc(i, "stp")
                for hh in range(NH):
                    n = tb * NH + hh
                    if n >= 1:
                        S.wait(tensor, "hmax", n)
                    i = tensor.transpose(pbt[:],
                                         at[:, hh, tb * 128:(tb + 1) * 128],
                                         idb[:])
                    S.inc(i, "atp")
        block.tensor(_atp)

        def _amax(vector):
            for tb in range(NTB):
                S.wait(vector, "stp", tb + 1)
                for hh in range(NH):
                    S.wait(vector, "atp", tb * NH + hh + 1)
                    i = vector.reduce_max(out=mcol[:], in_=pbt[:], axis=AX,
                                          apply_absolute_value=True)
                    c = S.inc(i, "vch")
                    S.wait(vector, "vch", c)
                    i = vector.tensor_tensor(out=mcol[:], in0=mcol[:],
                                             in1=pb[5][0:128, hh:hh + 1]
                                             if False else pb[5][:, hh:hh + 1],
                                             op=ALU.mult)
                    c = S.inc(i, "vch")
                    S.wait(vector, "vch", c)
                    i = vector.tensor_tensor(out=mx8[:, tb:tb + 1],
                                             in0=mx8[:, tb:tb + 1],
                                             in1=mcol[:], op=ALU.max)
                    S.inc(i, "hmax")
                i = vector.nop()
                S.inc(i, "tbmax")
        block.vector(_amax)


        def _mtp(tensor):
            S.wait(tensor, "hmax", NTB * NH)
            i = tensor.transpose(pb[5][0:8, 0:128], mx8[:], idf[:])
            S.inc(i, "mtp")
        block.tensor(_mtp)

        def _mev(vector):
            S.wait(vector, "mtp", 1)
            i = vector.tensor_copy(out=msb[:], in_=pb[5][0:8, 0:128])
            S.inc(i, "mev")
        block.vector(_mev)

        def _mmv(gpsimd):
            S.wait(gpsimd, "mev", 1)
            for tb in range(NTB):
                i = gpsimd.dma_start(out=mrow[:, tb * 128:(tb + 1) * 128],
                                     in_=msb[tb:tb + 1, :])
                S.dma(i, "mrowd")
        block.gpsimd(_mmv)

        def _sxr(vector):
            S.wait(vector, "mrowd", 128)
            S.wait(vector, "gx", 16)
            i = vector.tensor_scalar(out=mrow[:], in0=mrow[:], scalar1=1e-5,
                                     scalar2=None, op0=ALU.max)
            c = S.inc(i, "vch")
            S.wait(vector, "vch", c)
            i = vector.reciprocal(out=frj[:, 0, :], in_=mrow[:])
            c = S.inc(i, "vch")
            S.wait(vector, "vch", c)
            i = vector.tensor_scalar(out=frj[:, 0, :], in0=frj[:, 0, :],
                                     scalar1=127.0, scalar2=None, op0=ALU.mult)
            c = S.inc(i, "vch")
            S.wait(vector, "vch", c)
            i = vector.tensor_scalar(out=screw[:], in0=mrow[:],
                                     scalar1=grow[:, 0:1], scalar2=1.0 / 127.0,
                                     op0=ALU.mult, op1=ALU.mult)
            S.inc(i, "sxr")
        block.vector(_sxr)

        def _fbc(tensor):
            S.wait(tensor, "sxr", 1)
            for half in range(2):
                i = tensor.matmul(pb[0][0:NH, :], oner[:, 0:NH],
                                  frj[:, 0, half * 512:(half + 1) * 512],
                                  start=True, stop=True)
                S.inc(i, "fbps")
                S.wait(tensor, "fbev", half + 1)
        block.tensor(_fbc)

        def _fev(vector):
            for half in range(2):
                S.wait(vector, "fbps", half + 1)
                i = vector.tensor_tensor(out=fact[:, half * 512:(half + 1) * 512],
                                         in0=srec[:, half * 512:(half + 1) * 512],
                                         in1=pb[0][0:NH, :], op=ALU.mult)
                S.inc(i, "fbev")
        block.vector(_fev)

        def _fst(gpsimd):
            S.wait(gpsimd, "fbev", 2)
            i = gpsimd.dma_start(out=fscr, in_=fact[:])
            S.dma(i, "fst")
        block.gpsimd(_fst)

        def _scbc(tensor):
            S.wait(tensor, "fbev", 2)
            for half in range(2):
                i = tensor.matmul(pb[1][:], oner[:],
                                  screw[:, half * 512:(half + 1) * 512],
                                  start=True, stop=True)
                S.inc(i, "scps")
                S.wait(tensor, "scev", half + 1)
        block.tensor(_scbc)

        def _scev(vector):
            for half in range(2):
                S.wait(vector, "scps", half + 1)
                i = vector.tensor_copy(out=scb[:, half * 512:(half + 1) * 512],
                                       in_=pb[1][:])
                S.inc(i, "scev")
        block.vector(_scev)

        def _frl(sync):
            S.wait(sync, "fst", 16)
            for hh in range(NH):
                sl = hh % 2
                if hh >= 2:
                    S.wait(sync, "aqv", hh - 1)
                i = sync.dma_start(out=frj[:, sl, :], in_=fscr[hh:hh + 1, :])
                S.dma(i, f"frl{sl}")
        block.sync(_frl)

        def _qbc(tensor):
            for hh in range(NH):
                sl = hh % 2
                S.wait(tensor, f"frl{sl}", 16 * (hh // 2 + 1))
                if hh >= 1:
                    S.wait(tensor, "aqpc", 2 * hh)
                for half in range(2):
                    i = tensor.matmul(pb[2 + half][:], oner[:],
                                      frj[:, sl, half * 512:(half + 1) * 512],
                                      start=True, stop=True)
                S.inc(i, "qbps")
        block.tensor(_qbc)

        def _aqv(vector):
            for hh in range(NH):
                S.wait(vector, "qbps", hh + 1)
                S.wait(vector, "aqv", hh)
                for half in range(2):
                    i = vector.tensor_tensor(
                        out=tmp[:, half * 512:(half + 1) * 512],
                        in0=at[:, hh, half * 512:(half + 1) * 512],
                        in1=pb[2 + half][:], op=ALU.mult)
                    c = S.inc(i, "vch")
                    S.wait(vector, "vch", c)
                    S.inc(vector.nop(), "aqpc")
                i = vector.tensor_scalar(out=tmp[:], in0=tmp[:], scalar1=MAGIC,
                                         scalar2=None, op0=ALU.add)
                c = S.inc(i, "vch")
                S.wait(vector, "vch", c)
                i = vector.tensor_scalar(out=aq[:, hh, :], in0=tmp[:],
                                         scalar1=MAGIC, scalar2=None,
                                         op0=ALU.subtract)
                S.inc(i, "aqv")
        block.vector(_aqv)

        def _wdma2(gpsimd):
            for h in range(NTH):
                for sbi in range(NSB):
                    sl = sbi % 2
                    gsb = h * NSB + sbi
                    if gsb >= 2:
                        S.wait(gpsimd, "omm", (gsb - 1) * OCPSF)
                    for hc in range(HC):
                        i = gpsimd.dma_start(
                            out=terb[:, sl, hc, :],
                            in_=two[hc * 128:(hc + 1) * 128,
                                    sbi * SBW:(sbi + 1) * SBW])
                        S.dma(i, f"wsl{sl}")
        block.gpsimd(_wdma2)

        for h in range(NTH):
            def _x3ld(sync, h=h):
                for dg in range(2):
                    idx = h * 2 + dg
                    if idx > 0:
                        S.wait(sync, "oev", idx * 8)
                    i = sync.dma_start(
                        out=x3j[:],
                        in_=xrv[:, dg * 8:(dg + 1) * 8, h * TH:(h + 1) * TH])
                    S.dma(i, "x3l")
            block.sync(_x3ld)

            def _vecq(vector, h=h):
                def evac(oc):
                    bank = 0 if oc % 2 == 0 else 1
                    dg = oc // 8
                    idx = h * 2 + dg
                    S.wait(vector, "omm", h * HC + oc + 1)
                    S.wait(vector, "x3l", 16 * (idx + 1))
                    i = vector.tensor_tensor(
                        out=hres[:, oc, :], in0=pb[bank][:],
                        in1=scb[:, h * TH:(h + 1) * TH], op=ALU.mult)
                    c = S.inc(i, "vch")
                    S.wait(vector, "vch", c)
                    i = vector.tensor_tensor(out=hres[:, oc, :],
                                             in0=hres[:, oc, :],
                                             in1=x3j[:, oc % 8, :], op=ALU.add)
                    S.inc(i, "oev")

                for oc in range(HC):
                    evac(oc)
            block.vector(_vecq)

            def _omm(tensor, h=h):
                if h == 0:
                    S.wait(tensor, "aqv", NH)
                    S.wait(tensor, "scev", 2)
                else:
                    S.wait(tensor, "ln_norm", h * HC)
                for oc in range(HC):
                    sbi = oc // OCPSF
                    sl = sbi % 2
                    gsb = h * NSB + sbi
                    bank = 0 if oc % 2 == 0 else 1
                    S.wait(tensor, f"wsl{sl}", 256 * (gsb // 2 + 1))
                    if oc >= 2:
                        S.wait(tensor, "oev", h * HC + oc - 1)
                    for hc in range(HC):
                        i = tensor.matmul(
                            pb[bank][:],
                            terb[:, sl, hc,
                                 (oc % OCPSF) * 128:(oc % OCPSF + 1) * 128],
                            aq[:, hc, h * TH:(h + 1) * TH],
                            start=(hc == 0), stop=(hc == HC - 1))
                    S.inc(i, "omm")
            block.tensor(_omm)


            emit_postln(block, S, "ln", h, hres, hbf, hsq, onec, oner, rows,
                        (pb[2], pb[3]), (pb[4], pb[5]), xo2,
                        x4v[:, :, h * TH:(h + 1) * TH], epsc,
                        ("oev", h * HC, 1, 1))

        def _final(sync):
            S.wait(sync, "ln_st0", 16 * HC * NTH // 2)
            S.wait(sync, "ln_st1", 16 * HC * NTH // 2)
        block.sync(_final)

        return nc


def _rope_tables():
    freqs = 1.0 / (10000.0 ** (np.arange(0, DH, 2, dtype=np.float64) / DH))
    t = np.arange(T, dtype=np.float64)
    ang = np.outer(t, freqs)
    return (np.cos(ang).T.astype(np.float32),
            np.sin(ang).T.astype(np.float32))


def _l1_qkv_in_maps(inputs, x3_full):
    wqT = np.ascontiguousarray(np.asarray(inputs["L1_attn_Wq"], dtype=np.float32).T)
    wkT1 = np.ascontiguousarray(np.asarray(inputs["L1_attn_Wk"], dtype=np.float32).T)
    wvT1 = np.ascontiguousarray(np.asarray(inputs["L1_attn_Wv"], dtype=np.float32).T)
    cosf, sinf = _rope_tables()
    idtf = np.eye(128, dtype=np.float32)
    idtb = np.eye(128, dtype=BFD)
    dvs = np.array([[1.0 / (H * H)], [1.0 / (512 * H)], [1.0 / (512 * H)],
                    [0.0]], dtype=np.float32)
    maps = []
    for c in range(NCORES):
        b, q = c // 4, c % 4
        lo = q * TPC
        maps.append({"x3": fmaj(x3_full[b, lo:lo + TPC, :]), "wqT": wqT,
                     "wkT1": wkT1, "wvT1": wvT1,
                     "cosr": np.ascontiguousarray(cosf[:, lo:lo + TPC]),
                     "sinr": np.ascontiguousarray(sinf[:, lo:lo + TPC]),
                     "idtf": idtf, "idtb": idtb, "dvs": dvs})
    return maps


def _l1_attn_in_maps(d_results):
    qf = {}
    kf = {}
    vf = {}
    for b in range(B):
        qf[b] = np.hstack([d_results[b * 4 + q]["qout"] for q in range(4)])
        kf[b] = np.hstack([d_results[b * 4 + q]["kout"] for q in range(4)])
        vf[b] = np.vstack([np.asarray(d_results[b * 4 + q]["vout"]).reshape(TPC, NKV, DH) for q in range(4)])
    msk = np.zeros((128, 4, 512), dtype=BFD)
    for off in range(4):
        i = np.arange(128)[:, None]
        j = np.arange(512)[None, :]
        msk[:, off, :] = (i <= j - 128 * off).astype(BFD)
    maps = []
    for c in range(NCORES):
        b, kvh = c // 4, c % 4
        maps.append({"qin": np.ascontiguousarray(qf[b][kvh * 512:(kvh + 1) * 512]),
                     "kin": np.ascontiguousarray(kf[b][kvh * 128:(kvh + 1) * 128]),
                     "vin": np.ascontiguousarray(vf[b][:, kvh, :]),
                     "msk": msk})
    return maps


def _l1_oproj_in_maps(inputs, x3_full, e_results):
    woT1 = np.ascontiguousarray(np.asarray(inputs["L1_attn_Wo"], dtype=np.float32).T)
    idtf = np.eye(128, dtype=np.float32)
    idtb = np.eye(128, dtype=BFD)
    af = {}
    sf = {}
    for b in range(B):
        af[b] = np.vstack([e_results[b * 4 + kvh]["aout"] for kvh in range(4)])
        sf[b] = np.vstack([e_results[b * 4 + kvh]["sout"] for kvh in range(4)])
    maps = []
    for c in range(NCORES):
        b, q = c // 4, c % 4
        lo = q * TPC
        maps.append({"ain": np.ascontiguousarray(af[b][:, lo:lo + TPC]),
                     "sin_": np.ascontiguousarray(sf[b][:, lo:lo + TPC]),
                     "woT1": woT1, "x3r": fmaj(x3_full[b, lo:lo + TPC, :]),
                     "idtf": idtf, "idtb": idtb})
    return maps


def run_l1_attn_block(inputs, x3_full):
    ncd = build_l1_qkv()
    rd = run_bass_kernel_spmd(ncd, _l1_qkv_in_maps(inputs, x3_full),
                              core_ids=list(range(NCORES)))
    nce = build_l1_attn()
    re_ = run_bass_kernel_spmd(nce, _l1_attn_in_maps(rd.results),
                               core_ids=list(range(NCORES)))
    ncf = build_l1_oproj()
    rf = run_bass_kernel_spmd(ncf, _l1_oproj_in_maps(inputs, x3_full,
                                                     re_.results),
                              core_ids=list(range(NCORES)))
    return _gather_fm(rf.results, "x4")


def kernel(**inputs):
    x = np.asarray(inputs["L0_attn_x"], dtype=np.float32)
    nca = build_l0_mixer_a()
    ra = run_bass_kernel_spmd(nca, _l0_mixer_a_in_maps(inputs, x),
                              core_ids=list(range(NCORES)))
    ncb = build_l0_mixer_b()
    rb = run_bass_kernel_spmd(ncb, _l0_mixer_b_in_maps(inputs, x, ra.results),
                              core_ids=list(range(NCORES)))
    x2 = _gather_fm(rb.results, "x2")
    ncc = build_l0_ffn()
    rc = run_bass_kernel_spmd(ncc, _l0_ffn_in_maps(inputs, x2),
                              core_ids=list(range(NCORES)))
    x3 = _gather_fm(rc.results, "x3")
    x4 = run_l1_attn_block(inputs, x3)
    ncg = build_l1_ffn()
    rg = run_bass_kernel_spmd(ncg, _l1_ffn_in_maps(inputs, x4),
                              core_ids=list(range(NCORES)))
    return _gather_fm(rg.results, "x5")



def emit_attn_fused(nc=None, G=None, ext=None):
    nc, dram = _phase_setup(nc, ext, "e_")
    qa2a = dram("qa2a", [8 * 256, TPC], BF16, "ExternalInput")
    ka2a = dram("ka2a", [8 * 128, TPC], BF16, "ExternalInput")
    va2a = dram("va2a", [8 * TPC, DH], BF16, "ExternalInput")
    msk4 = dram("msk4", [128, 4, 512], BF16, "ExternalInput")
    snd_a = dram("snd_a", [8 * 256, TPC], BF16, "ExternalOutput")
    snd_s = dram("snd_s", [NH, TPC], F32, "ExternalOutput")

    qv = qa2a.rearrange("(r l p) t -> p r l t", p=128, l=2)
    kv_ = ka2a.rearrange("(r p) t -> p r t", p=128)
    vv = va2a.rearrange("(kb p) d -> p kb d", p=128)
    SCALE = float(1.0 / np.sqrt(DH))
    NQB = 8

    with contextlib.ExitStack() as ctx:
        sb = lambda n, *a: ctx.enter_context(nc.sbuf_tensor(_mangle(n), *a))
        psb = lambda n, *a: ctx.enter_context(nc.psum_tensor(_mangle(n), *a))
        S = PhaseSync(G) if G is not None else PhaseSync(Sync(nc, ctx))

        qt = sb("qt", [128, 2, 512], BF16)
        kt = sb("kt", [128, 8, TPC], BF16)
        vt = sb("vt", [128, 2 * T // 128, DH], BF16)
        mskt = sb("mskt", [128, 4, 512], BF16)
        pt = sb("pt", [128, 4, 512], BF16)
        srow = sb("srow", [1, 2, 512], F32)
        onec = sb("onec", [128, 1], BF16)
        ao = sb("ao", [128, 2, 512], BF16)
        pb = [psb(f"pb{i}", [128, 512], F32) for i in range(6)]
        qkb = (0, 1, 4, 5)

        block = ctx.enter_context(nc.Block())

        def _ld(sync):
            i = sync.dma_start(out=kt[:], in_=kv_)
            S.dma(i, "ld")
            i = sync.dma_start(out=vt[:], in_=vv)
            S.dma(i, "ld")
            i = sync.dma_start(out=mskt[:], in_=msk4)
            S.dma(i, "ld")
        block.sync(_ld)

        def _sst1(sync, m):
            u, qb = m // NQB, m % NQB
            g, l = u // 2, u % 2
            j = g * 4 + qb // 2
            S.wait(sync, "oev", m + 1)
            i = sync.dma_start(
                out=snd_a[j * 256 + l * 128:j * 256 + (l + 1) * 128,
                          (qb % 2) * 512:(qb % 2 + 1) * 512],
                in_=ao[:, m % 2, :])
            S.dma(i, f"ast{m % 2}")
            S.wait(sync, "srw", m + 1)
            i = sync.dma_start(
                out=snd_s[j * 2 + l:j * 2 + l + 1,
                          (qb % 2) * 512:(qb % 2 + 1) * 512],
                in_=srow[:, m % 2, :])
            S.dma(i, f"sst{m % 2}")

        def _sp(sync):
            for n in range(4 * NQB):
                u, qb = n // NQB, n % NQB
                g, l = u // 2, u % 2
                sl = n % 2
                if n >= 2:
                    S.wait(sync, "oev", n - 1)
                i = sync.dma_start(out=qt[:, sl, :],
                                   in_=qv[:, g * 4 + qb // 2, l,
                                          (qb % 2) * 512:(qb % 2 + 1) * 512])
                S.dma(i, f"qld{sl}")
                if n >= 2:
                    _sst1(sync, n - 2)
            _sst1(sync, 4 * NQB - 2)
            _sst1(sync, 4 * NQB - 1)
            S.wait(sync, "ast0", 16 * 16)
            S.wait(sync, "ast1", 16 * 16)
            S.wait(sync, "sst0", 16 * 16)
            S.wait(sync, "sst1", 16 * 16)
        block.sync(_sp)

        def _ones(gpsimd):
            i = gpsimd.memset(onec[:], 1.0)
            S.inc(i, "ones")
        block.gpsimd(_ones)

        nit = 0
        ni = 0
        nb = 0
        nev = 0
        for u in range(4):
            g, l = u // 2, u % 2
            for qb in range(NQB):
                nkb = 4 * (qb + 1)
                n = u * NQB + qb
                qsl = n % 2
                masked = set(range(4 * qb, 4 * qb + 4))
                ni0, nb0 = ni, nb

                def _qk(tensor, g=g, qb=qb, nkb=nkb, n=n, qsl=qsl, nit=nit,
                        ni0=ni0, nb0=nb0, masked=masked, nev=nev):
                    if n == 0:
                        S.wait(tensor, "ld", 48)
                        S.wait(tensor, "ones", 1)
                    S.wait(tensor, f"qld{qsl}", 16 * (n // 2 + 1))
                    if nev > 0:
                        S.wait(tensor, "oev", nev)
                        S.wait(tensor, "srw", nev)
                    xni, xnb = ni0, nb0
                    expcnt = []
                    for j in range(nkb):
                        if j in masked:
                            xnb += 1
                            expcnt.append(("pteb", xnb))
                        else:
                            xni += 1
                            expcnt.append(("ptei", xni))
                    for j in range(nkb + 1):
                        if j < nkb:
                            it = nit + j
                            if j >= 4:
                                tg, cv = expcnt[j - 4]
                                S.wait(tensor, tg, cv)
                            i = tensor.matmul(pb[qkb[j % 4]][:],
                                              kt[:, g * 4 + j // 8,
                                                 (j % 8) * 128:(j % 8 + 1) * 128],
                                              qt[:, qsl, :],
                                              start=True, stop=True)
                            S.inc(i, "qkps")
                        if j >= 1:
                            jj = j - 1
                            itj = nit + jj
                            slj = jj % 4
                            tg, cv = expcnt[jj]
                            S.wait(tensor, tg, cv)
                            i = tensor.matmul(pb[2][:], vt[:, g * 32 + jj, :],
                                              pt[:, slj, :],
                                              start=(jj == 0), stop=(jj == nkb - 1))
                            i = tensor.matmul(pb[3][0:1, :], onec[:],
                                              pt[:, slj, :],
                                              start=(jj == 0), stop=(jj == nkb - 1))
                            S.inc(i, "pvd")
                block.tensor(_qk)

                def _pt(scalar, nkb=nkb, nit=nit, masked=masked):
                    for j in range(nkb):
                        it = nit + j
                        sl = j % 4
                        S.wait(scalar, "qkps", it + 1)
                        if j >= 4:
                            S.wait(scalar, "pvd", it - 3)
                        i = scalar.activation(out=pt[:, sl, :],
                                              in_=pb[qkb[sl]][:],
                                              func=AF.Exp, scale=SCALE)
                        if j in masked:
                            S.inc(i, "ptee")
                        else:
                            S.inc(i, "ptei")
                block.scalar(_pt)

                def _msk(vector, qb=qb, nb0=nb0, masked=masked):
                    lnb = nb0
                    for j in sorted(masked):
                        sl = j % 4
                        r_ = j - 4 * qb
                        lnb += 1
                        S.wait(vector, "ptee", lnb)
                        i = vector.tensor_tensor(
                            out=pt[:, sl, :], in0=pt[:, sl, :],
                            in1=mskt[:, r_, :], op=ALU.mult)
                        S.inc(i, "pteb")
                block.vector(_msk)

                def _oev(vector, n=n, nkb=nkb, nit=nit):
                    S.wait(vector, "pvd", nit + nkb)
                    if n >= 2:
                        S.wait(vector, f"ast{n % 2}", 16 * (n // 2))
                    i = vector.tensor_copy(out=ao[:, n % 2, :], in_=pb[2][:])
                    S.inc(i, "oev")
                    if n >= 2:
                        S.wait(vector, f"sst{n % 2}", 16 * (n // 2))
                    i = vector.tensor_copy(out=srow[:, n % 2, :],
                                           in_=pb[3][0:1, :])
                    S.inc(i, "srw")
                block.vector(_oev)

                nit += nkb
                ni += nkb - 4
                nb += 4
                nev += 1

    return nc


def _attn_msk4():
    kt = np.arange(128)[:, None]
    qt = np.arange(512)[None, :]
    m = np.zeros((128, 4, 512), dtype=BFD)
    for r in range(4):
        m[:, r, :] = (r * 128 + kt <= qt).astype(BFD)
    return m


def _attn_fused_consts(q):
    biasc = np.zeros((128, 64), dtype=np.float32)
    msku = np.zeros((128, 2, 16, 512), dtype=BFD)
    i = np.arange(128)[:, None]
    j = np.arange(512)[None, :]
    for qb in range(2):
        gqb = q * 2 + qb
        for kb in range(32):
            if kb > gqb * 4 + 3:
                biasc[:, qb * 32 + kb] = -1e30
        for u in range(4):
            for r in range(4):
                kb = u * 8 + qb * 4 + r
                if u == q:
                    msku[:, qb, u * 4 + r, :] = (i <= j - 128 * r).astype(BFD)
                else:
                    msku[:, qb, u * 4 + r, :] = np.ones((128, 512), dtype=BFD)
    return biasc, msku



GROUPS = [[0, 1, 2, 3], [4, 5, 6, 7]]


def _emit_cc(nc, G, ins_outs, kinds, tag, groups=None):
    from concourse.aot_env import allow_ici_all_to_all
    if groups is None:
        groups = GROUPS
    with nc.Block() as blk:
        def _cc(g):
            with allow_ici_all_to_all():
                for (i_, o_), kind in zip(ins_outs, kinds):
                    inst = g.collective_compute(kind, ALU.bypass,
                                                replica_groups=groups,
                                                ins=[i_.opt()], outs=[o_.opt()])
                    G.inc(inst, tag, 1)
            G.wait(g, tag, None)
        blk.gpsimd(_cc)


def build_fused(debug=False):
    nc = bass.Bass("TRN2", target_bir_lowering=False)
    gctx = contextlib.ExitStack()
    with gctx:
        G = Sync(nc, gctx)
        ext = {}
        dbg = {"x2", "x3", "x4", "qout", "ain", "sin_"} if debug else set()

        def ein(name, shape, dt):
            ext[name] = nc.dram_tensor(name, shape, dt, kind="ExternalInput").ap()

        def itn(name, shape, dt):
            kind = "ExternalOutput" if name in dbg else "Internal"
            ext[name] = nc.dram_tensor(name, shape, dt, kind=kind).ap()

        ein("xin", [H, TPC + 1], F32)
        for n in ("wkT", "wvT", "wrT", "woT"):
            ein(n, [H, H], F32)
        for n in ("bkc", "bvc", "brc", "tmk", "tmv", "tmr", "tdec", "tfir",
                  "boc", "dsc"):
            ein(n, [128, HC], F32)
        ein("vg", [128, 8], F32)
        ein("iota", [1, TPC], F32)
        ein("gw", [H, I0], F32)
        ein("uw", [H, I0], F32)
        ein("dw", [I0, H], F32)
        ein("gsc", [128, I0 // 128], F32)
        ein("usc", [128, I0 // 128], F32)
        ein("cosr", [64, TPC], F32)
        ein("sinr", [64, TPC], F32)
        ein("idtf", [128, 128], F32)
        ein("idtb", [128, 128], BF16)
        ein("dvs", [4, 1], F32)
        ein("msk4", [128, 4, 512], BF16)

        tmn = {"q": "twq", "k": "twk", "v": "twv", "o": "two",
               "1": "tw1", "2": "tw2", "3": "tw3"}
        for nm, rows, cols, rs in TMATS:
            ext[tmn[nm]] = nc.dram_tensor(tmn[nm], [rows, cols], FP8,
                                          kind="Internal").ap()
            ext[f"sh{nm}"] = nc.dram_tensor(f"sh{nm}", [rs, cols], F32,
                                            kind="ExternalInput").ap()
        ext["swv"] = nc.dram_tensor("swv", [1, 8], F32, kind="Internal").ap()
        itn("gprt", [128, 8], F32)
        itn("gprd", [128, 8], F32)

        for n in ("nout", "dout", "srout", "qout"):
            itn(n, [H, TPC], BF16)
        itn("totout", [128, 48], F32)
        itn("totga", [512, 48], F32)
        for n in ("x2", "x3", "x4"):
            itn(n, [H, TPC], F32)
        itn("kout", [1024, TPC], BF16)
        ext["vout"] = nc.dram_tensor("vout", [2 * NKV, TPC, DH], BF16,
                                     kind="Internal").ap()
        itn("qa2a", [8 * 256, TPC], BF16)
        itn("ka2a", [8 * 128, TPC], BF16)
        itn("va2a", [8 * TPC, DH], BF16)
        itn("snd_a", [8 * 256, TPC], BF16)
        itn("snd_s", [NH, TPC], F32)
        itn("aina", [8 * 256, TPC], BF16)
        itn("sina", [NH, TPC], F32)
        ext["x5"] = nc.dram_tensor("x5", [H, TPC], F32,
                                   kind="ExternalOutput").ap()

        ext["xres"] = ext["xin"][:, 1:TPC + 1]
        ext["ownt"] = ext["totout"]
        ext["x3r"] = ext["x3"]
        ext["nin"] = ext["nout"]
        ext["din"] = ext["dout"]
        ext["srin"] = ext["srout"]
        ext["ain"] = ext["aina"]
        ext["sin_"] = ext["sina"]

        build_l0_mixer_a(nc, G, ext)
        _emit_cc(nc, G, [(ext["totout"], ext["totga"])], ["AllGather"], "ccA")
        emit_tern_phase(nc, G, ext)
        build_l0_mixer_b(nc, G, ext)
        build_l0_ffn(nc, G, ext)
        build_l1_qkv(nc, G, ext)
        _emit_cc(nc, G,
                 [(ext["kout"], ext["ka2a"]),
                  (ext["vout"].rearrange("k t d -> (k t) d"), ext["va2a"]),
                  (ext["qout"], ext["qa2a"])],
                 ["AllToAll", "AllToAll", "AllToAll"], "ccD",
                 groups=[list(range(8))])
        emit_attn_fused(nc, G, ext)
        _emit_cc(nc, G,
                 [(ext["snd_a"], ext["aina"]),
                  (ext["snd_s"], ext["sina"])],
                 ["AllToAll", "AllToAll"], "ccE",
                 groups=[list(range(8))])
        build_l1_oproj(nc, G, ext)
        build_l1_ffn(nc, G, ext)
    return nc


def _fused_in_maps(inputs):
    x = np.asarray(inputs["L0_attn_x"], dtype=np.float32)
    ma = _l0_mixer_a_in_maps(inputs, x)
    dummy = np.zeros((B, T, H), dtype=np.float32)
    woT = np.ascontiguousarray(np.asarray(inputs["L0_attn_Wo"], dtype=np.float32).T)
    boc = cols128(inputs["L0_attn_bo"])
    iota = np.arange(TPC, dtype=np.float32).reshape(1, TPC)
    gw = np.ascontiguousarray(np.asarray(inputs["L0_ffn_gate_w"], dtype=np.float32))
    uw = np.ascontiguousarray(np.asarray(inputs["L0_ffn_up_w"], dtype=np.float32))
    dw = np.ascontiguousarray(np.asarray(inputs["L0_ffn_down_w"], dtype=np.float32))
    gsc = cols128(inputs["L0_ffn_gate_scale"])
    usc = cols128(inputs["L0_ffn_up_scale"])
    dsc = cols128(inputs["L0_ffn_down_scale"])
    cosf, sinf = _rope_tables()
    idtf = np.eye(128, dtype=np.float32)
    idtb = np.eye(128, dtype=BFD)
    dvs = np.array([[1.0 / (H * H)], [1.0 / (512 * H)], [1.0 / (512 * H)],
                    [0.0]], dtype=np.float32)
    maps = []
    for c in range(NCORES):
        b, q = c // 4, c % 4
        lo = q * TPC
        m = dict(ma[c])
        m.update({"vg": _mixer_vg(q), "woT": woT, "boc": boc, "iota": iota,
                  "gw": gw, "uw": uw, "dw": dw, "gsc": gsc, "usc": usc,
                  "dsc": dsc,
                  "cosr": np.ascontiguousarray(cosf[:, lo:lo + TPC]),
                  "sinr": np.ascontiguousarray(sinf[:, lo:lo + TPC]),
                  "idtf": idtf, "idtb": idtb, "dvs": dvs,
                  "msk4": _attn_msk4(),
                  "tdiv": _TDIV, "teps": _TEPS})
        m.update(_tern_shards(inputs, c))
        maps.append(m)
    return maps


def kernel(**inputs):
    nc = build_fused()
    res = run_bass_kernel_spmd(nc, _fused_in_maps(inputs),
                               core_ids=list(range(NCORES)))
    return _gather_fm(res.results, "x5")


def kernel_unfused(**inputs):
    x = np.asarray(inputs["L0_attn_x"], dtype=np.float32)
    nca = build_l0_mixer_a()
    ra = run_bass_kernel_spmd(nca, _l0_mixer_a_in_maps(inputs, x),
                              core_ids=list(range(NCORES)))
    ncb = build_l0_mixer_b()
    rb = run_bass_kernel_spmd(ncb, _l0_mixer_b_in_maps(inputs, x, ra.results),
                              core_ids=list(range(NCORES)))
    x2 = _gather_fm(rb.results, "x2")
    ncc = build_l0_ffn()
    rc = run_bass_kernel_spmd(ncc, _l0_ffn_in_maps(inputs, x2),
                              core_ids=list(range(NCORES)))
    x3 = _gather_fm(rc.results, "x3")
    x4 = run_l1_attn_block(inputs, x3)
    ncg = build_l1_ffn()
    rg = run_bass_kernel_spmd(ncg, _l1_ffn_in_maps(inputs, x4),
                              core_ids=list(range(NCORES)))
    return _gather_fm(rg.results, "x5")



TMATS = (("q", H, H, 256), ("k", H, 512, 256), ("v", H, 512, 256),
         ("o", H, H, 256), ("1", H, I1, 256), ("2", H, I1, 256),
         ("3", I1, H, 704))


def _tern_chunks():
    ch = []
    for mi, (nm, rows, cols, rs) in enumerate(TMATS):
        off = 0
        while off < rs:
            nr = min(128, rs - off)
            ch.append((mi, nm, off, nr, cols))
            off += nr
    return ch


def emit_tern_phase(nc=None, G=None, ext=None):
    nc, dram = _phase_setup(nc, ext, "t_")
    sh = {nm: dram(f"sh{nm}", [rs, cols], F32, "ExternalInput")
          for nm, rows, cols, rs in TMATS}
    tdiv = dram("tdiv", [8, 1], F32, "ExternalInput")
    teps = dram("teps", [8, 1], F32, "ExternalInput")
    ctw = {nm: dram(f"c_tw{nm}", [rs, cols], FP8, "Internal")
           for nm, rows, cols, rs in TMATS}
    tw = {nm: dram(f"tw{nm}", [rows, cols], FP8, "Internal")
          for nm, rows, cols, rs in TMATS}
    gprt = dram("gprt", [128, 8], F32, "Internal")
    gprd = dram("gprd", [128, 8], F32, "Internal")
    swv = dram("swv", [1, 8], F32, "ExternalOutput" if ext is None else "Internal")
    bsc = dram("bsc", [128, 16], F32, "Internal")

    CH = _tern_chunks()

    with contextlib.ExitStack() as ctx:
        sb = lambda n, *a: ctx.enter_context(nc.sbuf_tensor(_mangle(n), *a))
        psb = lambda n, *a: ctx.enter_context(nc.psum_tensor(_mangle(n), *a))
        S = PhaseSync(G) if G is not None else PhaseSync(Sync(nc, ctx))
        S.g.pin("cct")

        stg = sb("stg", [128, 2, I1], BF16)
        wstg = sb("wstg", [128, 2, I1], F32)
        tero = sb("tero", [128, 2, I1], BF16)
        gacc = sb("gacc", [128, 8], F32)
        mcol = sb("mcol", [128, 1], F32)
        gaccT = sb("gaccT", [8, 128], F32)
        gsum = sb("gsum", [8, 3], F32)
        tdc = sb("tdc", [8, 2], F32)
        throw = sb("throw", [1, 16], F32)
        thrc = sb("thrc", [128, 16], F32)
        oner = sb("oner", [1, 128], F32)
        pbt = psb("pbt", [128, 512], F32)

        block = ctx.enter_context(nc.Block(no_gpsimd_drain=True))

        def _init(gpsimd):
            gpsimd.memset(gacc[:], 0.0)
            i = gpsimd.memset(oner[:], 1.0)
            S.inc(i, "ones")
        block.gpsimd(_init)

        def _cl(sync):
            i = sync.dma_start(out=tdc[:, 0:1], in_=tdiv)
            S.dma(i, "cl")
            i = sync.dma_start(out=tdc[:, 1:2], in_=teps)
            S.dma(i, "cl")
        block.sync(_cl)

        if G is None:
            def _gdma(gpsimd):
                for n, (mi, nm, off, nr, cols) in enumerate(CH):
                    sl = n % 2
                    if n >= 2:
                        S.wait(gpsimd, "gred", n - 1)
                    i = gpsimd.dma_start(out=stg[0:nr, sl, 0:cols],
                                         in_=sh[nm][off:off + nr, :])
                    S.dma(i, f"gld{sl}")
            block.gpsimd(_gdma)

            def _gred(vector):
                for n, (mi, nm, off, nr, cols) in enumerate(CH):
                    sl = n % 2
                    S.wait(vector, f"gld{sl}", 16 * (n // 2 + 1))
                    S.wait(vector, "gred", n)
                    i = vector.reduce_sum(out=mcol[0:nr, :],
                                          in_=stg[0:nr, sl, 0:cols],
                                          axis=AX, apply_absolute_value=True)
                    c = S.inc(i, "vch")
                    S.wait(vector, "vch", c)
                    i = vector.tensor_tensor(out=gacc[0:nr, mi:mi + 1],
                                             in0=gacc[0:nr, mi:mi + 1],
                                             in1=mcol[0:nr, :], op=ALU.add)
                    S.inc(i, "gred")
            block.vector(_gred)

            def _ar(gpsimd):
                S.wait(gpsimd, "gred", len(CH))
                i = gpsimd.dma_start(out=gprt, in_=gacc[:])
                S.dma(i, "gx")
                S.wait(gpsimd, "gx", 16)
                i = gpsimd.collective_compute("AllReduce", ALU.add,
                                              replica_groups=[list(range(8))],
                                              ins=[gprt.opt()], outs=[gprd.opt()])
                S.inc(i, "ccg")
                S.wait(gpsimd, "ccg", 1)
                i = gpsimd.dma_start(out=gaccT[:],
                                     in_=gprd.rearrange("p m -> m p"),
                                     allow_slow_non_contiguous=True)
                S.dma(i, "gx")
            block.gpsimd(_ar)
            gxb = 32
        else:
            def _ar(gpsimd):
                S.g.pin("gar")
                S.g.wait(gpsimd, "gar", 1)
                i = gpsimd.dma_start(out=gaccT[:],
                                     in_=gprd.rearrange("p m -> m p"),
                                     allow_slow_non_contiguous=True)
                S.dma(i, "gx")
            block.gpsimd(_ar)
            gxb = 16

        def _thr(vector):
            S.wait(vector, "gx", gxb)
            S.wait(vector, "cl", 32)
            i = vector.reduce_sum(out=gsum[:, 0:1], in_=gaccT[:], axis=AX)
            c = S.inc(i, "vch")
            S.wait(vector, "vch", c)
            i = vector.tensor_scalar(out=gsum[:, 1:2], in0=gsum[:, 0:1],
                                     scalar1=tdc[:, 0:1], scalar2=None,
                                     op0=ALU.mult)
            c = S.inc(i, "vch")
            S.wait(vector, "vch", c)
            i = vector.tensor_tensor(out=gsum[:, 1:2], in0=gsum[:, 1:2],
                                     in1=tdc[:, 1:2], op=ALU.add)
            c = S.inc(i, "vch")
            S.wait(vector, "vch", c)
            i = vector.tensor_scalar(out=gsum[:, 2:3], in0=gsum[:, 1:2],
                                     scalar1=0.5, scalar2=None, op0=ALU.mult)
            S.inc(i, "thr")
        block.vector(_thr)

        def _bounce(gpsimd):
            S.wait(gpsimd, "thr", 1)
            i = gpsimd.dma_start(out=bsc[0:8, 0:3], in_=gsum[:])
            S.dma(i, "gx")
            S.wait(gpsimd, "gx", gxb + 16)
            bv = bsc[0:8, 0:3].rearrange("m c -> c m")
            i = gpsimd.dma_start(out=throw[:, 0:8], in_=bv[1:2, :],
                                 allow_slow_non_contiguous=True)
            S.dma(i, "gx")
            i = gpsimd.dma_start(out=throw[:, 8:16], in_=bv[2:3, :],
                                 allow_slow_non_contiguous=True)
            S.dma(i, "gx")
            S.wait(gpsimd, "gx", gxb + 48)
            i = gpsimd.dma_start(out=swv, in_=throw[:, 0:8])
            S.dma(i, "gx")
        block.gpsimd(_bounce)

        def _bc(tensor):
            S.wait(tensor, "gx", gxb + 64)
            S.wait(tensor, "ones", 1)
            i = tensor.matmul(pbt[:, 0:8], oner[:], throw[:, 8:16],
                              start=True, stop=True)
            S.inc(i, "bc")
        block.tensor(_bc)

        def _cols(vector):
            S.wait(vector, "bc", 1)
            i = vector.tensor_copy(out=thrc[:, 0:8], in_=pbt[:, 0:8])
            c = S.inc(i, "vch")
            S.wait(vector, "vch", c)
            i = vector.tensor_scalar(out=thrc[:, 8:16], in0=thrc[:, 0:8],
                                     scalar1=-1.0, scalar2=None, op0=ALU.mult)
            S.inc(i, "cols")
        block.vector(_cols)

        def _store(gpsimd, n):
            mi, nm, off, nr, cols = CH[n]
            sl = n % 2
            i = gpsimd.dma_start(out=ctw[nm][off:off + nr, :],
                                 in_=tero[0:nr, sl, 0:cols])
            S.dma(i, f"tst{sl}")

        def _tdma(gpsimd):
            for n, (mi, nm, off, nr, cols) in enumerate(CH):
                sl = n % 2
                if n >= 2:
                    S.wait(gpsimd, "tern", n - 1)
                i = gpsimd.dma_start(out=wstg[0:nr, sl, 0:cols],
                                     in_=sh[nm][off:off + nr, :])
                S.dma(i, f"tld{sl}")
                if n >= 2:
                    _store(gpsimd, n - 2)
            S.wait(gpsimd, "tern", len(CH))
            _store(gpsimd, len(CH) - 2)
            _store(gpsimd, len(CH) - 1)
        block.gpsimd(_tdma)

        def _tv(vector):
            S.wait(vector, "cols", 1)
            for n, (mi, nm, off, nr, cols) in enumerate(CH):
                sl = n % 2
                S.wait(vector, f"tld{sl}", 16 * (n // 2 + 1))
                if n >= 2:
                    S.wait(vector, f"tst{sl}", 16 * (n // 2))
                i = vector.tensor_scalar(
                    out=tero[0:nr, sl, 0:cols], in0=wstg[0:nr, sl, 0:cols],
                    scalar1=thrc[0:nr, mi:mi + 1], scalar2=None, op0=ALU.is_ge)
                c = S.inc(i, "vch")
                S.wait(vector, "vch", c)
                i = vector.tensor_scalar(
                    out=stg[0:nr, sl, 0:cols], in0=wstg[0:nr, sl, 0:cols],
                    scalar1=thrc[0:nr, 8 + mi:9 + mi], scalar2=None,
                    op0=ALU.is_le)
                c = S.inc(i, "vch")
                S.wait(vector, "vch", c)
                i = vector.tensor_tensor(out=tero[0:nr, sl, 0:cols],
                                         in0=tero[0:nr, sl, 0:cols],
                                         in1=stg[0:nr, sl, 0:cols],
                                         op=ALU.subtract)
                S.inc(i, "tern")
        block.vector(_tv)

        def _cc(gpsimd):
            nall = len(CH)
            S.wait(gpsimd, "tst0", 16 * ((nall + 1) // 2))
            S.wait(gpsimd, "tst1", 16 * (nall // 2))
            for nm, rows, cols, rs in TMATS:
                i = gpsimd.collective_compute("AllGather", ALU.bypass,
                                              replica_groups=[list(range(8))],
                                              ins=[ctw[nm].opt()],
                                              outs=[tw[nm].opt()])
                S.g.inc(i, "cct")
        block.gpsimd(_cc)

    return nc


def _tern_shards(inputs, core):
    mats = {"q": np.asarray(inputs["L1_attn_Wq"], dtype=np.float32).T,
            "k": np.asarray(inputs["L1_attn_Wk"], dtype=np.float32).T,
            "v": np.asarray(inputs["L1_attn_Wv"], dtype=np.float32).T,
            "o": np.asarray(inputs["L1_attn_Wo"], dtype=np.float32).T,
            "1": np.asarray(inputs["L1_ffn_w1"], dtype=np.float32).T,
            "2": np.asarray(inputs["L1_ffn_w2"], dtype=np.float32).T,
            "3": np.asarray(inputs["L1_ffn_w3"], dtype=np.float32).T}
    out = {}
    for nm, rows, cols, rs in TMATS:
        out[f"sh{nm}"] = np.ascontiguousarray(mats[nm][core * rs:(core + 1) * rs])
    return out


_TDIV = np.array([[1.0 / (H * H)], [1.0 / (512 * H)], [1.0 / (512 * H)],
                  [1.0 / (H * H)], [1.0 / (I1 * H)], [1.0 / (I1 * H)],
                  [1.0 / (I1 * H)], [0.0]], dtype=np.float32)
_TEPS = np.array([[1e-5], [1e-5], [1e-5], [1e-5], [1e-8], [1e-8], [1e-8],
                  [0.0]], dtype=np.float32)


# version 17
# speedup vs baseline: 1.0091x; 1.0091x over previous
import contextlib
import numpy as np
import ml_dtypes
import concourse.bass as bass
import concourse.mybir as mybir
from concourse.bass_utils import run_bass_kernel_spmd

F32 = mybir.dt.float32
BF16 = mybir.dt.bfloat16
FP8 = mybir.dt.float8e4
AF = mybir.ActivationFunctionType
ALU = mybir.AluOpType
AX = mybir.AxisListType.X

NCORES = 8
H = 2048
HC = H // 128
T = 4096
B = 2
TPC = 1024
I0 = 6144
I1 = 5632
NH = 16
NKV = 4
DH = 128
LN_EPS = 1e-5
BFD = ml_dtypes.bfloat16


class Sync:

    def __init__(self, nc, ctx):
        self.nc = nc
        self.ctx = ctx
        self.sems = {}
        self.counts = {}
        self.alias = {}
        self.pinned = set()
        self.nphys = 0

    def pin(self, tag):
        self.pinned.add(tag)

    def new_phase(self):
        self.alias = {}
        self._next = 0

    def phys(self, tag):
        if tag in self.pinned:
            return "pin_" + tag
        import os
        if os.environ.get("SYNC_NOPOOL"):
            return tag
        if tag not in self.alias:
            if not hasattr(self, "_next"):
                self._next = 0
            p = f"ps{self._next}"
            self._next += 1
            self.alias[tag] = p
        return self.alias[tag]

    def _sem(self, tag):
        p = self.phys(tag)
        if p not in self.sems:
            self.sems[p] = self.ctx.enter_context(self.nc.semaphore(p))
        if p not in self.counts:
            self.counts[p] = 0
        return self.sems[p]

    def inc(self, inst, tag, amount=1):
        s = self._sem(tag)
        p = self.phys(tag)
        self.counts[p] += amount
        inst.then_inc(s, amount)
        return self.counts[p]

    def dma(self, inst, tag):
        return self.inc(inst, tag, 16)

    def wait(self, eng, tag, count=None):
        s = self._sem(tag)
        p = self.phys(tag)
        c = self.counts[p] if count is None else count
        if c > 0:
            eng.wait_ge(s, c)


class PhaseSync:

    class _Counts:
        def __init__(self, ps):
            self.ps = ps

        def __getitem__(self, tag):
            g = self.ps.g
            p = g.phys(tag)
            return g.counts.get(p, 0) - self.ps.base_of(tag)

    def __init__(self, g):
        self.g = g
        g.new_phase()
        self._base = dict(g.counts)
        self.counts = PhaseSync._Counts(self)

    def base_of(self, tag):
        return self._base.get(self.g.phys(tag), 0)

    def inc(self, inst, tag, amount=1):
        return self.g.inc(inst, tag, amount) - self.base_of(tag)

    def dma(self, inst, tag):
        return self.inc(inst, tag, 16)

    def wait(self, eng, tag, count=None):
        if count is None:
            self.g.wait(eng, tag, None)
        else:
            self.g.wait(eng, tag, count + self.base_of(tag))


_CURPFX = [""]


def _mangle(n):
    return _CURPFX[0] + n


def _phase_setup(nc, ext, prefix=""):
    _CURPFX[0] = prefix
    if nc is None:
        nc = bass.Bass("TRN2", target_bir_lowering=False)

    def dram(name, shape, dt, kind):
        if ext is not None and name in ext:
            return ext[name]
        nm = (prefix + name) if kind == "Internal" else name
        return nc.dram_tensor(nm, shape, dt, kind=kind).ap()

    return nc, dram


def fmaj(x):
    return np.ascontiguousarray(np.asarray(x, dtype=np.float32).T)


def cols128(v):
    v = np.asarray(v, dtype=np.float32).reshape(-1, 128)
    return np.ascontiguousarray(v.T)


def emit_postln(block, S, pre, idx, hres, hbf, hsq, onec, oner, rows, pb_sum,
                pb_bc, xo2, store_dst, epsc, hres_wait):
    TH = hres.shape[-1]
    perhc = len(hres_wait) == 4

    def _scalar1(scalar):
        if not perhc:
            S.wait(scalar, hres_wait[0], hres_wait[1])
            scalar.copy(out=hbf[:], in_=hres[:])
            i = scalar.activation(out=hsq[:], in_=hres[:], func=AF.Square)
            S.inc(i, f"{pre}_stat", HC)
        else:
            tag, base, step, grp = hres_wait
            for hc in range(HC):
                S.wait(scalar, tag, base + step * (hc // grp + 1))
                scalar.copy(out=hbf[:, hc, :], in_=hres[:, hc, :])
                i = scalar.activation(out=hsq[:, hc, :], in_=hres[:, hc, :],
                                      func=AF.Square)
                S.inc(i, f"{pre}_stat")
    block.scalar(_scalar1)

    def _tensor1(tensor):
        S.wait(tensor, "ones", 1)
        for hc in range(HC):
            S.wait(tensor, f"{pre}_stat", idx * HC + hc + 1)
            tensor.matmul(pb_sum[0][0:1, :], onec[:], hbf[:, hc, :],
                          start=(hc == 0), stop=(hc == HC - 1))
            i = tensor.matmul(pb_sum[1][0:1, :], onec[:], hsq[:, hc, :],
                              start=(hc == 0), stop=(hc == HC - 1))
        S.inc(i, f"{pre}_sums")
    block.tensor(_tensor1)

    def _vector1(vector):
        S.wait(vector, f"{pre}_sums", idx + 1)
        i = vector.tensor_scalar(out=rows[:, 0, :], in0=pb_sum[0][0:1, :],
                                 scalar1=1.0 / H, scalar2=None, op0=ALU.mult)
        c = S.inc(i, "vch")
        vector.tensor_scalar(out=rows[:, 1, :], in0=pb_sum[1][0:1, :],
                             scalar1=1.0 / H, scalar2=None, op0=ALU.mult)
        S.wait(vector, "vch", c)
        i = vector.tensor_tensor(out=rows[:, 2, :], in0=rows[:, 0, :],
                                 in1=rows[:, 0, :], op=ALU.mult)
        c = S.inc(i, "vch")
        S.wait(vector, "vch", c)
        i = vector.tensor_tensor(out=rows[:, 1, :], in0=rows[:, 1, :],
                                 in1=rows[:, 2, :], op=ALU.subtract)
        S.inc(i, f"{pre}_var")
    block.vector(_vector1)

    def _scalar2(scalar):
        S.wait(scalar, f"{pre}_var", idx + 1)
        i = scalar.activation(out=rows[:, 3, :], in_=rows[:, 1, :],
                              func=AF.Sqrt, bias=epsc[:])
        S.inc(i, f"{pre}_sd")
    block.scalar(_scalar2)

    def _vector2(vector):
        S.wait(vector, f"{pre}_sd", idx + 1)
        i = vector.reciprocal(out=rows[:, 4, :], in_=rows[:, 3, :])
        S.inc(i, f"{pre}_rinv")
    block.vector(_vector2)

    def _tensor2(tensor):
        S.wait(tensor, f"{pre}_rinv", idx + 1)
        tensor.matmul(pb_bc[0][:], oner[:], rows[:, 0, :], start=True, stop=True)
        i = tensor.matmul(pb_bc[1][:], oner[:], rows[:, 4, :], start=True, stop=True)
        S.inc(i, f"{pre}_bc")
    block.tensor(_tensor2)

    def _vector3(vector):
        S.wait(vector, f"{pre}_bc", idx + 1)
        for hc in range(HC):
            sl = hc % 2
            g = idx * HC + hc
            if g >= 2:
                S.wait(vector, f"{pre}_st{sl}", 16 * (g // 2))
            i = vector.tensor_tensor(out=xo2[:, sl, :], in0=hres[:, hc, :],
                                     in1=pb_bc[0][:], op=ALU.subtract)
            c = S.inc(i, "vch")
            S.wait(vector, "vch", c)
            i = vector.tensor_tensor(out=xo2[:, sl, :], in0=xo2[:, sl, :],
                                     in1=pb_bc[1][:], op=ALU.mult)
            S.inc(i, f"{pre}_norm")
    block.vector(_vector3)

    def _sync1(sync):
        for hc in range(HC):
            sl = hc % 2
            S.wait(sync, f"{pre}_norm", idx * HC + hc + 1)
            i = sync.dma_start(out=store_dst[:, hc, :], in_=xo2[:, sl, :])
            S.dma(i, f"{pre}_st{sl}")
    block.sync(_sync1)



def build_l0_ffn(nc=None, G=None, ext=None):
    nc, dram = _phase_setup(nc, ext, "c_")
    x2 = dram("x2", [H, TPC], F32, "ExternalInput")
    gw = dram("gw", [H, I0], F32, "ExternalInput")
    uw = dram("uw", [H, I0], F32, "ExternalInput")
    dw = dram("dw", [I0, H], F32, "ExternalInput")
    gsc = dram("gsc", [128, I0 // 128], F32, "ExternalInput")
    usc = dram("usc", [128, I0 // 128], F32, "ExternalInput")
    dsc = dram("dsc", [128, HC], F32, "ExternalInput")
    x3 = dram("x3", [H, TPC], F32, "ExternalOutput")

    x2v = x2.rearrange("(c p) t -> p c t", p=128)
    x3v = x3.rearrange("(c p) t -> p c t", p=128)

    NOB = I0 // 128
    OSB = 8
    OPB = NOB // OSB
    OSW = OPB * 128
    NTH = 2
    TH = TPC // NTH

    with contextlib.ExitStack() as ctx:
        sb = lambda n, *a: ctx.enter_context(nc.sbuf_tensor(_mangle(n), *a))
        psb = lambda n, *a: ctx.enter_context(nc.psum_tensor(_mangle(n), *a))
        S = PhaseSync(G) if G is not None else PhaseSync(Sync(nc, ctx))

        x2bf = sb("x2bf", [128, HC, TH], BF16)
        g8 = sb("g8", [128, 2, HC, OSW], FP8)
        u8 = sb("u8", [128, 2, HC, OSW], FP8)
        d8 = sb("d8", [128, 8, 1024], FP8)
        fuse = sb("fuse", [128, NOB, TH], BF16)
        gsil = sb("gsil", [128, 2, TH], BF16)
        hres = sb("hres", [128, HC, TH], F32)
        x2f = sb("x2f", [128, 8, TH], F32)
        xo2 = sb("xo2", [128, 2, TH], F32)
        gscT = sb("gscT", [128, NOB], F32)
        uscT = sb("uscT", [128, NOB], F32)
        dscT = sb("dscT", [128, HC], F32)
        onec = sb("onec", [128, 1], BF16)
        oner = sb("oner", [1, 128], F32)
        epsc = sb("epsc", [1, 1], F32)
        rows = sb("rows", [1, 5, TH], F32)
        pb = [psb(f"pb{i}", [128, 512], F32) for i in range(8)]
        hbf = fuse[:, 0:HC, :]
        hsq = fuse[:, HC:2 * HC, :]

        block = ctx.enter_context(nc.Block())

        def _consts(sync):
            for ap_, src in ((gscT, gsc), (uscT, usc), (dscT, dsc)):
                i = sync.dma_start(out=ap_[:], in_=src)
                S.dma(i, "consts")
        block.sync(_consts)

        def _ones(gpsimd):
            gpsimd.memset(onec[:], 1.0)
            gpsimd.memset(epsc[:], LN_EPS)
            i = gpsimd.memset(oner[:], 1.0)
            S.inc(i, "ones")
        block.gpsimd(_ones)

        for h in range(NTH):
            def _wdma(gpsimd, h=h):
                if h > 0:
                    S.wait(gpsimd, "gups", h * NOB)
                i = gpsimd.dma_start(out=x2bf[:],
                                     in_=x2v[:, :, h * TH:(h + 1) * TH])
                S.dma(i, "x2bf")
                for o in range(OSB):
                    sl = o % 2
                    g = h * OSB + o
                    if g >= 2:
                        S.wait(gpsimd, "gups", (g - 1) * OPB)
                    for hc in range(HC):
                        i = gpsimd.dma_start(
                            out=g8[:, sl, hc, :],
                            in_=gw[hc * 128:(hc + 1) * 128, o * OSW:(o + 1) * OSW])
                        S.dma(i, f"g8s{sl}")
                    for hc in range(HC):
                        i = gpsimd.dma_start(
                            out=u8[:, sl, hc, :],
                            in_=uw[hc * 128:(hc + 1) * 128, o * OSW:(o + 1) * OSW])
                        S.dma(i, f"u8s{sl}")
                for dg in range(2):
                    for oc in range(NOB):
                        sl = oc % 8
                        k = (h * 2 + dg) * NOB + oc
                        if k >= 8:
                            S.wait(gpsimd, "d_mm", k - 7)
                        i = gpsimd.dma_start(
                            out=d8[:, sl, :],
                            in_=dw[oc * 128:(oc + 1) * 128,
                                   dg * 1024:(dg + 1) * 1024])
                        S.dma(i, f"d8s{sl}")
            block.gpsimd(_wdma)

            def _x2f(sync, h=h):
                for dg in range(2):
                    idx = h * 2 + dg
                    if idx > 0:
                        S.wait(sync, "dev", idx)
                    i = sync.dma_start(
                        out=x2f[:],
                        in_=x2v[:, dg * 8:(dg + 1) * 8, h * TH:(h + 1) * TH])
                    S.dma(i, "x2f")
            block.sync(_x2f)

            def _mm(tensor, h=h):
                S.wait(tensor, "x2bf", 16 * (h + 1))
                if h > 0:
                    S.wait(tensor, "ln_norm", h * HC)
                for o in range(OSB):
                    sl = o % 2
                    g = h * OSB + o
                    S.wait(tensor, f"g8s{sl}", 256 * (g // 2 + 1))
                    S.wait(tensor, f"u8s{sl}", 256 * (g // 2 + 1))
                    for ob in range(OPB):
                        gob = o * OPB + ob
                        gbank, ubank = (0, 1) if gob % 2 == 0 else (2, 3)
                        if gob >= 2:
                            S.wait(tensor, "guev", h * NOB + gob - 1)
                        for hc in range(HC):
                            tensor.matmul(pb[gbank][:],
                                          g8[:, sl, hc, ob * 128:(ob + 1) * 128],
                                          x2bf[:, hc, :],
                                          start=(hc == 0), stop=(hc == HC - 1))
                        for hc in range(HC):
                            i = tensor.matmul(pb[ubank][:],
                                              u8[:, sl, hc, ob * 128:(ob + 1) * 128],
                                              x2bf[:, hc, :],
                                              start=(hc == 0), stop=(hc == HC - 1))
                        S.inc(i, "gups")
                S.wait(tensor, "guev", (h + 1) * NOB)
                for dg in range(2):
                    if dg == 1:
                        S.wait(tensor, "dev", h * 2 + 1)
                    for oc in range(NOB):
                        sl = oc % 8
                        k = (h * 2 + dg) * NOB + oc
                        S.wait(tensor, f"d8s{sl}", 16 * (k // 8 + 1))
                        for db in range(8):
                            i = tensor.matmul(pb[db][:],
                                              d8[:, sl, db * 128:(db + 1) * 128],
                                              fuse[:, oc, :],
                                              start=(oc == 0), stop=(oc == NOB - 1))
                        S.inc(i, "d_mm")
            block.tensor(_mm)

            def _silu(scalar, h=h):
                if h == 0:
                    S.wait(scalar, "consts", 48)
                for gob in range(NOB):
                    gbank = 0 if gob % 2 == 0 else 2
                    sl = gob % 2
                    S.wait(scalar, "gups", h * NOB + gob + 1)
                    if gob >= 2:
                        S.wait(scalar, "guev", h * NOB + gob - 1)
                    i = scalar.activation(out=gsil[:, sl, :], in_=pb[gbank][:],
                                          func=AF.Silu,
                                          scale=gscT[:, gob:gob + 1])
                    S.inc(i, "gsil")
            block.scalar(_silu)

            def _evac(vector, h=h):
                if h == 0:
                    S.wait(vector, "consts", 48)
                for gob in range(NOB):
                    ubank = 1 if gob % 2 == 0 else 3
                    sl = gob % 2
                    S.wait(vector, "gups", h * NOB + gob + 1)
                    i = vector.tensor_scalar(out=fuse[:, gob, :], in0=pb[ubank][:],
                                             scalar1=uscT[:, gob:gob + 1],
                                             scalar2=None, op0=ALU.mult)
                    c = S.inc(i, "vch")
                    S.wait(vector, "gsil", h * NOB + gob + 1)
                    S.wait(vector, "vch", c)
                    i = vector.tensor_tensor(out=fuse[:, gob, :],
                                             in0=fuse[:, gob, :],
                                             in1=gsil[:, sl, :], op=ALU.mult)
                    S.inc(i, "guev")
                for dg in range(2):
                    idx = h * 2 + dg
                    S.wait(vector, "d_mm", (idx + 1) * NOB)
                    S.wait(vector, "x2f", 16 * (idx + 1))
                    for db in range(8):
                        hc = dg * 8 + db
                        i = vector.tensor_scalar(out=hres[:, hc, :], in0=pb[db][:],
                                                 scalar1=dscT[:, hc:hc + 1],
                                                 scalar2=None, op0=ALU.mult)
                        c = S.inc(i, "vch")
                        S.wait(vector, "vch", c)
                        i = vector.tensor_tensor(out=hres[:, hc, :],
                                                 in0=hres[:, hc, :],
                                                 in1=x2f[:, db, :], op=ALU.add)
                    S.inc(i, "dev")
            block.vector(_evac)

            emit_postln(block, S, "ln", h, hres, hbf, hsq, onec, oner, rows,
                        (pb[0], pb[1]), (pb[2], pb[3]), xo2,
                        x3v[:, :, h * TH:(h + 1) * TH], epsc,
                        ("dev", 2 * (h + 1)))

        def _final(sync):
            S.wait(sync, "ln_st0", 16 * HC * NTH // 2)
            S.wait(sync, "ln_st1", 16 * HC * NTH // 2)
        block.sync(_final)

    return nc


def _l0_ffn_in_maps(inputs, x2_full):
    gw = np.ascontiguousarray(np.asarray(inputs["L0_ffn_gate_w"], dtype=np.float32))
    uw = np.ascontiguousarray(np.asarray(inputs["L0_ffn_up_w"], dtype=np.float32))
    dw = np.ascontiguousarray(np.asarray(inputs["L0_ffn_down_w"], dtype=np.float32))
    gsc = cols128(inputs["L0_ffn_gate_scale"])
    usc = cols128(inputs["L0_ffn_up_scale"])
    dsc = cols128(inputs["L0_ffn_down_scale"])
    maps = []
    for c in range(NCORES):
        b, q = c // 4, c % 4
        x2 = fmaj(x2_full[b, q * TPC:(q + 1) * TPC, :])
        maps.append({"x2": x2, "gw": gw, "uw": uw, "dw": dw,
                     "gsc": gsc, "usc": usc, "dsc": dsc})
    return maps


def _gather_fm(results, key):
    out = np.empty((B, T, H), dtype=np.float32)
    for c in range(NCORES):
        b, q = c // 4, c % 4
        out[b, q * TPC:(q + 1) * TPC, :] = results[c][key].T
    return out


def run_l0_ffn(inputs, x2_full):
    nc = build_l0_ffn()
    res = run_bass_kernel_spmd(nc, _l0_ffn_in_maps(inputs, x2_full),
                               core_ids=list(range(NCORES)))
    return _gather_fm(res.results, "x3")



def build_l1_ffn(nc=None, G=None, ext=None):
    nc, dram = _phase_setup(nc, ext, "g_")
    x4 = dram("x4", [H, TPC], F32, "ExternalInput")
    tw1 = dram("tw1", [H, I1], FP8, "ExternalInput")
    tw2 = dram("tw2", [H, I1], FP8, "ExternalInput")
    tw3 = dram("tw3", [I1, H], FP8, "ExternalInput")
    x5 = dram("x5", [H, TPC], F32, "ExternalOutput")

    x4v = x4.rearrange("(c p) t -> p c t", p=128)
    x5v = x5.rearrange("(c p) t -> p c t", p=128)

    NOB = I1 // 128
    OSB = 22
    OPB = NOB // OSB
    OSW = OPB * 128
    NTH = 2
    TH = TPC // NTH

    with contextlib.ExitStack() as ctx:
        sb = lambda n, *a: ctx.enter_context(nc.sbuf_tensor(_mangle(n), *a))
        psb = lambda n, *a: ctx.enter_context(nc.psum_tensor(_mangle(n), *a))
        S = PhaseSync(G) if G is not None else PhaseSync(Sync(nc, ctx))

        x4bf = sb("x4bf", [128, HC, TH], BF16)
        t1 = sb("t1", [128, 2, HC, OSW], FP8)
        t2 = sb("t2", [128, 2, HC, OSW], FP8)
        t3 = sb("t3", [128, 4, 512], FP8)
        hidden = sb("hidden", [128, NOB, TH], BF16)
        lnst = sb("lnst", [128, 2 * HC, TH], BF16)
        sig = sb("sig", [128, 2, TH], BF16)
        hres = sb("hres", [128, HC, TH], F32)
        x4f = sb("x4f", [128, 4, TH], F32)
        xo2 = sb("xo2", [128, 2, TH], F32)
        onec = sb("onec", [128, 1], BF16)
        oner = sb("oner", [1, 128], F32)
        epsc = sb("epsc", [1, 1], F32)
        rows = sb("rows", [1, 5, TH], F32)
        pb = [psb(f"pb{i}", [128, 512], F32) for i in range(8)]
        hbf = lnst[:, 0:HC, :]
        hsq = lnst[:, HC:2 * HC, :]

        block = ctx.enter_context(nc.Block())

        def _init(gpsimd):
            gpsimd.memset(onec[:], 1.0)
            gpsimd.memset(epsc[:], LN_EPS)
            i = gpsimd.memset(oner[:], 1.0)
            S.inc(i, "ones")
        block.gpsimd(_init)

        for h in range(NTH):
            def _wdma(gpsimd, h=h):
                if h == 0 and G is not None:
                    S.g.wait(gpsimd, "cct", 7)
                if h > 0:
                    S.wait(gpsimd, "mm12ps", h * NOB)
                i = gpsimd.dma_start(out=x4bf[:],
                                     in_=x4v[:, :, h * TH:(h + 1) * TH])
                S.dma(i, "x4bf")
                for o in range(OSB):
                    sl = o % 2
                    g = h * OSB + o
                    if g >= 2:
                        S.wait(gpsimd, "mm12ps", (g - 1) * OPB)
                    for hc in range(HC):
                        i = gpsimd.dma_start(
                            out=t1[:, sl, hc, 0:OSW],
                            in_=tw1[hc * 128:(hc + 1) * 128, o * OSW:(o + 1) * OSW])
                        S.dma(i, f"w1ld{sl}")
                    for hc in range(HC):
                        i = gpsimd.dma_start(
                            out=t2[:, sl, hc, 0:OSW],
                            in_=tw2[hc * 128:(hc + 1) * 128, o * OSW:(o + 1) * OSW])
                        S.dma(i, f"w2ld{sl}")
                for g4 in range(4):
                    for oc in range(NOB):
                        k = (h * 4 + g4) * NOB + oc
                        sl = k % 4
                        if k >= 4:
                            S.wait(gpsimd, "mm3", k - 3)
                        i = gpsimd.dma_start(
                            out=t3[:, sl, :],
                            in_=tw3[oc * 128:(oc + 1) * 128,
                                    g4 * 512:(g4 + 1) * 512])
                        S.dma(i, f"w3ld{sl}")
            block.gpsimd(_wdma)

            def _x4f(sync, h=h):
                for hc in range(HC):
                    ci = h * 16 + hc
                    if ci >= 4:
                        S.wait(sync, "devc", ci - 3)
                    i = sync.dma_start(
                        out=x4f[:, ci % 4, :],
                        in_=x4v[:, hc, h * TH:(h + 1) * TH])
                    S.dma(i, f"x4f{ci % 4}")
            block.sync(_x4f)

            def _vec(vector, h=h):
                for gob in range(NOB):
                    b2 = 1 if gob % 2 == 0 else 3
                    sl = gob % 2
                    S.wait(vector, "mm12ps", h * NOB + gob + 1)
                    S.wait(vector, "sig", h * NOB + gob + 1)
                    i = vector.tensor_tensor(out=hidden[:, gob, :],
                                             in0=pb[b2][:],
                                             in1=sig[:, sl, :], op=ALU.mult)
                    S.inc(i, "hev")
                for g4 in range(4):
                    S.wait(vector, "mm3", (h * 4 + g4 + 1) * NOB)
                    for j in range(4):
                        hc = g4 * 4 + j
                        ci = h * 16 + hc
                        i = vector.tensor_copy(out=hres[:, hc, :], in_=pb[j][:])
                        c = S.inc(i, "vch")
                        S.wait(vector, "vch", c)
                        S.wait(vector, f"x4f{ci % 4}", 16 * (ci // 4 + 1))
                        i = vector.tensor_tensor(out=hres[:, hc, :],
                                                 in0=hres[:, hc, :],
                                                 in1=x4f[:, ci % 4, :], op=ALU.add)
                        S.inc(i, "devc")
            block.vector(_vec)

            def _mm(tensor, h=h):
                S.wait(tensor, "x4bf", 16 * (h + 1))
                if h > 0:
                    S.wait(tensor, "devc", h * 16)
                for o in range(OSB):
                    sl = o % 2
                    g = h * OSB + o
                    S.wait(tensor, f"w1ld{sl}", 256 * (g // 2 + 1))
                    S.wait(tensor, f"w2ld{sl}", 256 * (g // 2 + 1))
                    for ob in range(OPB):
                        gob = o * OPB + ob
                        b1, b2 = (0, 1) if gob % 2 == 0 else (2, 3)
                        if gob >= 2:
                            S.wait(tensor, "hev", h * NOB + gob - 1)
                        for hc in range(HC):
                            tensor.matmul(pb[b1][:],
                                          t1[:, sl, hc, ob * 128:(ob + 1) * 128],
                                          x4bf[:, hc, :],
                                          start=(hc == 0), stop=(hc == HC - 1))
                        for hc in range(HC):
                            i = tensor.matmul(pb[b2][:],
                                              t2[:, sl, hc, ob * 128:(ob + 1) * 128],
                                              x4bf[:, hc, :],
                                              start=(hc == 0), stop=(hc == HC - 1))
                        S.inc(i, "mm12ps")
                for g4 in range(4):
                    if g4 == 0:
                        S.wait(tensor, "hev", (h + 1) * NOB)
                        if h > 0:
                            S.wait(tensor, "ln_norm", h * HC)
                    else:
                        S.wait(tensor, "devc", h * 16 + g4 * 4)
                    for oc in range(NOB):
                        k = (h * 4 + g4) * NOB + oc
                        sl = k % 4
                        S.wait(tensor, f"w3ld{sl}", 16 * (k // 4 + 1))
                        for db in range(4):
                            i = tensor.matmul(pb[db][:],
                                              t3[:, sl, db * 128:(db + 1) * 128],
                                              hidden[:, oc, :],
                                              start=(oc == 0), stop=(oc == NOB - 1))
                        S.inc(i, "mm3")
            block.tensor(_mm)

            def _sig(scalar, h=h):
                for gob in range(NOB):
                    b1 = 0 if gob % 2 == 0 else 2
                    sl = gob % 2
                    S.wait(scalar, "mm12ps", h * NOB + gob + 1)
                    if gob >= 2:
                        S.wait(scalar, "hev", h * NOB + gob - 1)
                    i = scalar.activation(out=sig[:, sl, :], in_=pb[b1][:],
                                          func=AF.Sigmoid)
                    S.inc(i, "sig")
            block.scalar(_sig)

            emit_postln(block, S, "ln", h, hres, hbf, hsq, onec, oner, rows,
                        (pb[4], pb[5]), (pb[6], pb[7]), xo2,
                        x5v[:, :, h * TH:(h + 1) * TH], epsc,
                        ("devc", h * 16, 1, 1))

        def _final(sync):
            S.wait(sync, "ln_st0", 16 * HC * NTH // 2)
            S.wait(sync, "ln_st1", 16 * HC * NTH // 2)
        block.sync(_final)

    return nc


def _l1_ffn_in_maps(inputs, x4_full):
    w1t = np.ascontiguousarray(np.asarray(inputs["L1_ffn_w1"], dtype=np.float32).T)
    w2t = np.ascontiguousarray(np.asarray(inputs["L1_ffn_w2"], dtype=np.float32).T)
    w3t = np.ascontiguousarray(np.asarray(inputs["L1_ffn_w3"], dtype=np.float32).T)
    maps = []
    for c in range(NCORES):
        b, q = c // 4, c % 4
        x4 = fmaj(x4_full[b, q * TPC:(q + 1) * TPC, :])
        maps.append({"x4": x4, "w1t": w1t, "w2t": w2t, "w3t": w3t})
    return maps


def run_l1_ffn(inputs, x4_full):
    nc = build_l1_ffn()
    res = run_bass_kernel_spmd(nc, _l1_ffn_in_maps(inputs, x4_full),
                               core_ids=list(range(NCORES)))
    return _gather_fm(res.results, "x5")





def build_l0_mixer_a(nc=None, G=None, ext=None):
    nc, dram = _phase_setup(nc, ext, "a_")
    xin = dram("xin", [H, TPC + 1], F32, "ExternalInput")
    wkT = dram("wkT", [H, H], F32, "ExternalInput")
    wvT = dram("wvT", [H, H], F32, "ExternalInput")
    wrT = dram("wrT", [H, H], F32, "ExternalInput")
    bkc = dram("bkc", [128, HC], F32, "ExternalInput")
    bvc = dram("bvc", [128, HC], F32, "ExternalInput")
    brc = dram("brc", [128, HC], F32, "ExternalInput")
    tmk = dram("tmk", [128, HC], F32, "ExternalInput")
    tmv = dram("tmv", [128, HC], F32, "ExternalInput")
    tmr = dram("tmr", [128, HC], F32, "ExternalInput")
    tdec = dram("tdec", [128, HC], F32, "ExternalInput")
    tfir = dram("tfir", [128, HC], F32, "ExternalInput")
    nout = dram("nout", [H, TPC], BF16, "ExternalOutput")
    dout = dram("dout", [H, TPC], BF16, "ExternalOutput")
    srout = dram("srout", [H, TPC], BF16, "ExternalOutput")
    totout = dram("totout", [128, 48], F32, "ExternalOutput")

    xv_ = xin.rearrange("(c p) t -> p c t", p=128)
    noutv = nout.rearrange("(c p) t -> p c t", p=128)
    doutv = dout.rearrange("(c p) t -> p c t", p=128)
    sroutv = srout.rearrange("(c p) t -> p c t", p=128)

    SBW = 256
    NSB = H // SBW
    OCPS = SBW // 128

    with contextlib.ExitStack() as ctx:
        sb = lambda n, *a: ctx.enter_context(nc.sbuf_tensor(_mangle(n), *a))
        psb = lambda n, *a: ctx.enter_context(nc.psum_tensor(_mangle(n), *a))
        S = PhaseSync(G) if G is not None else PhaseSync(Sync(nc, ctx))

        xbf = sb("xbf", [128, HC, TPC + 1], BF16)
        xl = sb("xl", [128, HC, TPC], BF16)
        z = sb("z", [128, HC, TPC], BF16)
        ws = sb("ws", [128, 2, HC, SBW], BF16)
        vtmp = sb("vtmp", [128, TPC], BF16)
        ytmp = sb("ytmp", [128, TPC], BF16)
        dt_t = sb("dt_t", [128, TPC], F32)
        s_oc = sb("s_oc", [128, TPC], F32)
        z_oc = sb("z_oc", [128, TPC], F32)
        t1 = sb("t1", [128, TPC], F32)
        ndb = sb("ndb", [128, 2, TPC], BF16)
        ddb = sb("ddb", [128, 2, TPC], BF16)
        srdb = sb("srdb", [128, 2, TPC], BF16)
        one4 = sb("one4", [128, TPC], F32)
        tmkc = sb("tmkc", [128, HC], F32)
        tmvc = sb("tmvc", [128, HC], F32)
        tmrc = sb("tmrc", [128, HC], F32)
        bkcc = sb("bkcc", [128, HC], F32)
        bvcc = sb("bvcc", [128, HC], F32)
        brcc = sb("brcc", [128, HC], F32)
        tdc = sb("tdc", [128, HC], F32)
        tfc = sb("tfc", [128, HC], F32)
        euc = sb("euc", [128, HC], F32)
        dcol = sb("dcol", [128, HC], F32)
        bkm = sb("bkm", [128, HC], F32)
        mc2 = sb("mc2", [128, 2], F32)
        tots = sb("tots", [128, 48], F32)
        pb = [psb(f"pb{i}", [128, 512], F32) for i in range(4)]

        block = ctx.enter_context(nc.Block(no_gpsimd_drain=True))

        def _consts(sync):
            for ap_, src in ((tmkc, tmk), (tmvc, tmv), (tmrc, tmr),
                             (bkcc, bkc), (bvcc, bvc), (brcc, brc),
                             (tdc, tdec), (tfc, tfir)):
                i = sync.dma_start(out=ap_[:], in_=src)
                S.dma(i, "consts")
        block.sync(_consts)

        def _xld(gpsimd):
            i = gpsimd.memset(one4[:], 1.0)
            S.inc(i, "ones")
            i = gpsimd.dma_start(out=xbf[:], in_=xv_)
            S.dma(i, "xbf")
        block.gpsimd(_xld)

        def _prep(scalar):
            S.wait(scalar, "consts", 128)
            i = scalar.activation(out=euc[:], in_=tfc[:], func=AF.Exp)
            c = S.inc(i, "sch")
            i = scalar.activation(out=dcol[:], in_=tdc[:], func=AF.Exp)
            c = S.inc(i, "sch")
            S.wait(scalar, "sch", c)
            i = scalar.mul(out=dcol[:], in_=dcol[:], mul=-1.0)
            c = S.inc(i, "sch")
            S.wait(scalar, "sch", c)
            i = scalar.activation(out=dcol[:], in_=dcol[:], func=AF.Exp)
            S.inc(i, "prep")
        block.scalar(_prep)

        PROJ = (("k", wkT, "zev"), ("v", wvT, "vsc"), ("r", wrT, "rsc"))

        def _wdma(gpsimd):
            for pi, (pname, wT, _) in enumerate(PROJ):
                for sbi in range(NSB):
                    sl = sbi % 2
                    gsb = pi * NSB + sbi
                    if gsb >= 2:
                        prev = gsb - 2
                        if prev < NSB:
                            S.wait(gpsimd, "kps", (prev + 1) * OCPS)
                        elif prev < 2 * NSB:
                            S.wait(gpsimd, "vps", (prev - NSB + 1) * OCPS)
                        else:
                            S.wait(gpsimd, "rps", (prev - 2 * NSB + 1) * OCPS)
                    for hc in range(HC):
                        i = gpsimd.dma_start(
                            out=ws[:, sl, hc, :],
                            in_=wT[hc * 128:(hc + 1) * 128,
                                   sbi * SBW:(sbi + 1) * SBW])
                        S.dma(i, f"ws{sl}")
        block.gpsimd(_wdma)

        def _vec(vector):
            S.wait(vector, "prep", 1)

            def lerp(tm, dst_tag):
                i = vector.tensor_tensor(out=xl[:], in0=xbf[:, :, 1:TPC + 1],
                                         in1=xbf[:, :, 0:TPC], op=ALU.subtract)
                c = S.inc(i, "vch")
                S.wait(vector, "vch", c)
                for hc in range(HC):
                    i = vector.scalar_tensor_tensor(
                        out=xl[:, hc, :], in0=xl[:, hc, :],
                        scalar=tm[:, hc:hc + 1], in1=xbf[:, hc, 0:TPC],
                        op0=ALU.mult, op1=ALU.add)
                    if hc == HC - 1:
                        S.inc(i, dst_tag)
                    else:
                        S.inc(i, "vch")

            S.wait(vector, "xbf", 16)
            lerp(tmkc, "xlk")
            for oc in range(HC):
                e, o = (0, 1) if oc % 2 == 0 else (2, 3)
                S.wait(vector, "kps", oc + 1)
                S.wait(vector, "mrdy", oc)
                i = vector.reduce_max(out=mc2[:, 0:1], in_=pb[e][:], axis=AX)
                c = S.inc(i, "vch")
                i = vector.reduce_max(out=mc2[:, 1:2], in_=pb[o][:], axis=AX)
                c = S.inc(i, "vch")
                S.wait(vector, "vch", c)
                i = vector.tensor_tensor(out=tots[:, 32 + oc:33 + oc],
                                         in0=mc2[:, 0:1], in1=mc2[:, 1:2],
                                         op=ALU.max)
                c = S.inc(i, "vch")
                S.wait(vector, "vch", c)
                i = vector.tensor_tensor(out=bkm[:, oc:oc + 1],
                                         in0=bkcc[:, oc:oc + 1],
                                         in1=tots[:, 32 + oc:33 + oc],
                                         op=ALU.subtract)
                S.inc(i, "mrdy")
            S.wait(vector, "kps", HC)
            lerp(tmvc, "xlv")
            for oc in range(HC):
                S.wait(vector, "vsc", 2 * (oc + 1))
                S.wait(vector, "ndrdy", oc)
                i = vector.tensor_tensor(out=ytmp[:], in0=z[:, oc, :],
                                         in1=vtmp[:], op=ALU.mult)
                c = S.inc(i, "vch")
                i = vector.tensor_scalar(out=dt_t[:], in0=one4[:],
                                         scalar1=dcol[:, oc:oc + 1],
                                         scalar2=None, op0=ALU.mult)
                c = S.inc(i, "vch")
                S.wait(vector, "vch", c)
                i = vector.tensor_tensor_scan(out=s_oc[:], data0=dt_t[:],
                                              data1=ytmp[:], initial=0.0,
                                              op0=ALU.mult, op1=ALU.add)
                c = S.inc(i, "vch")
                i = vector.tensor_tensor_scan(out=z_oc[:], data0=dt_t[:],
                                              data1=z[:, oc, :], initial=0.0,
                                              op0=ALU.mult, op1=ALU.add)
                c = S.inc(i, "vch")
                i = vector.tensor_scalar(out=t1[:], in0=ytmp[:],
                                         scalar1=euc[:, oc:oc + 1],
                                         scalar2=None, op0=ALU.mult)
                c = S.inc(i, "vch")
                S.wait(vector, "vch", c)
                sl = oc % 2
                if oc >= 2:
                    S.wait(vector, "ndst0" if sl == 0 else "ndst1",
                           32 * (oc // 2))
                i = vector.tensor_tensor(out=ndb[:, sl, 1:TPC], in0=t1[:, 1:TPC],
                                         in1=s_oc[:, 0:TPC - 1], op=ALU.add)
                c = S.inc(i, "vch")
                i = vector.tensor_copy(out=ndb[:, sl, 0:1], in_=t1[:, 0:1])
                c = S.inc(i, "vch")
                S.wait(vector, "vch", c)
                i = vector.tensor_scalar(out=t1[:], in0=z[:, oc, :],
                                         scalar1=euc[:, oc:oc + 1],
                                         scalar2=None, op0=ALU.mult)
                c = S.inc(i, "vch")
                S.wait(vector, "vch", c)
                i = vector.tensor_tensor(out=ddb[:, sl, 1:TPC], in0=t1[:, 1:TPC],
                                         in1=z_oc[:, 0:TPC - 1], op=ALU.add)
                c = S.inc(i, "vch")
                i = vector.tensor_copy(out=ddb[:, sl, 0:1], in_=t1[:, 0:1])
                c = S.inc(i, "vch")
                vector.tensor_copy(out=tots[:, oc:oc + 1],
                                   in_=s_oc[:, TPC - 1:TPC])
                S.wait(vector, "vch", c)
                i = vector.tensor_copy(out=tots[:, 16 + oc:17 + oc],
                                       in_=z_oc[:, TPC - 1:TPC])
                S.inc(i, "ndrdy")
            S.wait(vector, "vps", HC)
            lerp(tmrc, "xlr")
        block.vector(_vec)

        def _mm(tensor):
            S.wait(tensor, "xlk", 1)
            for oc in range(HC):
                sbi = oc // OCPS
                sl = sbi % 2
                e, o = (0, 1) if oc % 2 == 0 else (2, 3)
                S.wait(tensor, "ws0" if sl == 0 else "ws1",
                       256 * (sbi // 2 + 1))
                if oc >= 2:
                    S.wait(tensor, "zev", 2 * (oc - 1))
                for hh in range(2):
                    for hc in range(HC):
                        i = tensor.matmul(
                            pb[e if hh == 0 else o][:],
                            ws[:, sl, hc, (oc % OCPS) * 128:(oc % OCPS + 1) * 128],
                            xl[:, hc, hh * 512:(hh + 1) * 512],
                            start=(hc == 0), stop=(hc == HC - 1))
                S.inc(i, "kps")
            S.wait(tensor, "xlv", 1)
            S.wait(tensor, "zev", 2 * HC)
            for oc in range(HC):
                sbi = oc // OCPS
                sl = sbi % 2
                gsb = NSB + sbi
                e, o = (0, 1) if oc % 2 == 0 else (2, 3)
                S.wait(tensor, "ws0" if sl == 0 else "ws1",
                       256 * (gsb // 2 + 1))
                if oc >= 2:
                    S.wait(tensor, "vsc", 2 * (oc - 1))
                for hh in range(2):
                    for hc in range(HC):
                        i = tensor.matmul(
                            pb[e if hh == 0 else o][:],
                            ws[:, sl, hc, (oc % OCPS) * 128:(oc % OCPS + 1) * 128],
                            xl[:, hc, hh * 512:(hh + 1) * 512],
                            start=(hc == 0), stop=(hc == HC - 1))
                S.inc(i, "vps")
            S.wait(tensor, "xlr", 1)
            S.wait(tensor, "vsc", 2 * HC)
            for oc in range(HC):
                sbi = oc // OCPS
                sl = sbi % 2
                gsb = 2 * NSB + sbi
                e, o = (0, 1) if oc % 2 == 0 else (2, 3)
                S.wait(tensor, "ws0" if sl == 0 else "ws1",
                       256 * (gsb // 2 + 1))
                if oc >= 2:
                    S.wait(tensor, "rsc", 2 * (oc - 1))
                for hh in range(2):
                    for hc in range(HC):
                        i = tensor.matmul(
                            pb[e if hh == 0 else o][:],
                            ws[:, sl, hc, (oc % OCPS) * 128:(oc % OCPS + 1) * 128],
                            xl[:, hc, hh * 512:(hh + 1) * 512],
                            start=(hc == 0), stop=(hc == HC - 1))
                S.inc(i, "rps")
        block.tensor(_mm)

        def _sev(scalar):
            for oc in range(HC):
                e, o = (0, 1) if oc % 2 == 0 else (2, 3)
                S.wait(scalar, "mrdy", oc + 1)
                scalar.activation(out=z[:, oc, 0:512], in_=pb[e][:],
                                  func=AF.Exp, bias=bkm[:, oc:oc + 1])
                i = scalar.activation(out=z[:, oc, 512:1024], in_=pb[o][:],
                                      func=AF.Exp, bias=bkm[:, oc:oc + 1])
                S.inc(i, "zev", 2)
            for oc in range(HC):
                e, o = (0, 1) if oc % 2 == 0 else (2, 3)
                S.wait(scalar, "vps", oc + 1)
                if oc >= 1:
                    S.wait(scalar, "ndrdy", oc)
                scalar.activation(out=vtmp[:, 0:512], in_=pb[e][:],
                                  func=AF.Identity, bias=bvcc[:, oc:oc + 1])
                i = scalar.activation(out=vtmp[:, 512:1024], in_=pb[o][:],
                                      func=AF.Identity, bias=bvcc[:, oc:oc + 1])
                S.inc(i, "vsc", 2)
            for oc in range(HC):
                e, o = (0, 1) if oc % 2 == 0 else (2, 3)
                sl = oc % 2
                S.wait(scalar, "rps", oc + 1)
                if oc >= 2:
                    S.wait(scalar, "srst0" if sl == 0 else "srst1",
                           16 * (oc // 2))
                scalar.activation(out=srdb[:, sl, 0:512], in_=pb[e][:],
                                  func=AF.Sigmoid, bias=brcc[:, oc:oc + 1])
                i = scalar.activation(out=srdb[:, sl, 512:1024], in_=pb[o][:],
                                      func=AF.Sigmoid, bias=brcc[:, oc:oc + 1])
                S.inc(i, "rsc", 2)
        block.scalar(_sev)

        def _st(sync):
            for oc in range(HC):
                sl = oc % 2
                S.wait(sync, "ndrdy", oc + 1)
                i = sync.dma_start(out=noutv[:, oc, :], in_=ndb[:, sl, :])
                S.dma(i, f"ndst{sl}")
                i = sync.dma_start(out=doutv[:, oc, :], in_=ddb[:, sl, :])
                S.dma(i, f"ndst{sl}")
            for oc in range(HC):
                sl = oc % 2
                S.wait(sync, "rsc", 2 * (oc + 1))
                i = sync.dma_start(out=sroutv[:, oc, :], in_=srdb[:, sl, :])
                S.dma(i, f"srst{sl}")
            S.wait(sync, "ndrdy", HC)
            i = sync.dma_start(out=totout, in_=tots[:])
            S.dma(i, "totst")
            S.wait(sync, "ndst0", 32 * 8)
            S.wait(sync, "ndst1", 32 * 8)
            S.wait(sync, "srst0", 16 * 8)
            S.wait(sync, "srst1", 16 * 8)
            S.wait(sync, "totst", 16)
        block.sync(_st)

        if G is not None:
            TCH = _tern_chunks()
            shv = {nm: dram(f"sh{nm}", [rs, cols], F32, "ExternalInput")
                   for nm, rows, cols, rs in TMATS}
            gprt = dram("gprt", [128, 8], F32, "Internal")
            gprd = dram("gprd", [128, 8], F32, "Internal")
            tstg = sb("tstg", [128, 2, I1], BF16)
            tgacc = sb("tgacc", [128, 8], F32)
            tmcol = sb("tmcol", [128, 1], F32)

            def _tgld(gpsimd):
                gpsimd.memset(tgacc[:, 7:8], 0.0)
                for n, (mi, nm, off, nr, cols) in enumerate(TCH):
                    sl = n % 2
                    if n >= 2:
                        S.wait(gpsimd, "tgred", n - 1)
                    i = gpsimd.dma_start(out=tstg[0:nr, sl, 0:cols],
                                         in_=shv[nm][off:off + nr, :])
                    S.dma(i, f"tgld{sl}")
                S.wait(gpsimd, "tgred", len(TCH))
                i = gpsimd.dma_start(out=gprt, in_=tgacc[:])
                S.dma(i, "tgx")
                S.wait(gpsimd, "tgx", 16)
                i = gpsimd.collective_compute("AllReduce", ALU.add,
                                              replica_groups=[list(range(8))],
                                              ins=[gprt.opt()],
                                              outs=[gprd.opt()])
                S.g.pin("gar")
                S.g.inc(i, "gar")
            block.gpsimd(_tgld)

            def _tgred(vector):
                for n, (mi, nm, off, nr, cols) in enumerate(TCH):
                    sl = n % 2
                    S.wait(vector, f"tgld{sl}", 16 * (n // 2 + 1))
                    if off == 0:
                        i = vector.reduce_sum(out=tgacc[0:nr, mi:mi + 1],
                                              in_=tstg[0:nr, sl, 0:cols],
                                              axis=AX,
                                              apply_absolute_value=True)
                    else:
                        i = vector.reduce_sum(out=tmcol[0:nr, :],
                                              in_=tstg[0:nr, sl, 0:cols],
                                              axis=AX,
                                              apply_absolute_value=True)
                        c = S.inc(i, "vch")
                        S.wait(vector, "vch", c)
                        i = vector.tensor_tensor(out=tgacc[0:nr, mi:mi + 1],
                                                 in0=tgacc[0:nr, mi:mi + 1],
                                                 in1=tmcol[0:nr, :],
                                                 op=ALU.add)
                    S.inc(i, "tgred")
            block.vector(_tgred)

    return nc



def build_l0_mixer_b(nc=None, G=None, ext=None):
    nc, dram = _phase_setup(nc, ext, "b_")
    nin = dram("nin", [H, TPC], BF16, "ExternalInput")
    din = dram("din", [H, TPC], BF16, "ExternalInput")
    srin = dram("srin", [H, TPC], BF16, "ExternalInput")
    totga = dram("totga", [512, 48], F32, "ExternalInput")
    ownt = dram("ownt", [128, 48], F32, "ExternalInput")
    vg = dram("vg", [128, 8], F32, "ExternalInput")
    woT = dram("woT", [H, H], F32, "ExternalInput")
    boc = dram("boc", [128, HC], F32, "ExternalInput")
    tdec = dram("tdec", [128, HC], F32, "ExternalInput")
    iota = dram("iota", [1, TPC], F32, "ExternalInput")
    xres = dram("xres", [H, TPC], F32, "ExternalInput")
    x2 = dram("x2", [H, TPC], F32, "ExternalOutput")

    ninv = nin.rearrange("(c p) t -> p c t", p=128)
    dinv = din.rearrange("(c p) t -> p c t", p=128)
    srinv = srin.rearrange("(c p) t -> p c t", p=128)
    xrv = xres.rearrange("(c p) t -> p c t", p=128)
    x2v = x2.rearrange("(c p) t -> p c t", p=128)

    SBW = 256
    NSB = H // SBW
    OCPS = SBW // 128
    NTH = 2
    TH = TPC // NTH

    with contextlib.ExitStack() as ctx:
        sb = lambda n, *a: ctx.enter_context(nc.sbuf_tensor(_mangle(n), *a))
        psb = lambda n, *a: ctx.enter_context(nc.psum_tensor(_mangle(n), *a))
        S = PhaseSync(G) if G is not None else PhaseSync(Sync(nc, ctx))

        rwkv = sb("rwkv", [128, HC, TPC], BF16)
        ws = sb("ws", [128, 2, HC, SBW], BF16)
        ndb = sb("ndb", [128, 2, TPC], BF16)
        ddb = sb("ddb", [128, 2, TPC], BF16)
        srdb = sb("srdb", [128, 2, TPC], BF16)
        iotab = sb("iotab", [128, TPC], F32)
        dp = sb("dp", [128, TPC], F32)
        nf = sb("nf", [128, TPC], F32)
        df = sb("df", [128, TPC], F32)
        hres = sb("hres", [128, HC, TH], F32)
        xrf = sb("xrf", [128, 8, TH], F32)
        xo2 = sb("xo2", [128, 2, TH], F32)
        predst = sb("predst", [128, 4, 48], F32)
        ownc = sb("ownc", [128, 48], F32)
        vgc = sb("vgc", [128, 8], F32)
        tdc = sb("tdc", [128, HC], F32)
        dcc = sb("dcc", [128, HC], F32)
        bocc = sb("bocc", [128, HC], F32)
        sinz = sb("sinz", [128, 2, HC], F32)
        tq = sb("tq", [128, 2, HC], F32)
        iorow = sb("iorow", [1, TPC], F32)
        onec = sb("onec", [128, 1], BF16)
        oner = sb("oner", [1, 128], F32)
        epsc = sb("epsc", [1, 1], F32)
        rows = sb("rows", [1, 5, TH], F32)
        pb = [psb(f"pb{i}", [128, 512], F32) for i in range(8)]
        hbf = rwkv[:, 0:HC, 0:TH]
        lnst = sb("lnst", [128, 2 * HC, TH], BF16)
        hbf = lnst[:, 0:HC, :]
        hsq = lnst[:, HC:2 * HC, :]

        block = ctx.enter_context(nc.Block())

        def _consts(sync):
            for ap_, src in ((predst, totga.rearrange("(j p) c -> p j c", p=128)),
                             (ownc, ownt), (tdc, tdec), (bocc, boc),
                             (iorow, iota), (vgc, vg)):
                i = sync.dma_start(out=ap_[:], in_=src)
                S.dma(i, "consts")
        block.sync(_consts)

        def _ones(gpsimd):
            gpsimd.memset(onec[:], 1.0)
            gpsimd.memset(epsc[:], LN_EPS)
            gpsimd.memset(sinz[:], 0.0)
            i = gpsimd.memset(oner[:], 1.0)
            S.inc(i, "ones")
        block.gpsimd(_ones)

        def _prep(scalar):
            S.wait(scalar, "consts", 96)
            i = scalar.activation(out=dcc[:], in_=tdc[:], func=AF.Exp)
            c = S.inc(i, "sch")
            S.wait(scalar, "sch", c)
            i = scalar.mul(out=dcc[:], in_=dcc[:], mul=-1.0)
            S.inc(i, "prep")
        block.scalar(_prep)

        def _iobc(tensor):
            S.wait(tensor, "ones", 1)
            S.wait(tensor, "consts", 96)
            tensor.matmul(pb[0][:], oner[:], iorow[:, 0:512], start=True, stop=True)
            i = tensor.matmul(pb[1][:], oner[:], iorow[:, 512:1024],
                              start=True, stop=True)
            S.inc(i, "iops")
        block.tensor(_iobc)

        def _sin(vector):
            S.wait(vector, "iops", 1)
            vector.tensor_copy(out=iotab[:, 0:512], in_=pb[0][:])
            i = vector.tensor_copy(out=iotab[:, 512:1024], in_=pb[1][:])
            S.inc(i, "iotab")
            S.wait(vector, "prep", 1)
            for j in range(4):
                S.wait(vector, "sinz", j)
                i = vector.tensor_tensor(out=tq[:, 0, :],
                                         in0=predst[:, j, 32:48],
                                         in1=ownc[:, 32:48], op=ALU.subtract)
                c = S.inc(i, "vch")
                i = vector.tensor_scalar(out=tq[:, 1, :], in0=dcc[:],
                                         scalar1=vgc[:, 4 + j:5 + j],
                                         scalar2=None, op0=ALU.mult)
                c = S.inc(i, "vch")
                S.wait(vector, "vch", c)
                i = vector.tensor_tensor(out=tq[:, 0, :], in0=tq[:, 0, :],
                                         in1=tq[:, 1, :], op=ALU.add)
                S.inc(i, f"alpha{j}")
                S.wait(vector, f"alphae{j}", 1)
                i = vector.tensor_scalar(out=tq[:, 1, :], in0=tq[:, 1, :],
                                         scalar1=vgc[:, j:j + 1],
                                         scalar2=None, op0=ALU.mult)
                c = S.inc(i, "vch")
                S.wait(vector, "vch", c)
                i = vector.tensor_tensor(out=tq[:, 0, :], in0=predst[:, j, 0:16],
                                         in1=tq[:, 1, :], op=ALU.mult)
                c = S.inc(i, "vch")
                S.wait(vector, "vch", c)
                i = vector.tensor_tensor(out=sinz[:, 0, :], in0=sinz[:, 0, :],
                                         in1=tq[:, 0, :], op=ALU.add)
                c = S.inc(i, "vch")
                S.wait(vector, "vch", c)
                i = vector.tensor_tensor(out=tq[:, 0, :], in0=predst[:, j, 16:32],
                                         in1=tq[:, 1, :], op=ALU.mult)
                c = S.inc(i, "vch")
                S.wait(vector, "vch", c)
                i = vector.tensor_tensor(out=sinz[:, 1, :], in0=sinz[:, 1, :],
                                         in1=tq[:, 0, :], op=ALU.add)
                S.inc(i, "sinz")
        block.vector(_sin)

        def _alphae(scalar):
            for j in range(4):
                S.wait(scalar, f"alpha{j}", 1)
                i = scalar.activation(out=tq[:, 1, :], in_=tq[:, 0, :], func=AF.Exp)
                S.inc(i, f"alphae{j}")
        block.scalar(_alphae)

        def _ndld(sync):
            for hc in range(HC):
                sl = hc % 2
                if hc >= 2:
                    S.wait(sync, "fix", hc - 1)
                i = sync.dma_start(out=ndb[:, sl, :], in_=ninv[:, hc, :])
                S.dma(i, f"ndld{sl}")
                i = sync.dma_start(out=ddb[:, sl, :], in_=dinv[:, hc, :])
                S.dma(i, f"ndld{sl}")
                i = sync.dma_start(out=srdb[:, sl, :], in_=srinv[:, hc, :])
                S.dma(i, f"ndld{sl}")
        block.sync(_ndld)

        def _dpe(scalar):
            for hc in range(HC):
                S.wait(scalar, "dpv", hc + 1)
                i = scalar.activation(out=dp[:], in_=dp[:], func=AF.Exp)
                S.inc(i, "dpe")
        block.scalar(_dpe)

        def _fix(vector):
            S.wait(vector, "sinz", 4)
            S.wait(vector, "iotab", 1)
            for hc in range(HC):
                sl = hc % 2
                if hc >= 1:
                    S.wait(vector, "fix", hc)
                i = vector.tensor_scalar(out=dp[:], in0=iotab[:],
                                         scalar1=dcc[:, hc:hc + 1], scalar2=None,
                                         op0=ALU.mult)
                S.inc(i, "dpv")
                S.wait(vector, "dpe", hc + 1)
                S.wait(vector, f"ndld{sl}", 48 * (hc // 2 + 1))
                i = vector.scalar_tensor_tensor(
                    out=nf[:], in0=dp[:], scalar=sinz[:, 0, hc:hc + 1],
                    in1=ndb[:, sl, :], op0=ALU.mult, op1=ALU.add)
                c = S.inc(i, "vch")
                i = vector.scalar_tensor_tensor(
                    out=df[:], in0=dp[:], scalar=sinz[:, 1, hc:hc + 1],
                    in1=ddb[:, sl, :], op0=ALU.mult, op1=ALU.add)
                c = S.inc(i, "vch")
                S.wait(vector, "vch", c)
                i = vector.reciprocal(out=df[:], in_=df[:])
                c = S.inc(i, "vch")
                S.wait(vector, "vch", c)
                i = vector.tensor_tensor(out=nf[:], in0=nf[:], in1=df[:],
                                         op=ALU.mult)
                c = S.inc(i, "vch")
                S.wait(vector, "vch", c)
                i = vector.tensor_tensor(out=rwkv[:, hc, :], in0=nf[:],
                                         in1=srdb[:, sl, :], op=ALU.mult)
                S.inc(i, "fix")
        block.vector(_fix)

        def _wdma(gpsimd):
            for h in range(NTH):
                for sbi in range(NSB):
                    sl = sbi % 2
                    gsb = h * NSB + sbi
                    if gsb >= 2:
                        S.wait(gpsimd, "ops", (gsb - 1) * OCPS)
                    for hc in range(HC):
                        i = gpsimd.dma_start(
                            out=ws[:, sl, hc, :],
                            in_=woT[hc * 128:(hc + 1) * 128,
                                    sbi * SBW:(sbi + 1) * SBW])
                        S.dma(i, f"ws{sl}")
        block.gpsimd(_wdma)

        for h in range(NTH):
            def _xrf(sync, h=h):
                for dg in range(2):
                    idx = h * 2 + dg
                    if idx > 0:
                        S.wait(sync, "oev", idx * 8)
                    i = sync.dma_start(
                        out=xrf[:],
                        in_=xrv[:, dg * 8:(dg + 1) * 8, h * TH:(h + 1) * TH])
                    S.dma(i, "xrf")
            block.sync(_xrf)

            def _omm(tensor, h=h):
                S.wait(tensor, "fix", HC)
                if h > 0:
                    S.wait(tensor, "ln_norm", h * HC)
                for oc in range(HC):
                    sbi = oc // OCPS
                    sl = sbi % 2
                    gsb = h * NSB + sbi
                    bank = 0 if oc % 2 == 0 else 1
                    S.wait(tensor, f"ws{sl}", 256 * (gsb // 2 + 1))
                    if oc >= 2:
                        S.wait(tensor, "oev", h * HC + oc - 1)
                    for hc in range(HC):
                        i = tensor.matmul(
                            pb[bank][:],
                            ws[:, sl, hc, (oc % OCPS) * 128:(oc % OCPS + 1) * 128],
                            rwkv[:, hc, h * TH:(h + 1) * TH],
                            start=(hc == 0), stop=(hc == HC - 1))
                    S.inc(i, "ops")
            block.tensor(_omm)

            def _oev(vector, h=h):
                for oc in range(HC):
                    bank = 0 if oc % 2 == 0 else 1
                    dg = oc // 8
                    idx = h * 2 + dg
                    S.wait(vector, "ops", h * HC + oc + 1)
                    S.wait(vector, "xrf", 16 * (idx + 1))
                    i = vector.scalar_tensor_tensor(
                        out=hres[:, oc, :], in0=pb[bank][:],
                        scalar=bocc[:, oc:oc + 1], in1=xrf[:, oc % 8, :],
                        op0=ALU.add, op1=ALU.add)
                    S.inc(i, "oev")
            block.vector(_oev)

            emit_postln(block, S, "ln", h, hres, hbf, hsq, onec, oner, rows,
                        (pb[4], pb[5]), (pb[6], pb[7]), xo2,
                        x2v[:, :, h * TH:(h + 1) * TH], epsc,
                        ("oev", h * HC, 1, 1))

        def _final(sync):
            S.wait(sync, "ln_st0", 16 * HC * NTH // 2)
            S.wait(sync, "ln_st1", 16 * HC * NTH // 2)
        block.sync(_final)

    return nc


def _l0_mixer_a_in_maps(inputs, x_full):
    wkT = np.ascontiguousarray(np.asarray(inputs["L0_attn_Wk"], dtype=np.float32).T)
    wvT = np.ascontiguousarray(np.asarray(inputs["L0_attn_Wv"], dtype=np.float32).T)
    wrT = np.ascontiguousarray(np.asarray(inputs["L0_attn_Wr"], dtype=np.float32).T)
    cc = {"bkc": cols128(inputs["L0_attn_bk"]), "bvc": cols128(inputs["L0_attn_bv"]),
          "brc": cols128(inputs["L0_attn_br"]),
          "tmk": cols128(np.asarray(inputs["L0_attn_tmk"]).reshape(-1)),
          "tmv": cols128(np.asarray(inputs["L0_attn_tmv"]).reshape(-1)),
          "tmr": cols128(np.asarray(inputs["L0_attn_tmr"]).reshape(-1)),
          "tdec": cols128(inputs["L0_attn_time_decay"]),
          "tfir": cols128(inputs["L0_attn_time_first"])}
    maps = []
    for c in range(NCORES):
        b, q = c // 4, c % 4
        lo = q * TPC
        xin = np.zeros((H, TPC + 1), dtype=np.float32)
        xin[:, 1:] = np.asarray(x_full[b, lo:lo + TPC, :], dtype=np.float32).T
        if q > 0:
            xin[:, 0] = np.asarray(x_full[b, lo - 1, :], dtype=np.float32)
        maps.append({"xin": np.ascontiguousarray(xin), "wkT": wkT, "wvT": wvT,
                     "wrT": wrT, **cc})
    return maps


def _l0_mixer_b_in_maps(inputs, x_full, a_results):
    woT = np.ascontiguousarray(np.asarray(inputs["L0_attn_Wo"], dtype=np.float32).T)
    boc = cols128(inputs["L0_attn_bo"])
    tdec = cols128(inputs["L0_attn_time_decay"])
    iota = np.arange(TPC, dtype=np.float32).reshape(1, TPC)
    maps = []
    for c in range(NCORES):
        b, q = c // 4, c % 4
        own = np.ascontiguousarray(a_results[c]["totout"])
        totga = np.vstack([a_results[b * 4 + j]["totout"] for j in range(4)])
        maps.append({"nin": a_results[c]["nout"], "din": a_results[c]["dout"],
                     "srin": a_results[c]["srout"],
                     "totga": np.ascontiguousarray(totga), "ownt": own,
                     "vg": _mixer_vg(q),
                     "woT": woT, "boc": boc, "tdec": tdec, "iota": iota,
                     "xres": fmaj(x_full[b, q * TPC:(q + 1) * TPC, :])})
    return maps


def _mixer_vg(q):
    vg = np.zeros((128, 8), dtype=np.float32)
    for j in range(4):
        if j < q:
            vg[:, j] = 1.0
            vg[:, 4 + j] = 1024.0 * (q - 1 - j)
    return vg


def run_l0_mixer(inputs, x_full):
    nca = build_l0_mixer_a()
    resa = run_bass_kernel_spmd(nca, _l0_mixer_a_in_maps(inputs, x_full),
                                core_ids=list(range(NCORES)))
    ncb = build_l0_mixer_b()
    resb = run_bass_kernel_spmd(ncb, _l0_mixer_b_in_maps(inputs, x_full,
                                                         resa.results),
                                core_ids=list(range(NCORES)))
    return _gather_fm(resb.results, "x2")


MAGIC = 12582912.0
ROPE_SCALE = 1.0


def build_l1_qkv(nc=None, G=None, ext=None):
    nc, dram = _phase_setup(nc, ext, "d_")
    x3 = dram("x3", [H, TPC], F32, "ExternalInput")
    twq = dram("twq", [H, H], FP8, "ExternalInput")
    twk = dram("twk", [H, 512], FP8, "ExternalInput")
    twv = dram("twv", [H, 512], FP8, "ExternalInput")
    swv = dram("swv", [1, 8], F32, "ExternalInput")
    cosr = dram("cosr", [64, TPC], F32, "ExternalInput")
    sinr = dram("sinr", [64, TPC], F32, "ExternalInput")
    idtf = dram("idtf", [128, 128], F32, "ExternalInput")
    idtb = dram("idtb", [128, 128], BF16, "ExternalInput")
    dvs = dram("dvs", [4, 1], F32, "ExternalInput")
    qout = dram("qout", [H, TPC], BF16, "ExternalOutput")
    kout = dram("kout", [1024, TPC], BF16, "ExternalOutput")
    vout = dram("vout", [2 * NKV, TPC, DH], BF16, "ExternalOutput")

    x3v = x3.rearrange("(c p) t -> p c t", p=128)
    qov = qout.rearrange("(c p) t -> p c t", p=128)
    kov = kout.rearrange("(c p) t -> p c t", p=128)
    vov = vout.rearrange("kv (tb p) d -> p kv tb d", p=128)

    SBW = 256
    NTB = TPC // 128
    NOCQ, NOCK = NH, NKV
    SBL = []
    for m, src, noc in ((0, twq, NOCQ), (1, twk, NOCK), (2, twv, NOCK)):
        for sbi in range(noc * 128 // SBW):
            SBL.append((m, src, sbi))
    OCL = []
    for gsb, (m, src, sbi) in enumerate(SBL):
        for obi in range(SBW // 128):
            OCL.append((m, sbi * (SBW // 128) + obi, gsb, obi))

    with contextlib.ExitStack() as ctx:
        sb = lambda n, *a: ctx.enter_context(nc.sbuf_tensor(_mangle(n), *a))
        psb = lambda n, *a: ctx.enter_context(nc.psum_tensor(_mangle(n), *a))
        S = PhaseSync(G) if G is not None else PhaseSync(Sync(nc, ctx))

        xq = sb("xq", [128, HC, TPC], BF16)
        q = sb("q", [128, NH, TPC], BF16)
        kk = sb("kk", [128, NKV, TPC], BF16)
        vraw = sb("vraw", [128, NKV, TPC], BF16)
        vtm = sb("vtm", [128, NKV * NTB, 128], BF16)
        terb = sb("terb", [128, 2, HC, SBW], FP8)
        xjit = sb("xjit", [128, 2, TPC], F32)
        cosb = sb("cosb", [64, TPC], BF16)
        sinb = sb("sinb", [64, TPC], BF16)
        rt = sb("rt", [64, 5, TPC], BF16)
        idf = sb("idf", [128, 128], F32)
        idb = sb("idb", [128, 128], BF16)
        dvsc = sb("dvsc", [4, 1], F32)
        grow = sb("grow", [1, 8], F32)
        swcc = sb("swcc", [128, 1], F32)
        mx8 = sb("mx8", [128, 8], F32)
        mxc = sb("mxc", [128, 8], F32)
        scvc = sb("scvc", [128, 8], F32)
        mcol = sb("mcol", [128, 1], F32)
        msb = sb("msb", [8, 128], F32)
        mrow = sb("mrow", [1, TPC], F32)
        sxrow = sb("sxrow", [1, TPC], F32)
        screw = sb("screw", [1, 2, TPC], F32)
        sxb = sb("sxb", [128, TPC], F32)
        scb = sb("scb", [128, 2, TPC], F32)
        tmp = sb("tmp", [128, TPC], F32)
        oner = sb("oner", [1, 128], F32)
        pb = [psb(f"pb{i}", [128, 512], F32) for i in range(6)]
        pbt = psb("pbt", [128, 128], BF16)

        block = ctx.enter_context(nc.Block())

        def _consts(sync):
            for ap_, src in ((idf, idtf), (dvsc, dvs)):
                i = sync.dma_start(out=ap_[:], in_=src)
                S.dma(i, "consts")
        block.sync(_consts)

        def _init(gpsimd):
            gpsimd.memset(mx8[:], 0.0)
            i = gpsimd.memset(oner[:], 1.0)
            S.inc(i, "ones")
            i = gpsimd.dma_start(out=cosb[:], in_=cosr)
            S.dma(i, "ropet")
            i = gpsimd.dma_start(out=sinb[:], in_=sinr)
            S.dma(i, "ropet")
            i = gpsimd.dma_start(out=idb[:], in_=idtb)
            S.dma(i, "idbl")
            if G is not None:
                S.g.wait(gpsimd, "cct", 7)
            i = gpsimd.dma_start(out=grow[:, 0:8], in_=swv)
            S.dma(i, "gx")
        block.gpsimd(_init)

        def _gbc(tensor):
            S.wait(tensor, "gx", 16)
            S.wait(tensor, "ones", 1)
            i = tensor.matmul(pb[4][:, 0:4], oner[:], grow[:, 0:4],
                              start=True, stop=True)
            S.inc(i, "gbc")
        block.tensor(_gbc)

        def _gcols(vector):
            S.wait(vector, "gbc", 1)
            i = vector.tensor_copy(out=swcc[:], in_=pb[4][:, 2:3])
            S.inc(i, "gcols")
        block.vector(_gcols)

        def _xjit1(sync):
            for hc in range(HC):
                sl = hc % 2
                if hc >= 2:
                    S.wait(sync, "tmax", hc - 1)
                i = sync.dma_start(out=xjit[:, sl, :], in_=x3v[:, hc, :])
                S.dma(i, f"xj{sl}")
        block.sync(_xjit1)

        def _xtp(tensor):
            S.wait(tensor, "consts", 32)
            S.wait(tensor, "gbc", 1)
            for hc in range(HC):
                sl = hc % 2
                S.wait(tensor, f"xj{sl}", 16 * (hc // 2 + 1))
                if hc >= 1:
                    S.wait(tensor, "tmax", hc)
                for tb in range(NTB):
                    bank = tb // 4
                    i = tensor.transpose(
                        pb[bank][:, (tb % 4) * 128:(tb % 4 + 1) * 128],
                        xjit[:, sl, tb * 128:(tb + 1) * 128], idf[:])
                S.inc(i, "xtp")
        block.tensor(_xtp)

        def _vmax(vector):
            for hc in range(HC):
                S.wait(vector, "xtp", hc + 1)
                for tb in range(NTB):
                    bank = tb // 4
                    i = vector.reduce_max(
                        out=mcol[:],
                        in_=pb[bank][:, (tb % 4) * 128:(tb % 4 + 1) * 128],
                        axis=AX, apply_absolute_value=True)
                    c = S.inc(i, "vch")
                    S.wait(vector, "vch", c)
                    i = vector.tensor_tensor(out=mx8[:, tb:tb + 1],
                                             in0=mx8[:, tb:tb + 1], in1=mcol[:],
                                             op=ALU.max)
                    if tb == NTB - 1:
                        S.inc(i, "tmax")
                    else:
                        c = S.inc(i, "vch")
                        S.wait(vector, "vch", c)
        block.vector(_vmax)

        def _mclip(vector):
            S.wait(vector, "tmax", HC)
            S.wait(vector, "gcols", 1)
            i = vector.tensor_scalar(out=mxc[:], in0=mx8[:], scalar1=1e-5,
                                     scalar2=None, op0=ALU.max)
            c = S.inc(i, "vch")
            S.wait(vector, "vch", c)
            i = vector.tensor_scalar(out=scvc[:], in0=mxc[:],
                                     scalar1=swcc[:],
                                     scalar2=1.0 / 127.0,
                                     op0=ALU.mult, op1=ALU.mult)
            S.inc(i, "mclip")
        block.vector(_mclip)

        def _mtp(tensor):
            S.wait(tensor, "mclip", 1)
            i = tensor.transpose(pb[5][0:8, 0:128], mxc[:], idf[:])
            S.inc(i, "mtp")
        block.tensor(_mtp)

        def _mev(vector):
            S.wait(vector, "mtp", 1)
            i = vector.tensor_copy(out=msb[:], in_=pb[5][0:8, 0:128])
            S.inc(i, "mev")
        block.vector(_mev)

        def _mmv(gpsimd):
            S.wait(gpsimd, "mev", 1)
            for tb in range(NTB):
                i = gpsimd.dma_start(out=mrow[:, tb * 128:(tb + 1) * 128],
                                     in_=msb[tb:tb + 1, :])
                S.dma(i, "mrowd")
        block.gpsimd(_mmv)

        def _sxr(vector):
            S.wait(vector, "mrowd", 128)
            i = vector.reciprocal(out=sxrow[:], in_=mrow[:])
            c = S.inc(i, "vch")
            S.wait(vector, "vch", c)
            i = vector.tensor_scalar(out=sxrow[:], in0=sxrow[:], scalar1=127.0,
                                     scalar2=None, op0=ALU.mult)
            c = S.inc(i, "vch")
            for m in range(2):
                i = vector.tensor_scalar(out=screw[:, m, :], in0=mrow[:],
                                         scalar1=grow[:, m:m + 1],
                                         scalar2=1.0 / 127.0,
                                         op0=ALU.mult, op1=ALU.mult)
                c = S.inc(i, "vch")
            S.wait(vector, "vch", c)
            i = vector.nop()
            S.inc(i, "sxr")
        block.vector(_sxr)

        def _sxbc(tensor):
            S.wait(tensor, "sxr", 1)
            for half in range(2):
                tensor.matmul(pb[0][:], oner[:],
                              sxrow[:, half * 512:(half + 1) * 512],
                              start=True, stop=True)
                tensor.matmul(pb[1][:], oner[:],
                              screw[:, 0, half * 512:(half + 1) * 512],
                              start=True, stop=True)
                i = tensor.matmul(pb[2][:], oner[:],
                                  screw[:, 1, half * 512:(half + 1) * 512],
                                  start=True, stop=True)
                S.inc(i, "sxps")
                S.wait(tensor, "sxev", half + 1)
        block.tensor(_sxbc)

        def _sxev(vector):
            for half in range(2):
                S.wait(vector, "sxps", half + 1)
                vector.tensor_copy(out=sxb[:, half * 512:(half + 1) * 512],
                                   in_=pb[0][:])
                vector.tensor_copy(out=scb[:, 0, half * 512:(half + 1) * 512],
                                   in_=pb[1][:])
                i = vector.tensor_copy(out=scb[:, 1, half * 512:(half + 1) * 512],
                                       in_=pb[2][:])
                S.inc(i, "sxev")
        block.vector(_sxev)

        def _xjit2(sync):
            for hc in range(HC):
                sl = hc % 2
                if hc >= 2:
                    S.wait(sync, "xqv", hc - 1)
                else:
                    S.wait(sync, "xtp", HC)
                i = sync.dma_start(out=xjit[:, sl, :], in_=x3v[:, hc, :])
                S.dma(i, f"xq{sl}")
        block.sync(_xjit2)

        def _xqv(vector):
            S.wait(vector, "sxev", 2)
            for hc in range(HC):
                sl = hc % 2
                S.wait(vector, f"xq{sl}", 16 * (hc // 2 + 1))
                S.wait(vector, "xqv", hc)
                i = vector.tensor_tensor(out=tmp[:], in0=xjit[:, sl, :],
                                         in1=sxb[:], op=ALU.mult)
                c = S.inc(i, "vch")
                S.wait(vector, "vch", c)
                i = vector.tensor_scalar(out=tmp[:], in0=tmp[:], scalar1=MAGIC,
                                         scalar2=None, op0=ALU.add)
                c = S.inc(i, "vch")
                S.wait(vector, "vch", c)
                i = vector.tensor_scalar(out=xq[:, hc, :], in0=tmp[:],
                                         scalar1=MAGIC, scalar2=None,
                                         op0=ALU.subtract)
                S.inc(i, "xqv")
        block.vector(_xqv)

        def _wdma(gpsimd):
            for gsb, (m, src, sbi) in enumerate(SBL):
                sl = gsb % 2
                if gsb >= 2:
                    S.wait(gpsimd, "mmps", 2 * (gsb - 1))
                for hc in range(HC):
                    i = gpsimd.dma_start(
                        out=terb[:, sl, hc, :],
                        in_=src[hc * 128:(hc + 1) * 128,
                                sbi * SBW:(sbi + 1) * SBW])
                    S.dma(i, f"ws{sl}")
        block.gpsimd(_wdma)

        def _vec3(vector):
            S.wait(vector, "ropet", 32)

            def evac(go):
                m, oc, gsb, obi = OCL[go]
                bank = go % 2
                S.wait(vector, "mmps", go + 1)
                for hh in range(2):
                    if m == 0:
                        dsl = q[:, oc, hh * 512:(hh + 1) * 512]
                        scs = scb[:, 0, hh * 512:(hh + 1) * 512]
                    elif m == 1:
                        dsl = kk[:, oc, hh * 512:(hh + 1) * 512]
                        scs = scb[:, 1, hh * 512:(hh + 1) * 512]
                    else:
                        dsl = vraw[:, oc, hh * 512:(hh + 1) * 512]
                        scs = None
                    if m == 2:
                        i = vector.tensor_copy(out=dsl, in_=pb[bank * 2 + hh][:])
                    else:
                        i = vector.tensor_tensor(out=dsl,
                                                 in0=pb[bank * 2 + hh][:],
                                                 in1=scs, op=ALU.mult)
                    S.inc(i, "qev")

            def rope1(gh, buf, hd):
                S.wait(vector, "roped", gh)
                S.wait(vector, "qev", S.counts["qev"])
                x1 = buf[0:64, hd, :]
                x2 = buf[64:128, hd, :]
                i = vector.tensor_copy(out=rt[:, 2, :], in_=x2)
                c = S.inc(i, "vch")
                i = vector.tensor_tensor(out=rt[:, 0, :], in0=x1,
                                         in1=cosb[:], op=ALU.mult)
                S.inc(i, "vch")
                S.wait(vector, "vch", c)
                i = vector.tensor_tensor(out=rt[:, 1, :], in0=rt[:, 2, :],
                                         in1=sinb[:], op=ALU.mult)
                S.inc(i, "vch")
                i = vector.tensor_tensor(out=rt[:, 3, :], in0=rt[:, 2, :],
                                         in1=cosb[:], op=ALU.mult)
                S.inc(i, "vch")
                i = vector.tensor_tensor(out=rt[:, 4, :], in0=x1,
                                         in1=sinb[:], op=ALU.mult)
                c = S.inc(i, "vch")
                S.wait(vector, "vch", c)
                i = vector.tensor_tensor(out=x1, in0=rt[:, 0, :],
                                         in1=rt[:, 1, :], op=ALU.subtract)
                S.inc(i, "vch")
                i = vector.tensor_tensor(out=rt[:, 1, :], in0=rt[:, 3, :],
                                         in1=rt[:, 4, :], op=ALU.add)
                c = S.inc(i, "vch")
                S.wait(vector, "vch", c)
                i = vector.tensor_copy(out=x2, in_=rt[:, 1, :])
                S.inc(i, "roped")

            def post(go):
                m, oc, gsb, obi = OCL[go]
                if m == 0:
                    rope1(go, q, oc)
                elif m == 1:
                    rope1(NH + oc, kk, oc)

            for go in range(len(OCL)):
                evac(go)
                post(go)
        block.vector(_vec3)

        def _mm(tensor):
            S.wait(tensor, "xqv", HC)
            S.wait(tensor, "sxev", 2)
            for go, (m, oc, gsb, obi) in enumerate(OCL):
                sl = gsb % 2
                bank = go % 2
                S.wait(tensor, f"ws{sl}", 256 * (gsb // 2 + 1))
                if go >= 2:
                    S.wait(tensor, "qev", 2 * (go - 1))
                for hh in range(2):
                    for hc in range(HC):
                        i = tensor.matmul(
                            pb[bank * 2 + hh][:],
                            terb[:, sl, hc, obi * 128:(obi + 1) * 128],
                            xq[:, hc, hh * 512:(hh + 1) * 512],
                            start=(hc == 0), stop=(hc == HC - 1))
                S.inc(i, "mmps")
        block.tensor(_mm)

        def _vtp(tensor):
            S.wait(tensor, "idbl", 16)
            S.wait(tensor, "qev", 48)
            n = 0
            for kv in range(NKV):
                for tb in range(NTB):
                    if n >= 1:
                        S.wait(tensor, "vtev", n)
                    i = tensor.transpose(pbt[:],
                                         vraw[:, kv, tb * 128:(tb + 1) * 128],
                                         idb[:])
                    S.inc(i, "vtp")
                    n += 1
        block.tensor(_vtp)

        def _vtev(vector):
            S.wait(vector, "mclip", 1)
            n = 0
            for kv in range(NKV):
                for tb in range(NTB):
                    S.wait(vector, "vtp", n + 1)
                    i = vector.tensor_scalar(out=vtm[:, kv * NTB + tb, :],
                                             in0=pbt[:],
                                             scalar1=scvc[:, tb:tb + 1],
                                             scalar2=None, op0=ALU.mult)
                    S.inc(i, "vtev")
                    n += 1
        block.vector(_vtev)

        def _st(sync):
            for hd in range(NH):
                S.wait(sync, "roped", hd + 1)
                i = sync.dma_start(out=qov[:, hd, :], in_=q[:, hd, :])
                S.dma(i, "qst")
            for hd in range(NKV):
                S.wait(sync, "roped", NH + hd + 1)
                i = sync.dma_start(out=kov[:, 2 * hd, :], in_=kk[:, hd, :])
                S.dma(i, "qst")
                i = sync.dma_start(out=kov[:, 2 * hd + 1, :], in_=kk[:, hd, :])
                S.dma(i, "qst")
            for kv in range(NKV):
                for tb in range(NTB):
                    S.wait(sync, "vtev", kv * NTB + tb + 1)
                    i = sync.dma_start(out=vov[:, 2 * kv, tb, :],
                                       in_=vtm[:, kv * NTB + tb, :])
                    S.dma(i, "qst")
                    i = sync.dma_start(out=vov[:, 2 * kv + 1, tb, :],
                                       in_=vtm[:, kv * NTB + tb, :])
                    S.dma(i, "qst")
            S.wait(sync, "qst", 16 * (NH + 2 * NKV + 2 * NKV * NTB))
        block.sync(_st)

    return nc



def build_l1_attn(nc=None, G=None, ext=None):
    nc, dram = _phase_setup(nc, ext, "e_")
    qin = dram("qin", [4 * DH, T], BF16, "ExternalInput")
    kin = dram("kin", [DH, T], BF16, "ExternalInput")
    vin = dram("vin", [T, DH], BF16, "ExternalInput")
    msk = dram("msk", [128, 4, 512], BF16, "ExternalInput")
    aout = dram("aout", [4 * DH, T], BF16, "ExternalOutput")
    sout = dram("sout", [4, T], F32, "ExternalOutput")

    qv = qin.rearrange("(h p) t -> p h t", p=128)
    av = aout.rearrange("(h p) t -> p h t", p=128)
    vv = vin.rearrange("(kb p) d -> p kb d", p=128)
    SCALE = float(1.0 / np.sqrt(DH))
    NQB = T // 512

    with contextlib.ExitStack() as ctx:
        sb = lambda n, *a: ctx.enter_context(nc.sbuf_tensor(_mangle(n), *a))
        psb = lambda n, *a: ctx.enter_context(nc.psum_tensor(_mangle(n), *a))
        S = PhaseSync(G) if G is not None else PhaseSync(Sync(nc, ctx))

        qt = sb("qt", [128, 4, T], BF16)
        kt = sb("kt", [128, T], BF16)
        vt = sb("vt", [128, T // 128, DH], BF16)
        mskt = sb("mskt", [128, 4, 512], BF16)
        pt = sb("pt", [128, 2, 512], BF16)
        onec = sb("onec", [128, 1], BF16)
        ao = sb("ao", [128, 4, T], BF16)
        sums = sb("sums", [1, 4 * T], F32)
        pb = [psb(f"pb{i}", [128, 512], F32) for i in range(4)]

        block = ctx.enter_context(nc.Block())

        def _ld(sync):
            i = sync.dma_start(out=qt[:], in_=qv)
            S.dma(i, "ld")
            i = sync.dma_start(out=kt[:], in_=kin)
            S.dma(i, "ld")
            i = sync.dma_start(out=vt[:], in_=vv)
            S.dma(i, "ld")
            i = sync.dma_start(out=mskt[:], in_=msk)
            S.dma(i, "ld")
        block.sync(_ld)

        def _ones(gpsimd):
            i = gpsimd.memset(onec[:], 1.0)
            S.inc(i, "ones")
        block.gpsimd(_ones)

        nit = 0
        ni = 0
        nb = 0
        for hd in range(4):
            for qb in range(NQB):
                nkb = (qb + 1) * 4
                ni0, nb0 = ni, nb
                def _qk(tensor, hd=hd, qb=qb, nkb=nkb, nit=nit, ni0=ni0,
                        nb0=nb0):
                    S.wait(tensor, "ld", 64)
                    S.wait(tensor, "ones", 1)
                    g = hd * NQB + qb
                    if g > 0:
                        S.wait(tensor, "oev", 2 * g)
                    lni, lnb = ni0, nb0
                    for j in range(nkb):
                        it = nit + j
                        sl = j % 2
                        if j >= 2:
                            S.wait(tensor, "pvd", it - 1)
                        i = tensor.matmul(pb[sl][:],
                                          kt[:, j * 128:(j + 1) * 128],
                                          qt[:, hd, qb * 512:(qb + 1) * 512],
                                          start=True, stop=True)
                        S.inc(i, "qkps")
                        if j < qb * 4:
                            lni += 1
                            S.wait(tensor, "ptei", lni)
                        else:
                            lnb += 1
                            S.wait(tensor, "pteb", lnb)
                        i = tensor.matmul(pb[2][:], vt[:, j, :], pt[:, sl, :],
                                          start=(j == 0), stop=(j == nkb - 1))
                        i = tensor.matmul(pb[3][0:1, :], onec[:], pt[:, sl, :],
                                          start=(j == 0), stop=(j == nkb - 1))
                        S.inc(i, "pvd")
                block.tensor(_qk)

                def _pt(scalar, hd=hd, qb=qb, nkb=nkb, nit=nit, ni0=ni0,
                        nb0=nb0):
                    lni = ni0
                    for j in range(nkb):
                        it = nit + j
                        sl = j % 2
                        S.wait(scalar, "qkps", it + 1)
                        if j >= 2:
                            S.wait(scalar, "pvd", it - 1)
                        if j < qb * 4:
                            lni += 1
                            i = scalar.activation(out=pt[:, sl, :],
                                                  in_=pb[sl][:],
                                                  func=AF.Exp, scale=SCALE)
                            S.inc(i, "ptei")
                        else:
                            i = scalar.activation(out=pt[:, sl, :],
                                                  in_=pb[sl][:],
                                                  func=AF.Exp, scale=SCALE)
                            S.inc(i, "ptee")
                block.scalar(_pt)

                def _msk(vector, hd=hd, qb=qb, nkb=nkb, nit=nit, nb0=nb0):
                    lnb = nb0
                    for j in range(qb * 4, nkb):
                        sl = j % 2
                        off = j - qb * 4
                        lnb += 1
                        S.wait(vector, "ptee", lnb)
                        i = vector.tensor_tensor(out=pt[:, sl, :],
                                                 in0=pt[:, sl, :],
                                                 in1=mskt[:, off, :],
                                                 op=ALU.mult)
                        S.inc(i, "pteb")
                block.vector(_msk)

                def _oev(vector, hd=hd, qb=qb, nit=nit, nkb=nkb):
                    S.wait(vector, "pvd", nit + nkb)
                    i = vector.tensor_copy(out=ao[:, hd, qb * 512:(qb + 1) * 512],
                                           in_=pb[2][:])
                    S.inc(i, "oev")
                    i = vector.tensor_copy(
                        out=sums[:, hd * T + qb * 512:hd * T + (qb + 1) * 512],
                        in_=pb[3][0:1, :])
                    S.inc(i, "oev")
                block.vector(_oev)

                nit += nkb
                ni += qb * 4
                nb += 4

        def _st(sync):
            for hd in range(4):
                S.wait(sync, "oev", 2 * ((hd + 1) * NQB))
                i = sync.dma_start(out=av[:, hd, :], in_=ao[:, hd, :])
                S.dma(i, "ast")
                i = sync.dma_start(
                    out=sout[hd:hd + 1, :],
                    in_=sums[:, hd * T:(hd + 1) * T])
                S.dma(i, "ast")
            S.wait(sync, "ast", 16 * 8)
        block.sync(_st)

    return nc


def build_l1_oproj(nc=None, G=None, ext=None):
    nc, dram = _phase_setup(nc, ext, "f_")
    ain = dram("ain", [H, TPC], BF16, "ExternalInput")
    sin_ = dram("sin_", [NH, TPC], F32, "ExternalInput")
    two = dram("two", [H, H], FP8, "ExternalInput")
    swv = dram("swv", [1, 8], F32, "ExternalInput")
    x3r = dram("x3r", [H, TPC], F32, "ExternalInput")
    idtf = dram("idtf", [128, 128], F32, "ExternalInput")
    idtb = dram("idtb", [128, 128], BF16, "ExternalInput")
    x4 = dram("x4", [H, TPC], F32, "ExternalOutput")
    fscr = dram("fscr", [NH, TPC], F32, "Internal")

    av = ain.rearrange("(c p) t -> p c t", p=128)
    xrv = x3r.rearrange("(c p) t -> p c t", p=128)
    x4v = x4.rearrange("(c p) t -> p c t", p=128)

    SBW = 256
    NSB = H // SBW
    OCPSF = SBW // 128
    NTB = TPC // 128
    NTH = 2
    TH = 512

    with contextlib.ExitStack() as ctx:
        sb = lambda n, *a: ctx.enter_context(nc.sbuf_tensor(_mangle(n), *a))
        psb = lambda n, *a: ctx.enter_context(nc.psum_tensor(_mangle(n), *a))
        S = PhaseSync(G) if G is not None else PhaseSync(Sync(nc, ctx))

        at = sb("at", [128, HC, TPC], BF16)
        aq = sb("aq", [128, HC, TPC], BF16)
        terb = sb("terb", [128, 2, HC, SBW], FP8)
        srec = sb("srec", [NH, TPC], F32)
        fact = sb("fact", [NH, TPC], F32)
        frj = sb("frj", [1, 2, TPC], F32)
        idf = sb("idf", [128, 128], F32)
        idb = sb("idb", [128, 128], BF16)
        grow = sb("grow", [1, 2], F32)
        mx8 = sb("mx8", [128, 8], F32)
        mcol = sb("mcol", [128, 1], F32)
        msb = sb("msb", [8, 128], F32)
        mrow = sb("mrow", [1, TPC], F32)
        screw = sb("screw", [1, TPC], F32)
        scb = sb("scb", [128, TPC], F32)
        tmp = sb("tmp", [128, TPC], F32)
        hres = sb("hres", [128, HC, TH], F32)
        x3j = sb("x3j", [128, 8, TH], F32)
        xo2 = sb("xo2", [128, 2, TH], F32)
        onec = sb("onec", [128, 1], BF16)
        oner = sb("oner", [1, 128], F32)
        epsc = sb("epsc", [1, 1], F32)
        rows = sb("rows", [1, 5, TH], F32)
        hbf = at[:, :, 0:TH]
        hsq = at[:, :, TH:TPC]
        pb = [psb(f"pb{i}", [128, 512], F32) for i in range(6)]
        pbt = psb("pbt", [128, 128], BF16)
        pbt2 = psb("pbt2", [128, 128], BF16)

        block = ctx.enter_context(nc.Block())

        def _consts(sync):
            i = sync.dma_start(out=idf[:], in_=idtf)
            S.dma(i, "consts")
            i = sync.dma_start(out=idb[:], in_=idtb)
            S.dma(i, "consts")
            i = sync.dma_start(out=srec[:], in_=sin_)
            S.dma(i, "consts")
            i = sync.dma_start(out=at[:], in_=av)
            S.dma(i, "consts")
        block.sync(_consts)

        def _init(gpsimd):
            gpsimd.memset(mx8[:], 0.0)
            gpsimd.memset(onec[:], 1.0)
            gpsimd.memset(epsc[:], LN_EPS)
            i = gpsimd.memset(oner[:], 1.0)
            S.inc(i, "ones")
            if G is not None:
                S.g.wait(gpsimd, "cct", 7)
            i = gpsimd.dma_start(out=grow[:, 0:1], in_=swv[0:1, 3:4])
            S.dma(i, "gx")
        block.gpsimd(_init)

        def _srq(vector):
            S.wait(vector, "consts", 64)
            i = vector.reciprocal(out=srec[:], in_=srec[:])
            S.inc(i, "srq")
        block.vector(_srq)

        def _atp(tensor):
            S.wait(tensor, "consts", 64)
            S.wait(tensor, "srq", 1)
            for tb in range(NTB):
                if tb > 0:
                    S.wait(tensor, "hmax", tb * NH)
                i = tensor.transpose(pb[5][:, 0:NH],
                                     srec[:, tb * 128:(tb + 1) * 128],
                                     idf[0:NH, 0:NH])
                S.inc(i, "stp")
                for hh in range(NH):
                    n = tb * NH + hh
                    if n >= 2:
                        S.wait(tensor, "hmax", n - 1)
                    i = tensor.transpose(pbt[:] if n % 2 == 0 else pbt2[:],
                                         at[:, hh, tb * 128:(tb + 1) * 128],
                                         idb[:])
                    S.inc(i, "atp")
        block.tensor(_atp)

        def _amax(vector):
            for tb in range(NTB):
                S.wait(vector, "stp", tb + 1)
                for hh in range(NH):
                    n2 = tb * NH + hh
                    S.wait(vector, "atp", n2 + 1)
                    i = vector.reduce_max(out=mcol[:],
                                          in_=pbt[:] if n2 % 2 == 0 else pbt2[:],
                                          axis=AX,
                                          apply_absolute_value=True)
                    c = S.inc(i, "vch")
                    S.wait(vector, "vch", c)
                    i = vector.tensor_tensor(out=mcol[:], in0=mcol[:],
                                             in1=pb[5][0:128, hh:hh + 1]
                                             if False else pb[5][:, hh:hh + 1],
                                             op=ALU.mult)
                    c = S.inc(i, "vch")
                    S.wait(vector, "vch", c)
                    i = vector.tensor_tensor(out=mx8[:, tb:tb + 1],
                                             in0=mx8[:, tb:tb + 1],
                                             in1=mcol[:], op=ALU.max)
                    S.inc(i, "hmax")
                i = vector.nop()
                S.inc(i, "tbmax")
        block.vector(_amax)


        def _mtp(tensor):
            S.wait(tensor, "hmax", NTB * NH)
            i = tensor.transpose(pb[5][0:8, 0:128], mx8[:], idf[:])
            S.inc(i, "mtp")
        block.tensor(_mtp)

        def _mev(vector):
            S.wait(vector, "mtp", 1)
            i = vector.tensor_copy(out=msb[:], in_=pb[5][0:8, 0:128])
            S.inc(i, "mev")
        block.vector(_mev)

        def _mmv(gpsimd):
            S.wait(gpsimd, "mev", 1)
            for tb in range(NTB):
                i = gpsimd.dma_start(out=mrow[:, tb * 128:(tb + 1) * 128],
                                     in_=msb[tb:tb + 1, :])
                S.dma(i, "mrowd")
        block.gpsimd(_mmv)

        def _sxr(vector):
            S.wait(vector, "mrowd", 128)
            S.wait(vector, "gx", 16)
            i = vector.tensor_scalar(out=mrow[:], in0=mrow[:], scalar1=1e-5,
                                     scalar2=None, op0=ALU.max)
            c = S.inc(i, "vch")
            S.wait(vector, "vch", c)
            i = vector.reciprocal(out=frj[:, 0, :], in_=mrow[:])
            c = S.inc(i, "vch")
            S.wait(vector, "vch", c)
            i = vector.tensor_scalar(out=frj[:, 0, :], in0=frj[:, 0, :],
                                     scalar1=127.0, scalar2=None, op0=ALU.mult)
            c = S.inc(i, "vch")
            S.wait(vector, "vch", c)
            i = vector.tensor_scalar(out=screw[:], in0=mrow[:],
                                     scalar1=grow[:, 0:1], scalar2=1.0 / 127.0,
                                     op0=ALU.mult, op1=ALU.mult)
            S.inc(i, "sxr")
        block.vector(_sxr)

        def _fbc(tensor):
            S.wait(tensor, "sxr", 1)
            for half in range(2):
                i = tensor.matmul(pb[0][0:NH, :], oner[:, 0:NH],
                                  frj[:, 0, half * 512:(half + 1) * 512],
                                  start=True, stop=True)
                S.inc(i, "fbps")
                S.wait(tensor, "fbev", half + 1)
        block.tensor(_fbc)

        def _fev(vector):
            for half in range(2):
                S.wait(vector, "fbps", half + 1)
                i = vector.tensor_tensor(out=fact[:, half * 512:(half + 1) * 512],
                                         in0=srec[:, half * 512:(half + 1) * 512],
                                         in1=pb[0][0:NH, :], op=ALU.mult)
                S.inc(i, "fbev")
        block.vector(_fev)

        def _fst(gpsimd):
            S.wait(gpsimd, "fbev", 2)
            i = gpsimd.dma_start(out=fscr, in_=fact[:])
            S.dma(i, "fst")
        block.gpsimd(_fst)

        def _scbc(tensor):
            S.wait(tensor, "fbev", 2)
            for half in range(2):
                i = tensor.matmul(pb[1][:], oner[:],
                                  screw[:, half * 512:(half + 1) * 512],
                                  start=True, stop=True)
                S.inc(i, "scps")
                S.wait(tensor, "scev", half + 1)
        block.tensor(_scbc)

        def _scev(vector):
            for half in range(2):
                S.wait(vector, "scps", half + 1)
                i = vector.tensor_copy(out=scb[:, half * 512:(half + 1) * 512],
                                       in_=pb[1][:])
                S.inc(i, "scev")
        block.vector(_scev)

        def _frl(sync):
            S.wait(sync, "fst", 16)
            for hh in range(NH):
                sl = hh % 2
                if hh >= 2:
                    S.wait(sync, "aqv", hh - 1)
                i = sync.dma_start(out=frj[:, sl, :], in_=fscr[hh:hh + 1, :])
                S.dma(i, f"frl{sl}")
        block.sync(_frl)

        def _qbc(tensor):
            for hh in range(NH):
                sl = hh % 2
                S.wait(tensor, f"frl{sl}", 16 * (hh // 2 + 1))
                if hh >= 1:
                    S.wait(tensor, "aqpc", 2 * hh)
                for half in range(2):
                    i = tensor.matmul(pb[2 + half][:], oner[:],
                                      frj[:, sl, half * 512:(half + 1) * 512],
                                      start=True, stop=True)
                S.inc(i, "qbps")
        block.tensor(_qbc)

        def _aqv(vector):
            for hh in range(NH):
                S.wait(vector, "qbps", hh + 1)
                S.wait(vector, "aqv", hh)
                for half in range(2):
                    i = vector.tensor_tensor(
                        out=tmp[:, half * 512:(half + 1) * 512],
                        in0=at[:, hh, half * 512:(half + 1) * 512],
                        in1=pb[2 + half][:], op=ALU.mult)
                    c = S.inc(i, "vch")
                    S.wait(vector, "vch", c)
                    S.inc(vector.nop(), "aqpc")
                i = vector.tensor_scalar(out=tmp[:], in0=tmp[:], scalar1=MAGIC,
                                         scalar2=None, op0=ALU.add)
                c = S.inc(i, "vch")
                S.wait(vector, "vch", c)
                i = vector.tensor_scalar(out=aq[:, hh, :], in0=tmp[:],
                                         scalar1=MAGIC, scalar2=None,
                                         op0=ALU.subtract)
                S.inc(i, "aqv")
        block.vector(_aqv)

        def _wdma2(gpsimd):
            for h in range(NTH):
                for sbi in range(NSB):
                    sl = sbi % 2
                    gsb = h * NSB + sbi
                    if gsb >= 2:
                        S.wait(gpsimd, "omm", (gsb - 1) * OCPSF)
                    for hc in range(HC):
                        i = gpsimd.dma_start(
                            out=terb[:, sl, hc, :],
                            in_=two[hc * 128:(hc + 1) * 128,
                                    sbi * SBW:(sbi + 1) * SBW])
                        S.dma(i, f"wsl{sl}")
        block.gpsimd(_wdma2)

        for h in range(NTH):
            def _x3ld(sync, h=h):
                for dg in range(2):
                    idx = h * 2 + dg
                    if idx > 0:
                        S.wait(sync, "oev", idx * 8)
                    i = sync.dma_start(
                        out=x3j[:],
                        in_=xrv[:, dg * 8:(dg + 1) * 8, h * TH:(h + 1) * TH])
                    S.dma(i, "x3l")
            block.sync(_x3ld)

            def _vecq(vector, h=h):
                def evac(oc):
                    bank = 0 if oc % 2 == 0 else 1
                    dg = oc // 8
                    idx = h * 2 + dg
                    S.wait(vector, "omm", h * HC + oc + 1)
                    S.wait(vector, "x3l", 16 * (idx + 1))
                    i = vector.tensor_tensor(
                        out=hres[:, oc, :], in0=pb[bank][:],
                        in1=scb[:, h * TH:(h + 1) * TH], op=ALU.mult)
                    c = S.inc(i, "vch")
                    S.wait(vector, "vch", c)
                    i = vector.tensor_tensor(out=hres[:, oc, :],
                                             in0=hres[:, oc, :],
                                             in1=x3j[:, oc % 8, :], op=ALU.add)
                    S.inc(i, "oev")

                for oc in range(HC):
                    evac(oc)
            block.vector(_vecq)

            def _omm(tensor, h=h):
                if h == 0:
                    S.wait(tensor, "aqv", NH)
                    S.wait(tensor, "scev", 2)
                else:
                    S.wait(tensor, "ln_norm", h * HC)
                for oc in range(HC):
                    sbi = oc // OCPSF
                    sl = sbi % 2
                    gsb = h * NSB + sbi
                    bank = 0 if oc % 2 == 0 else 1
                    S.wait(tensor, f"wsl{sl}", 256 * (gsb // 2 + 1))
                    if oc >= 2:
                        S.wait(tensor, "oev", h * HC + oc - 1)
                    for hc in range(HC):
                        i = tensor.matmul(
                            pb[bank][:],
                            terb[:, sl, hc,
                                 (oc % OCPSF) * 128:(oc % OCPSF + 1) * 128],
                            aq[:, hc, h * TH:(h + 1) * TH],
                            start=(hc == 0), stop=(hc == HC - 1))
                    S.inc(i, "omm")
            block.tensor(_omm)


            emit_postln(block, S, "ln", h, hres, hbf, hsq, onec, oner, rows,
                        (pb[2], pb[3]), (pb[4], pb[5]), xo2,
                        x4v[:, :, h * TH:(h + 1) * TH], epsc,
                        ("oev", h * HC, 1, 1))

        def _final(sync):
            S.wait(sync, "ln_st0", 16 * HC * NTH // 2)
            S.wait(sync, "ln_st1", 16 * HC * NTH // 2)
        block.sync(_final)

        return nc


def _rope_tables():
    freqs = 1.0 / (10000.0 ** (np.arange(0, DH, 2, dtype=np.float64) / DH))
    t = np.arange(T, dtype=np.float64)
    ang = np.outer(t, freqs)
    return (np.cos(ang).T.astype(np.float32),
            np.sin(ang).T.astype(np.float32))


def _l1_qkv_in_maps(inputs, x3_full):
    wqT = np.ascontiguousarray(np.asarray(inputs["L1_attn_Wq"], dtype=np.float32).T)
    wkT1 = np.ascontiguousarray(np.asarray(inputs["L1_attn_Wk"], dtype=np.float32).T)
    wvT1 = np.ascontiguousarray(np.asarray(inputs["L1_attn_Wv"], dtype=np.float32).T)
    cosf, sinf = _rope_tables()
    idtf = np.eye(128, dtype=np.float32)
    idtb = np.eye(128, dtype=BFD)
    dvs = np.array([[1.0 / (H * H)], [1.0 / (512 * H)], [1.0 / (512 * H)],
                    [0.0]], dtype=np.float32)
    maps = []
    for c in range(NCORES):
        b, q = c // 4, c % 4
        lo = q * TPC
        maps.append({"x3": fmaj(x3_full[b, lo:lo + TPC, :]), "wqT": wqT,
                     "wkT1": wkT1, "wvT1": wvT1,
                     "cosr": np.ascontiguousarray(cosf[:, lo:lo + TPC]),
                     "sinr": np.ascontiguousarray(sinf[:, lo:lo + TPC]),
                     "idtf": idtf, "idtb": idtb, "dvs": dvs})
    return maps


def _l1_attn_in_maps(d_results):
    qf = {}
    kf = {}
    vf = {}
    for b in range(B):
        qf[b] = np.hstack([d_results[b * 4 + q]["qout"] for q in range(4)])
        kf[b] = np.hstack([d_results[b * 4 + q]["kout"] for q in range(4)])
        vf[b] = np.vstack([np.asarray(d_results[b * 4 + q]["vout"]).reshape(TPC, NKV, DH) for q in range(4)])
    msk = np.zeros((128, 4, 512), dtype=BFD)
    for off in range(4):
        i = np.arange(128)[:, None]
        j = np.arange(512)[None, :]
        msk[:, off, :] = (i <= j - 128 * off).astype(BFD)
    maps = []
    for c in range(NCORES):
        b, kvh = c // 4, c % 4
        maps.append({"qin": np.ascontiguousarray(qf[b][kvh * 512:(kvh + 1) * 512]),
                     "kin": np.ascontiguousarray(kf[b][kvh * 128:(kvh + 1) * 128]),
                     "vin": np.ascontiguousarray(vf[b][:, kvh, :]),
                     "msk": msk})
    return maps


def _l1_oproj_in_maps(inputs, x3_full, e_results):
    woT1 = np.ascontiguousarray(np.asarray(inputs["L1_attn_Wo"], dtype=np.float32).T)
    idtf = np.eye(128, dtype=np.float32)
    idtb = np.eye(128, dtype=BFD)
    af = {}
    sf = {}
    for b in range(B):
        af[b] = np.vstack([e_results[b * 4 + kvh]["aout"] for kvh in range(4)])
        sf[b] = np.vstack([e_results[b * 4 + kvh]["sout"] for kvh in range(4)])
    maps = []
    for c in range(NCORES):
        b, q = c // 4, c % 4
        lo = q * TPC
        maps.append({"ain": np.ascontiguousarray(af[b][:, lo:lo + TPC]),
                     "sin_": np.ascontiguousarray(sf[b][:, lo:lo + TPC]),
                     "woT1": woT1, "x3r": fmaj(x3_full[b, lo:lo + TPC, :]),
                     "idtf": idtf, "idtb": idtb})
    return maps


def run_l1_attn_block(inputs, x3_full):
    ncd = build_l1_qkv()
    rd = run_bass_kernel_spmd(ncd, _l1_qkv_in_maps(inputs, x3_full),
                              core_ids=list(range(NCORES)))
    nce = build_l1_attn()
    re_ = run_bass_kernel_spmd(nce, _l1_attn_in_maps(rd.results),
                               core_ids=list(range(NCORES)))
    ncf = build_l1_oproj()
    rf = run_bass_kernel_spmd(ncf, _l1_oproj_in_maps(inputs, x3_full,
                                                     re_.results),
                              core_ids=list(range(NCORES)))
    return _gather_fm(rf.results, "x4")


def kernel(**inputs):
    x = np.asarray(inputs["L0_attn_x"], dtype=np.float32)
    nca = build_l0_mixer_a()
    ra = run_bass_kernel_spmd(nca, _l0_mixer_a_in_maps(inputs, x),
                              core_ids=list(range(NCORES)))
    ncb = build_l0_mixer_b()
    rb = run_bass_kernel_spmd(ncb, _l0_mixer_b_in_maps(inputs, x, ra.results),
                              core_ids=list(range(NCORES)))
    x2 = _gather_fm(rb.results, "x2")
    ncc = build_l0_ffn()
    rc = run_bass_kernel_spmd(ncc, _l0_ffn_in_maps(inputs, x2),
                              core_ids=list(range(NCORES)))
    x3 = _gather_fm(rc.results, "x3")
    x4 = run_l1_attn_block(inputs, x3)
    ncg = build_l1_ffn()
    rg = run_bass_kernel_spmd(ncg, _l1_ffn_in_maps(inputs, x4),
                              core_ids=list(range(NCORES)))
    return _gather_fm(rg.results, "x5")



def emit_attn_fused(nc=None, G=None, ext=None):
    nc, dram = _phase_setup(nc, ext, "e_")
    qa2a = dram("qa2a", [8 * 256, TPC], BF16, "ExternalInput")
    ka2a = dram("ka2a", [8 * 128, TPC], BF16, "ExternalInput")
    va2a = dram("va2a", [8 * TPC, DH], BF16, "ExternalInput")
    msk4 = dram("msk4", [128, 4, 512], BF16, "ExternalInput")
    snd_a = dram("snd_a", [8 * 256, TPC], BF16, "ExternalOutput")
    snd_s = dram("snd_s", [NH, TPC], F32, "ExternalOutput")

    qv = qa2a.rearrange("(r l p) t -> p r l t", p=128, l=2)
    kv_ = ka2a.rearrange("(r p) t -> p r t", p=128)
    vv = va2a.rearrange("(kb p) d -> p kb d", p=128)
    SCALE = float(1.0 / np.sqrt(DH))
    NQB = 8

    with contextlib.ExitStack() as ctx:
        sb = lambda n, *a: ctx.enter_context(nc.sbuf_tensor(_mangle(n), *a))
        psb = lambda n, *a: ctx.enter_context(nc.psum_tensor(_mangle(n), *a))
        S = PhaseSync(G) if G is not None else PhaseSync(Sync(nc, ctx))

        qt = sb("qt", [128, 2, 512], BF16)
        kt = sb("kt", [128, 8, TPC], BF16)
        vt = sb("vt", [128, 2 * T // 128, DH], BF16)
        mskt = sb("mskt", [128, 4, 512], BF16)
        pt = sb("pt", [128, 4, 512], BF16)
        srow = sb("srow", [1, 2, 512], F32)
        onec = sb("onec", [128, 1], BF16)
        ao = sb("ao", [128, 2, 512], BF16)
        pb = [psb(f"pb{i}", [128, 512], F32) for i in range(6)]
        qkb = (0, 1, 4, 5)

        block = ctx.enter_context(nc.Block())

        def _ld(sync):
            i = sync.dma_start(out=kt[:], in_=kv_)
            S.dma(i, "ld")
            i = sync.dma_start(out=vt[:], in_=vv)
            S.dma(i, "ld")
            i = sync.dma_start(out=mskt[:], in_=msk4)
            S.dma(i, "ld")
        block.sync(_ld)

        def _sst1(sync, m):
            u, qb = m // NQB, m % NQB
            g, l = u // 2, u % 2
            j = g * 4 + qb // 2
            S.wait(sync, "oev", m + 1)
            i = sync.dma_start(
                out=snd_a[j * 256 + l * 128:j * 256 + (l + 1) * 128,
                          (qb % 2) * 512:(qb % 2 + 1) * 512],
                in_=ao[:, m % 2, :])
            S.dma(i, f"ast{m % 2}")
            S.wait(sync, "srw", m + 1)
            i = sync.dma_start(
                out=snd_s[j * 2 + l:j * 2 + l + 1,
                          (qb % 2) * 512:(qb % 2 + 1) * 512],
                in_=srow[:, m % 2, :])
            S.dma(i, f"sst{m % 2}")

        def _sp(sync):
            for n in range(4 * NQB):
                u, qb = n // NQB, n % NQB
                g, l = u // 2, u % 2
                sl = n % 2
                if n >= 2:
                    S.wait(sync, "oev", n - 1)
                i = sync.dma_start(out=qt[:, sl, :],
                                   in_=qv[:, g * 4 + qb // 2, l,
                                          (qb % 2) * 512:(qb % 2 + 1) * 512])
                S.dma(i, f"qld{sl}")
                if n >= 2:
                    _sst1(sync, n - 2)
            _sst1(sync, 4 * NQB - 2)
            _sst1(sync, 4 * NQB - 1)
            S.wait(sync, "ast0", 16 * 16)
            S.wait(sync, "ast1", 16 * 16)
            S.wait(sync, "sst0", 16 * 16)
            S.wait(sync, "sst1", 16 * 16)
        block.sync(_sp)

        def _ones(gpsimd):
            i = gpsimd.memset(onec[:], 1.0)
            S.inc(i, "ones")
        block.gpsimd(_ones)

        nit = 0
        ni = 0
        nb = 0
        nev = 0
        for u in range(4):
            g, l = u // 2, u % 2
            for qb in range(NQB):
                nkb = 4 * (qb + 1)
                n = u * NQB + qb
                qsl = n % 2
                masked = set(range(4 * qb, 4 * qb + 4))
                ni0, nb0 = ni, nb

                def _qk(tensor, g=g, qb=qb, nkb=nkb, n=n, qsl=qsl, nit=nit,
                        ni0=ni0, nb0=nb0, masked=masked, nev=nev):
                    if n == 0:
                        S.wait(tensor, "ld", 48)
                        S.wait(tensor, "ones", 1)
                    S.wait(tensor, f"qld{qsl}", 16 * (n // 2 + 1))
                    if nev > 0:
                        S.wait(tensor, "oev", nev)
                        S.wait(tensor, "srw", nev)
                    xni, xnb = ni0, nb0
                    expcnt = []
                    for j in range(nkb):
                        if j in masked:
                            xnb += 1
                            expcnt.append(("pteb", xnb))
                        else:
                            xni += 1
                            expcnt.append(("ptei", xni))
                    for j in range(nkb + 1):
                        if j < nkb:
                            it = nit + j
                            if j >= 4:
                                tg, cv = expcnt[j - 4]
                                S.wait(tensor, tg, cv)
                            i = tensor.matmul(pb[qkb[j % 4]][:],
                                              kt[:, g * 4 + j // 8,
                                                 (j % 8) * 128:(j % 8 + 1) * 128],
                                              qt[:, qsl, :],
                                              start=True, stop=True)
                            S.inc(i, "qkps")
                        if j >= 1:
                            jj = j - 1
                            itj = nit + jj
                            slj = jj % 4
                            tg, cv = expcnt[jj]
                            S.wait(tensor, tg, cv)
                            i = tensor.matmul(pb[2][:], vt[:, g * 32 + jj, :],
                                              pt[:, slj, :],
                                              start=(jj == 0), stop=(jj == nkb - 1))
                            i = tensor.matmul(pb[3][0:1, :], onec[:],
                                              pt[:, slj, :],
                                              start=(jj == 0), stop=(jj == nkb - 1))
                            S.inc(i, "pvd")
                block.tensor(_qk)

                def _pt(scalar, nkb=nkb, nit=nit, masked=masked):
                    for j in range(nkb):
                        it = nit + j
                        sl = j % 4
                        S.wait(scalar, "qkps", it + 1)
                        if j >= 4:
                            S.wait(scalar, "pvd", it - 3)
                        i = scalar.activation(out=pt[:, sl, :],
                                              in_=pb[qkb[sl]][:],
                                              func=AF.Exp, scale=SCALE)
                        if j in masked:
                            S.inc(i, "ptee")
                        else:
                            S.inc(i, "ptei")
                block.scalar(_pt)

                def _msk(vector, qb=qb, nb0=nb0, masked=masked):
                    lnb = nb0
                    for j in sorted(masked):
                        sl = j % 4
                        r_ = j - 4 * qb
                        lnb += 1
                        S.wait(vector, "ptee", lnb)
                        i = vector.tensor_tensor(
                            out=pt[:, sl, :], in0=pt[:, sl, :],
                            in1=mskt[:, r_, :], op=ALU.mult)
                        S.inc(i, "pteb")
                block.vector(_msk)

                def _oev(vector, n=n, nkb=nkb, nit=nit):
                    S.wait(vector, "pvd", nit + nkb)
                    if n >= 2:
                        S.wait(vector, f"ast{n % 2}", 16 * (n // 2))
                    i = vector.tensor_copy(out=ao[:, n % 2, :], in_=pb[2][:])
                    S.inc(i, "oev")
                    if n >= 2:
                        S.wait(vector, f"sst{n % 2}", 16 * (n // 2))
                    i = vector.tensor_copy(out=srow[:, n % 2, :],
                                           in_=pb[3][0:1, :])
                    S.inc(i, "srw")
                block.vector(_oev)

                nit += nkb
                ni += nkb - 4
                nb += 4
                nev += 1

    return nc


def _attn_msk4():
    kt = np.arange(128)[:, None]
    qt = np.arange(512)[None, :]
    m = np.zeros((128, 4, 512), dtype=BFD)
    for r in range(4):
        m[:, r, :] = (r * 128 + kt <= qt).astype(BFD)
    return m


def _attn_fused_consts(q):
    biasc = np.zeros((128, 64), dtype=np.float32)
    msku = np.zeros((128, 2, 16, 512), dtype=BFD)
    i = np.arange(128)[:, None]
    j = np.arange(512)[None, :]
    for qb in range(2):
        gqb = q * 2 + qb
        for kb in range(32):
            if kb > gqb * 4 + 3:
                biasc[:, qb * 32 + kb] = -1e30
        for u in range(4):
            for r in range(4):
                kb = u * 8 + qb * 4 + r
                if u == q:
                    msku[:, qb, u * 4 + r, :] = (i <= j - 128 * r).astype(BFD)
                else:
                    msku[:, qb, u * 4 + r, :] = np.ones((128, 512), dtype=BFD)
    return biasc, msku



GROUPS = [[0, 1, 2, 3], [4, 5, 6, 7]]


def _emit_cc(nc, G, ins_outs, kinds, tag, groups=None):
    from concourse.aot_env import allow_ici_all_to_all
    if groups is None:
        groups = GROUPS
    with nc.Block() as blk:
        def _cc(g):
            with allow_ici_all_to_all():
                for (i_, o_), kind in zip(ins_outs, kinds):
                    inst = g.collective_compute(kind, ALU.bypass,
                                                replica_groups=groups,
                                                ins=[i_.opt()], outs=[o_.opt()])
                    G.inc(inst, tag, 1)
            G.wait(g, tag, None)
        blk.gpsimd(_cc)


def build_fused(debug=False):
    nc = bass.Bass("TRN2", target_bir_lowering=False)
    gctx = contextlib.ExitStack()
    with gctx:
        G = Sync(nc, gctx)
        ext = {}
        dbg = {"x2", "x3", "x4", "qout", "ain", "sin_"} if debug else set()

        def ein(name, shape, dt):
            ext[name] = nc.dram_tensor(name, shape, dt, kind="ExternalInput").ap()

        def itn(name, shape, dt):
            kind = "ExternalOutput" if name in dbg else "Internal"
            ext[name] = nc.dram_tensor(name, shape, dt, kind=kind).ap()

        ein("xin", [H, TPC + 1], F32)
        for n in ("wkT", "wvT", "wrT", "woT"):
            ein(n, [H, H], F32)
        for n in ("bkc", "bvc", "brc", "tmk", "tmv", "tmr", "tdec", "tfir",
                  "boc", "dsc"):
            ein(n, [128, HC], F32)
        ein("vg", [128, 8], F32)
        ein("iota", [1, TPC], F32)
        ein("gw", [H, I0], F32)
        ein("uw", [H, I0], F32)
        ein("dw", [I0, H], F32)
        ein("gsc", [128, I0 // 128], F32)
        ein("usc", [128, I0 // 128], F32)
        ein("cosr", [64, TPC], F32)
        ein("sinr", [64, TPC], F32)
        ein("idtf", [128, 128], F32)
        ein("idtb", [128, 128], BF16)
        ein("dvs", [4, 1], F32)
        ein("msk4", [128, 4, 512], BF16)

        tmn = {"q": "twq", "k": "twk", "v": "twv", "o": "two",
               "1": "tw1", "2": "tw2", "3": "tw3"}
        for nm, rows, cols, rs in TMATS:
            ext[tmn[nm]] = nc.dram_tensor(tmn[nm], [rows, cols], FP8,
                                          kind="Internal").ap()
            ext[f"sh{nm}"] = nc.dram_tensor(f"sh{nm}", [rs, cols], F32,
                                            kind="ExternalInput").ap()
        ext["swv"] = nc.dram_tensor("swv", [1, 8], F32, kind="Internal").ap()
        itn("gprt", [128, 8], F32)
        itn("gprd", [128, 8], F32)

        for n in ("nout", "dout", "srout", "qout"):
            itn(n, [H, TPC], BF16)
        itn("totout", [128, 48], F32)
        itn("totga", [512, 48], F32)
        for n in ("x2", "x3", "x4"):
            itn(n, [H, TPC], F32)
        itn("kout", [1024, TPC], BF16)
        ext["vout"] = nc.dram_tensor("vout", [2 * NKV, TPC, DH], BF16,
                                     kind="Internal").ap()
        itn("qa2a", [8 * 256, TPC], BF16)
        itn("ka2a", [8 * 128, TPC], BF16)
        itn("va2a", [8 * TPC, DH], BF16)
        itn("snd_a", [8 * 256, TPC], BF16)
        itn("snd_s", [NH, TPC], F32)
        itn("aina", [8 * 256, TPC], BF16)
        itn("sina", [NH, TPC], F32)
        ext["x5"] = nc.dram_tensor("x5", [H, TPC], F32,
                                   kind="ExternalOutput").ap()

        ext["xres"] = ext["xin"][:, 1:TPC + 1]
        ext["ownt"] = ext["totout"]
        ext["x3r"] = ext["x3"]
        ext["nin"] = ext["nout"]
        ext["din"] = ext["dout"]
        ext["srin"] = ext["srout"]
        ext["ain"] = ext["aina"]
        ext["sin_"] = ext["sina"]

        build_l0_mixer_a(nc, G, ext)
        _emit_cc(nc, G, [(ext["totout"], ext["totga"])], ["AllGather"], "ccA")
        emit_tern_phase(nc, G, ext)
        build_l0_mixer_b(nc, G, ext)
        build_l0_ffn(nc, G, ext)
        build_l1_qkv(nc, G, ext)
        _emit_cc(nc, G,
                 [(ext["kout"], ext["ka2a"]),
                  (ext["vout"].rearrange("k t d -> (k t) d"), ext["va2a"]),
                  (ext["qout"], ext["qa2a"])],
                 ["AllToAll", "AllToAll", "AllToAll"], "ccD",
                 groups=[list(range(8))])
        emit_attn_fused(nc, G, ext)
        _emit_cc(nc, G,
                 [(ext["snd_a"], ext["aina"]),
                  (ext["snd_s"], ext["sina"])],
                 ["AllToAll", "AllToAll"], "ccE",
                 groups=[list(range(8))])
        build_l1_oproj(nc, G, ext)
        build_l1_ffn(nc, G, ext)
    return nc


def _fused_in_maps(inputs):
    x = np.asarray(inputs["L0_attn_x"], dtype=np.float32)
    ma = _l0_mixer_a_in_maps(inputs, x)
    dummy = np.zeros((B, T, H), dtype=np.float32)
    woT = np.ascontiguousarray(np.asarray(inputs["L0_attn_Wo"], dtype=np.float32).T)
    boc = cols128(inputs["L0_attn_bo"])
    iota = np.arange(TPC, dtype=np.float32).reshape(1, TPC)
    gw = np.ascontiguousarray(np.asarray(inputs["L0_ffn_gate_w"], dtype=np.float32))
    uw = np.ascontiguousarray(np.asarray(inputs["L0_ffn_up_w"], dtype=np.float32))
    dw = np.ascontiguousarray(np.asarray(inputs["L0_ffn_down_w"], dtype=np.float32))
    gsc = cols128(inputs["L0_ffn_gate_scale"])
    usc = cols128(inputs["L0_ffn_up_scale"])
    dsc = cols128(inputs["L0_ffn_down_scale"])
    cosf, sinf = _rope_tables()
    idtf = np.eye(128, dtype=np.float32)
    idtb = np.eye(128, dtype=BFD)
    dvs = np.array([[1.0 / (H * H)], [1.0 / (512 * H)], [1.0 / (512 * H)],
                    [0.0]], dtype=np.float32)
    maps = []
    for c in range(NCORES):
        b, q = c // 4, c % 4
        lo = q * TPC
        m = dict(ma[c])
        m.update({"vg": _mixer_vg(q), "woT": woT, "boc": boc, "iota": iota,
                  "gw": gw, "uw": uw, "dw": dw, "gsc": gsc, "usc": usc,
                  "dsc": dsc,
                  "cosr": np.ascontiguousarray(cosf[:, lo:lo + TPC]),
                  "sinr": np.ascontiguousarray(sinf[:, lo:lo + TPC]),
                  "idtf": idtf, "idtb": idtb, "dvs": dvs,
                  "msk4": _attn_msk4(),
                  "tdiv": _TDIV, "teps": _TEPS})
        m.update(_tern_shards(inputs, c))
        maps.append(m)
    return maps


def kernel(**inputs):
    nc = build_fused()
    res = run_bass_kernel_spmd(nc, _fused_in_maps(inputs),
                               core_ids=list(range(NCORES)))
    return _gather_fm(res.results, "x5")


def kernel_unfused(**inputs):
    x = np.asarray(inputs["L0_attn_x"], dtype=np.float32)
    nca = build_l0_mixer_a()
    ra = run_bass_kernel_spmd(nca, _l0_mixer_a_in_maps(inputs, x),
                              core_ids=list(range(NCORES)))
    ncb = build_l0_mixer_b()
    rb = run_bass_kernel_spmd(ncb, _l0_mixer_b_in_maps(inputs, x, ra.results),
                              core_ids=list(range(NCORES)))
    x2 = _gather_fm(rb.results, "x2")
    ncc = build_l0_ffn()
    rc = run_bass_kernel_spmd(ncc, _l0_ffn_in_maps(inputs, x2),
                              core_ids=list(range(NCORES)))
    x3 = _gather_fm(rc.results, "x3")
    x4 = run_l1_attn_block(inputs, x3)
    ncg = build_l1_ffn()
    rg = run_bass_kernel_spmd(ncg, _l1_ffn_in_maps(inputs, x4),
                              core_ids=list(range(NCORES)))
    return _gather_fm(rg.results, "x5")



TMATS = (("q", H, H, 256), ("k", H, 512, 256), ("v", H, 512, 256),
         ("o", H, H, 256), ("1", H, I1, 256), ("2", H, I1, 256),
         ("3", I1, H, 704))


def _tern_chunks():
    ch = []
    for mi, (nm, rows, cols, rs) in enumerate(TMATS):
        off = 0
        while off < rs:
            nr = min(128, rs - off)
            ch.append((mi, nm, off, nr, cols))
            off += nr
    return ch


def emit_tern_phase(nc=None, G=None, ext=None):
    nc, dram = _phase_setup(nc, ext, "t_")
    sh = {nm: dram(f"sh{nm}", [rs, cols], F32, "ExternalInput")
          for nm, rows, cols, rs in TMATS}
    tdiv = dram("tdiv", [8, 1], F32, "ExternalInput")
    teps = dram("teps", [8, 1], F32, "ExternalInput")
    ctw = {nm: dram(f"c_tw{nm}", [rs, cols], FP8, "Internal")
           for nm, rows, cols, rs in TMATS}
    tw = {nm: dram(f"tw{nm}", [rows, cols], FP8, "Internal")
          for nm, rows, cols, rs in TMATS}
    gprt = dram("gprt", [128, 8], F32, "Internal")
    gprd = dram("gprd", [128, 8], F32, "Internal")
    swv = dram("swv", [1, 8], F32, "ExternalOutput" if ext is None else "Internal")
    bsc = dram("bsc", [128, 16], F32, "Internal")

    CH = _tern_chunks()

    with contextlib.ExitStack() as ctx:
        sb = lambda n, *a: ctx.enter_context(nc.sbuf_tensor(_mangle(n), *a))
        psb = lambda n, *a: ctx.enter_context(nc.psum_tensor(_mangle(n), *a))
        S = PhaseSync(G) if G is not None else PhaseSync(Sync(nc, ctx))
        S.g.pin("cct")

        stg = sb("stg", [128, 2, I1], BF16)
        wstg = sb("wstg", [128, 2, I1], F32)
        tero = sb("tero", [128, 2, I1], BF16)
        gacc = sb("gacc", [128, 8], F32)
        mcol = sb("mcol", [128, 1], F32)
        gaccT = sb("gaccT", [8, 128], F32)
        gsum = sb("gsum", [8, 3], F32)
        tdc = sb("tdc", [8, 2], F32)
        throw = sb("throw", [1, 16], F32)
        thrc = sb("thrc", [128, 16], F32)
        oner = sb("oner", [1, 128], F32)
        pbt = psb("pbt", [128, 512], F32)

        block = ctx.enter_context(nc.Block(no_gpsimd_drain=True))

        def _init(gpsimd):
            gpsimd.memset(gacc[:], 0.0)
            i = gpsimd.memset(oner[:], 1.0)
            S.inc(i, "ones")
        block.gpsimd(_init)

        def _cl(sync):
            i = sync.dma_start(out=tdc[:, 0:1], in_=tdiv)
            S.dma(i, "cl")
            i = sync.dma_start(out=tdc[:, 1:2], in_=teps)
            S.dma(i, "cl")
        block.sync(_cl)

        if G is None:
            def _gdma(gpsimd):
                for n, (mi, nm, off, nr, cols) in enumerate(CH):
                    sl = n % 2
                    if n >= 2:
                        S.wait(gpsimd, "gred", n - 1)
                    i = gpsimd.dma_start(out=stg[0:nr, sl, 0:cols],
                                         in_=sh[nm][off:off + nr, :])
                    S.dma(i, f"gld{sl}")
            block.gpsimd(_gdma)

            def _gred(vector):
                for n, (mi, nm, off, nr, cols) in enumerate(CH):
                    sl = n % 2
                    S.wait(vector, f"gld{sl}", 16 * (n // 2 + 1))
                    S.wait(vector, "gred", n)
                    i = vector.reduce_sum(out=mcol[0:nr, :],
                                          in_=stg[0:nr, sl, 0:cols],
                                          axis=AX, apply_absolute_value=True)
                    c = S.inc(i, "vch")
                    S.wait(vector, "vch", c)
                    i = vector.tensor_tensor(out=gacc[0:nr, mi:mi + 1],
                                             in0=gacc[0:nr, mi:mi + 1],
                                             in1=mcol[0:nr, :], op=ALU.add)
                    S.inc(i, "gred")
            block.vector(_gred)

            def _ar(gpsimd):
                S.wait(gpsimd, "gred", len(CH))
                i = gpsimd.dma_start(out=gprt, in_=gacc[:])
                S.dma(i, "gx")
                S.wait(gpsimd, "gx", 16)
                i = gpsimd.collective_compute("AllReduce", ALU.add,
                                              replica_groups=[list(range(8))],
                                              ins=[gprt.opt()], outs=[gprd.opt()])
                S.inc(i, "ccg")
                S.wait(gpsimd, "ccg", 1)
                i = gpsimd.dma_start(out=gaccT[:],
                                     in_=gprd.rearrange("p m -> m p"),
                                     allow_slow_non_contiguous=True)
                S.dma(i, "gx")
            block.gpsimd(_ar)
            gxb = 32
        else:
            def _ar(gpsimd):
                S.g.pin("gar")
                S.g.wait(gpsimd, "gar", 1)
                i = gpsimd.dma_start(out=gaccT[:],
                                     in_=gprd.rearrange("p m -> m p"),
                                     allow_slow_non_contiguous=True)
                S.dma(i, "gx")
            block.gpsimd(_ar)
            gxb = 16

        def _thr(vector):
            S.wait(vector, "gx", gxb)
            S.wait(vector, "cl", 32)
            i = vector.reduce_sum(out=gsum[:, 0:1], in_=gaccT[:], axis=AX)
            c = S.inc(i, "vch")
            S.wait(vector, "vch", c)
            i = vector.tensor_scalar(out=gsum[:, 1:2], in0=gsum[:, 0:1],
                                     scalar1=tdc[:, 0:1], scalar2=None,
                                     op0=ALU.mult)
            c = S.inc(i, "vch")
            S.wait(vector, "vch", c)
            i = vector.tensor_tensor(out=gsum[:, 1:2], in0=gsum[:, 1:2],
                                     in1=tdc[:, 1:2], op=ALU.add)
            c = S.inc(i, "vch")
            S.wait(vector, "vch", c)
            i = vector.tensor_scalar(out=gsum[:, 2:3], in0=gsum[:, 1:2],
                                     scalar1=0.5, scalar2=None, op0=ALU.mult)
            S.inc(i, "thr")
        block.vector(_thr)

        def _bounce(gpsimd):
            S.wait(gpsimd, "thr", 1)
            i = gpsimd.dma_start(out=bsc[0:8, 0:3], in_=gsum[:])
            S.dma(i, "gx")
            S.wait(gpsimd, "gx", gxb + 16)
            bv = bsc[0:8, 0:3].rearrange("m c -> c m")
            i = gpsimd.dma_start(out=throw[:, 0:8], in_=bv[1:2, :],
                                 allow_slow_non_contiguous=True)
            S.dma(i, "gx")
            i = gpsimd.dma_start(out=throw[:, 8:16], in_=bv[2:3, :],
                                 allow_slow_non_contiguous=True)
            S.dma(i, "gx")
            S.wait(gpsimd, "gx", gxb + 48)
            i = gpsimd.dma_start(out=swv, in_=throw[:, 0:8])
            S.dma(i, "gx")
        block.gpsimd(_bounce)

        def _bc(tensor):
            S.wait(tensor, "gx", gxb + 64)
            S.wait(tensor, "ones", 1)
            i = tensor.matmul(pbt[:, 0:8], oner[:], throw[:, 8:16],
                              start=True, stop=True)
            S.inc(i, "bc")
        block.tensor(_bc)

        def _cols(vector):
            S.wait(vector, "bc", 1)
            i = vector.tensor_copy(out=thrc[:, 0:8], in_=pbt[:, 0:8])
            c = S.inc(i, "vch")
            S.wait(vector, "vch", c)
            i = vector.tensor_scalar(out=thrc[:, 8:16], in0=thrc[:, 0:8],
                                     scalar1=-1.0, scalar2=None, op0=ALU.mult)
            S.inc(i, "cols")
        block.vector(_cols)

        def _store(gpsimd, n):
            mi, nm, off, nr, cols = CH[n]
            sl = n % 2
            i = gpsimd.dma_start(out=ctw[nm][off:off + nr, :],
                                 in_=tero[0:nr, sl, 0:cols])
            S.dma(i, f"tst{sl}")

        def _tdma(gpsimd):
            for n, (mi, nm, off, nr, cols) in enumerate(CH):
                sl = n % 2
                if n >= 2:
                    S.wait(gpsimd, "tern", n - 1)
                i = gpsimd.dma_start(out=wstg[0:nr, sl, 0:cols],
                                     in_=sh[nm][off:off + nr, :])
                S.dma(i, f"tld{sl}")
                if n >= 2:
                    _store(gpsimd, n - 2)
            S.wait(gpsimd, "tern", len(CH))
            _store(gpsimd, len(CH) - 2)
            _store(gpsimd, len(CH) - 1)
        block.gpsimd(_tdma)

        def _tv(vector):
            S.wait(vector, "cols", 1)
            for n, (mi, nm, off, nr, cols) in enumerate(CH):
                sl = n % 2
                S.wait(vector, f"tld{sl}", 16 * (n // 2 + 1))
                if n >= 2:
                    S.wait(vector, f"tst{sl}", 16 * (n // 2))
                i = vector.tensor_scalar(
                    out=tero[0:nr, sl, 0:cols], in0=wstg[0:nr, sl, 0:cols],
                    scalar1=thrc[0:nr, mi:mi + 1], scalar2=None, op0=ALU.is_ge)
                c = S.inc(i, "vch")
                S.wait(vector, "vch", c)
                i = vector.tensor_scalar(
                    out=stg[0:nr, sl, 0:cols], in0=wstg[0:nr, sl, 0:cols],
                    scalar1=thrc[0:nr, 8 + mi:9 + mi], scalar2=None,
                    op0=ALU.is_le)
                c = S.inc(i, "vch")
                S.wait(vector, "vch", c)
                i = vector.tensor_tensor(out=tero[0:nr, sl, 0:cols],
                                         in0=tero[0:nr, sl, 0:cols],
                                         in1=stg[0:nr, sl, 0:cols],
                                         op=ALU.subtract)
                S.inc(i, "tern")
        block.vector(_tv)

        def _cc(gpsimd):
            nall = len(CH)
            S.wait(gpsimd, "tst0", 16 * ((nall + 1) // 2))
            S.wait(gpsimd, "tst1", 16 * (nall // 2))
            for nm, rows, cols, rs in TMATS:
                i = gpsimd.collective_compute("AllGather", ALU.bypass,
                                              replica_groups=[list(range(8))],
                                              ins=[ctw[nm].opt()],
                                              outs=[tw[nm].opt()])
                S.g.inc(i, "cct")
        block.gpsimd(_cc)

    return nc


def _tern_shards(inputs, core):
    mats = {"q": np.asarray(inputs["L1_attn_Wq"], dtype=np.float32).T,
            "k": np.asarray(inputs["L1_attn_Wk"], dtype=np.float32).T,
            "v": np.asarray(inputs["L1_attn_Wv"], dtype=np.float32).T,
            "o": np.asarray(inputs["L1_attn_Wo"], dtype=np.float32).T,
            "1": np.asarray(inputs["L1_ffn_w1"], dtype=np.float32).T,
            "2": np.asarray(inputs["L1_ffn_w2"], dtype=np.float32).T,
            "3": np.asarray(inputs["L1_ffn_w3"], dtype=np.float32).T}
    out = {}
    for nm, rows, cols, rs in TMATS:
        out[f"sh{nm}"] = np.ascontiguousarray(mats[nm][core * rs:(core + 1) * rs])
    return out


_TDIV = np.array([[1.0 / (H * H)], [1.0 / (512 * H)], [1.0 / (512 * H)],
                  [1.0 / (H * H)], [1.0 / (I1 * H)], [1.0 / (I1 * H)],
                  [1.0 / (I1 * H)], [0.0]], dtype=np.float32)
_TEPS = np.array([[1e-5], [1e-5], [1e-5], [1e-5], [1e-8], [1e-8], [1e-8],
                  [0.0]], dtype=np.float32)
